# version 02
# baseline (speedup 1.0000x reference)
import contextlib
import numpy as np
import ml_dtypes
import concourse.bass as bass
import concourse.mybir as mybir
from concourse.bass_utils import run_bass_kernel_spmd

F32 = mybir.dt.float32
BF16 = mybir.dt.bfloat16
I32 = mybir.dt.int32
AF = mybir.ActivationFunctionType
ALU = mybir.AluOpType
AX = mybir.AxisListType

D = 2048
L = 4096
T = 2048
NH = 16
NKV = 8
HD = 128
DH = 8192
SCALE = float(HD) ** -0.5
EPS_LN = 1e-5
EPS_RMS = 1e-6
EPS_Q = 1e-5
NEG = -1.0e30

DEBUG = False
STOP_AFTER = None
RACE_DETECT = False
PP_LIMIT = None
PP_NO_PRE = False
PP_ORDER = ("k", "v", "qa", "qb")


def _bf(x):
    return np.asarray(x).astype(ml_dtypes.bfloat16)


def make_mask(h):
    m = np.zeros((8, 128, 1024), dtype=np.float32)
    if h == 1:
        for c in range(8):
            kk = c * 128 + np.arange(128)[:, None]
            qq = np.arange(1024)[None, :]
            m[c] = np.where(kk <= qq, NEG, 0.0)
    return _bf(m)


def kernel(L0_attn_hidden_states, L0_attn_Wq, L0_attn_Wk, L0_attn_Wv, L0_attn_Wo,
           L0_attn_q_norm_w, L0_attn_k_norm_w, L0_ffn_w1, L0_ffn_w2):
    x = np.ascontiguousarray(np.asarray(L0_attn_hidden_states, dtype=np.float32))
    wqT = np.ascontiguousarray(np.asarray(L0_attn_Wq, dtype=np.float32).T)
    wkT = np.ascontiguousarray(np.asarray(L0_attn_Wk, dtype=np.float32).T)
    wvT = np.ascontiguousarray(np.asarray(L0_attn_Wv, dtype=np.float32).T)
    woT = np.ascontiguousarray(np.asarray(L0_attn_Wo, dtype=np.float32).T)
    w1T = np.ascontiguousarray(np.asarray(L0_ffn_w1, dtype=np.float32).T)
    w2T = np.ascontiguousarray(np.asarray(L0_ffn_w2, dtype=np.float32).T)
    qnw = np.ascontiguousarray(np.asarray(L0_attn_q_norm_w, dtype=np.float32).reshape(1, HD))
    knw = np.ascontiguousarray(np.asarray(L0_attn_k_norm_w, dtype=np.float32).reshape(1, HD))
    ident_f32 = np.eye(128, dtype=np.float32)
    ident_bf = _bf(np.eye(128))
    masks = [make_mask(0), make_mask(1)]

    nc = build_nc()
    in_maps = []
    for c in range(8):
        b, h = c // 2, c % 2
        own = np.ascontiguousarray(x[b, h * T:(h + 1) * T, :])
        oth = np.ascontiguousarray(x[b, (1 - h) * T:(2 - h) * T, :])
        in_maps.append({
            "x_own": own, "x_oth": oth,
            "wqT": wqT, "wkT": wkT, "wvT": wvT, "woT": woT,
            "w1T": w1T, "w2T": w2T,
            "qnw_row": qnw, "knw_row": knw,
            "ident_f32": ident_f32, "ident_bf": ident_bf,
            "mask": masks[h],
        })
    res = run_bass_kernel_spmd(nc, in_maps, core_ids=list(range(8)))
    out = np.empty((4, L, D), dtype=np.float32)
    for c in range(8):
        b, h = c // 2, c % 2
        out[b, h * T:(h + 1) * T, :] = res.results[c]["y"]
    return out


class NS:
    pass


def build_nc():
    nc = bass.Bass(trn_type="TRN2", target_bir_lowering=False,
                   detect_race_conditions=RACE_DETECT)
    s = NS()
    s.nc = nc
    skind = "ExternalOutput" if DEBUG else "Internal"

    s.x_own = nc.dram_tensor("x_own", [T, D], F32, kind="ExternalInput").ap()
    s.x_oth = nc.dram_tensor("x_oth", [T, D], F32, kind="ExternalInput").ap()
    s.wqT = nc.dram_tensor("wqT", [D, D], F32, kind="ExternalInput").ap()
    s.wkT = nc.dram_tensor("wkT", [D, NKV * HD], F32, kind="ExternalInput").ap()
    s.wvT = nc.dram_tensor("wvT", [D, NKV * HD], F32, kind="ExternalInput").ap()
    s.woT = nc.dram_tensor("woT", [D, D], F32, kind="ExternalInput").ap()
    s.w1T = nc.dram_tensor("w1T", [D, DH], F32, kind="ExternalInput").ap()
    s.w2T = nc.dram_tensor("w2T", [DH, D], F32, kind="ExternalInput").ap()
    s.qnw_row_d = nc.dram_tensor("qnw_row", [1, HD], F32, kind="ExternalInput").ap()
    s.knw_row_d = nc.dram_tensor("knw_row", [1, HD], F32, kind="ExternalInput").ap()
    s.ident_f32_d = nc.dram_tensor("ident_f32", [128, 128], F32, kind="ExternalInput").ap()
    s.ident_bf_d = nc.dram_tensor("ident_bf", [128, 128], BF16, kind="ExternalInput").ap()
    s.mask_d = nc.dram_tensor("mask", [8, 128, 1024], BF16, kind="ExternalInput").ap()

    s.y = nc.dram_tensor("y", [T, D], F32, kind="ExternalOutput").ap()

    def scr(name, shape, dtype):
        return nc.dram_tensor(name, shape, dtype, kind=skind).ap()
    s.lnT_hi = scr("lnT_hi", [16, 128, L], BF16)
    s.lnT_lo = scr("lnT_lo", [16, 128, L], BF16)
    s.kn_hi = scr("kn_hi", [NKV, 32, 128, 128], BF16)
    s.kn_lo = scr("kn_lo", [NKV, 32, 128, 128], BF16)
    s.vt_hi = scr("vt_hi", [L, NKV * HD], BF16)
    s.vt_lo = scr("vt_lo", [L, NKV * HD], BF16)
    s.qn_hi = scr("qn_hi", [NH, 16, 128, 128], BF16)
    s.qn_lo = scr("qn_lo", [NH, 16, 128, 128], BF16)
    s.wo_hi = scr("wo_hi", [D, D], BF16)
    s.wo_lo = scr("wo_lo", [D, D], BF16)
    s.y_sc = scr("y_sc", [T, D], F32)
    s.x2_sc = scr("x2_sc", [T, D], F32)
    s.xq_sc = scr("xq_sc", [16, 128, T], BF16)
    s.g_sc = scr("g_sc", [DH, T], F32)
    s.hq_sc = scr("hq_sc", [DH, T], BF16)
    s.t2_sc = scr("t2_sc", [DH, D], BF16)
    s.dbg_m = scr("dbg_m", [1, 4], F32)

    def sb(name, shape, dtype):
        return nc.alloc_sbuf_tensor(name, shape, dtype).ap()
    s.ident_f32 = sb("c_ident32", [128, 128], F32)
    s.ident_bf = sb("c_identbf", [128, 128], BF16)
    s.qnw_row = sb("c_qnw", [1, HD], F32)
    s.knw_row = sb("c_knw", [1, HD], F32)
    s.qnw_b = sb("c_qnwb", [128, HD], F32)
    s.knw_b = sb("c_knwb", [128, HD], F32)
    s.ones_bf = sb("c_onesbf", [128, 1], BF16)
    s.ones_col = sb("c_onescol", [128, 1], F32)
    s.ones_row = sb("c_onesrow", [1, 128], F32)
    s.halfc = sb("c_halfc", [128, 1], F32)
    s.nhalfc = sb("c_nhalfc", [128, 1], F32)
    s.m1c = sb("c_m1c", [1, 1], F32)
    s.m2c = sb("c_m2c", [1, 1], F32)
    s.s1col = sb("c_s1col", [128, 1], F32)
    s.s2col = sb("c_s2col", [128, 1], F32)
    s.m2col = sb("c_m2col", [128, 1], F32)
    s.mid_ctx = contextlib.ExitStack()
    s.m_acc1 = s.mid_ctx.enter_context(nc.sbuf_tensor("c_macc1", [128, 1], F32)).ap()
    s.m_acc2 = s.mid_ctx.enter_context(nc.sbuf_tensor("c_macc2", [128, 1], F32)).ap()
    s.m_tmpc = s.mid_ctx.enter_context(nc.sbuf_tensor("c_mtmpc", [128, 1], F32)).ap()
    s.pre_buf = s.mid_ctx.enter_context(nc.sbuf_tensor("c_prebuf", [128, 4096], F32)).ap()
    s.dbg_st = s.mid_ctx.enter_context(nc.sbuf_tensor("c_dbgst", [1, 4], F32)).ap()

    s.pb = [nc.alloc_psum_tensor(f"pb{i}", [128, 512], F32).ap() for i in range(8)]

    s.sems = {}
    def sem(name):
        h = nc.ctx.enter_context(nc.semaphore(name))
        s.sems[name] = h
        return h
    s.sem = sem

    s.pre_gld = sem("pre_gld")
    s.pre_red = sem("pre_red")
    s.pre_done = sem("pre_done")
    s.pre_plan = []
    for i in range(D // 128):
        for j in range(DH // 2048):
            s.pre_plan.append((s.w1T, i * 128, j * 2048, s.m_acc1))
    for i in range(DH // 128):
        s.pre_plan.append((s.w2T, i * 128, 0, s.m_acc2))
    s.pre_n = 0
    s.pre_nv = 0

    emit_init(s)
    if STOP_AFTER == "init":
        return nc
    emit_p1_ln(s)
    if STOP_AFTER == "ln":
        return nc
    for kind in PP_ORDER:
        emit_proj_pass(s, kind)
        if STOP_AFTER == kind:
            return nc
    emit_wo_split(s)
    emit_m_finalize(s)
    if STOP_AFTER == "p1":
        return nc
    s.mid_ctx.close()
    s.amax1_row = sb("c_am1row", [1, T], F32)
    s.amax2_row = sb("c_am2row", [1, T], F32)
    s.c1_b = sb("c_c1b", [128, T], F32)
    s.s2_b = sb("c_s2b", [128, T], F32)
    s.c2_cols = sb("c_c2cols", [128, 16], F32)
    s.am2t = sb("c_am2t", [128, 16], F32)
    emit_p2(s)
    if STOP_AFTER == "p2":
        return nc
    emit_p3(s)
    emit_c_rows(s)
    if STOP_AFTER == "p3":
        return nc
    emit_ffn_a(s)
    if STOP_AFTER == "ffa":
        return nc
    emit_ffn_b(s)
    return nc


def feed_prepass(s, g, nchunks):
    for _ in range(nchunks):
        if s.pre_n >= len(s.pre_plan):
            return
        w, r0, c0, acc = s.pre_plan[s.pre_n]
        n = s.pre_n
        if n >= 2:
            g.wait_ge(s.pre_red, n - 1)
        buf = s.pre_buf[:, (n % 2) * 2048:(n % 2 + 1) * 2048]
        g.dma_start(out=buf, in_=w[r0:r0 + 128, c0:c0 + 2048]).then_inc(s.pre_gld, 16)
        s.pre_n += 1


def feed_prepass_v(s, v, nchunks):
    for _ in range(nchunks):
        if s.pre_nv >= len(s.pre_plan):
            return
        w, r0, c0, acc = s.pre_plan[s.pre_nv]
        n = s.pre_nv
        v.wait_ge(s.pre_gld, 16 * (n + 1))
        buf = s.pre_buf[:, (n % 2) * 2048:(n % 2 + 1) * 2048]
        v.tensor_reduce(s.m_tmpc, buf, AX.X, ALU.add, apply_absolute_value=True)
        v.drain()
        v.tensor_tensor(acc, acc, s.m_tmpc, ALU.add).then_inc(s.pre_red, 1)
        s.pre_nv += 1
        if s.pre_nv == len(s.pre_plan):
            v.drain().then_inc(s.pre_done, 1)


def emit_init(s):
    nc = s.nc
    ld = s.sem("init_ld")
    ldv = s.sem("init_ldv")
    ldt = s.sem("init_ldt")
    with nc.Block() as block:
        @block.sync
        def _(sy):
            sy.dma_start(out=s.ident_f32, in_=s.ident_f32_d).then_inc(ld, 16)
            sy.dma_start(out=s.ident_bf, in_=s.ident_bf_d).then_inc(ld, 16)
            sy.dma_start(out=s.qnw_row, in_=s.qnw_row_d).then_inc(ld, 16)
            sy.dma_start(out=s.knw_row, in_=s.knw_row_d).then_inc(ld, 16)
            sy.wait_ge(ld, 16 * 4)
        @block.vector
        def _(v):
            v.memset(s.ones_bf, 1.0)
            v.memset(s.ones_col, 1.0)
            v.memset(s.ones_row, 1.0)
            v.memset(s.halfc, 0.5)
            v.memset(s.nhalfc, -0.5)
            v.memset(s.m_acc1, 0.0)
            v.memset(s.m_acc2, 0.0)
            v.memset(s.m_tmpc, 0.0)
            v.drain().then_inc(ldv, 1)
    with nc.Block() as block:
        @block.tensor
        def _(t):
            t.wait_ge(ld, 16 * 4)
            t.wait_ge(ldv, 1)
            t.matmul(s.pb[0][:, 0:HD], s.ones_row, s.qnw_row, start=True, stop=True)
            t.matmul(s.pb[1][:, 0:HD], s.ones_row, s.knw_row, start=True, stop=True).then_inc(ldt, 1)
        @block.vector
        def _(v):
            v.wait_ge(ldt, 1)
            v.tensor_copy(s.qnw_b, s.pb[0][:, 0:HD])
            v.tensor_copy(s.knw_b, s.pb[1][:, 0:HD])


def emit_p1_ln(s):
    nc = s.nc
    xld = s.sem("l_xld")
    lnr = s.sem("l_lnr")
    rs1 = s.sem("l_rs1")
    rs2 = s.sem("l_rs2")
    trd = s.sem("l_trd")
    evd = s.sem("l_evd")
    spd = s.sem("l_spd")
    ctx = contextlib.ExitStack()
    sbuf = nc.sbuf_tensor
    x_st = [ctx.enter_context(sbuf(f"l_x{i}", [128, D], F32)).ap() for i in range(2)]
    ln_st = [ctx.enter_context(sbuf(f"l_ln{i}", [128, D], F32)).ap() for i in range(2)]
    hi_st = [ctx.enter_context(sbuf(f"l_hi{i}", [128, D], BF16)).ap() for i in range(2)]
    lo_st = [ctx.enter_context(sbuf(f"l_lo{i}", [128, D], BF16)).ap() for i in range(2)]
    xsq = ctx.enter_context(sbuf("l_xsq", [128, D], F32)).ap()
    bn2 = ctx.enter_context(sbuf("l_bn2", [128, 2], F32)).ap()
    q2c = ctx.enter_context(sbuf("l_q2c", [128, 1], F32)).ap()
    veps = ctx.enter_context(sbuf("l_veps", [128, 1], F32)).ap()
    rvar = ctx.enter_context(sbuf("l_rvar", [128, 1], F32)).ap()
    rstd = ctx.enter_context(sbuf("l_rstd", [128, 1], F32)).ap()
    NTC = 32

    with nc.Block() as block:
        @block.sync
        def _(sy):
            for tc in range(NTC):
                src = s.x_own if tc < 16 else s.x_oth
                r = (tc % 16) * 128
                if tc >= 2:
                    sy.wait_ge(lnr, tc - 1)
                sy.dma_start(out=x_st[tc % 2], in_=src[r:r + 128, :]).then_inc(xld, 16)
            sy.wait_ge(spd, 16 * 32 * NTC)

        @block.vector
        def _(v):
            for tc in range(NTC):
                b = tc % 2
                v.wait_ge(xld, 16 * (tc + 1))
                v.tensor_reduce(bn2[:, 0:1], x_st[b], AX.X, ALU.add)
                v.tensor_tensor(xsq, x_st[b], x_st[b], ALU.mult)
                v.tensor_reduce(q2c, xsq, AX.X, ALU.add)
                v.drain()
                v.tensor_scalar(bn2[:, 0:1], bn2[:, 0:1], 1.0 / D, None, ALU.mult)
                v.drain()
                v.tensor_tensor(veps, bn2[:, 0:1], bn2[:, 0:1], ALU.mult)
                v.drain()
                v.scalar_tensor_tensor(veps, q2c, 1.0 / D, veps, ALU.mult, ALU.subtract)
                v.drain()
                v.tensor_scalar(veps, veps, EPS_LN, None, ALU.add)
                v.drain()
                v.reciprocal(rvar, veps).then_inc(rs1, 1)
                if tc >= 2:
                    v.wait_ge(trd, 16 * (tc - 1))
                    v.wait_ge(spd, 16 * 32 * (tc - 1))
                v.wait_ge(rs2, tc + 1)
                v.tensor_scalar(ln_st[b], x_st[b], bn2[:, 0:1], None, ALU.subtract)
                v.tensor_scalar(ln_st[b], ln_st[b], rstd, None, ALU.mult).then_inc(lnr, 1)
                for fc in range(16):
                    pbt = s.pb[6 + (fc % 2)]
                    v.wait_ge(trd, tc * 16 + fc + 1)
                    v.tensor_copy(hi_st[b][:, fc * 128:(fc + 1) * 128], pbt[:, 0:128])
                    v.tensor_tensor(lo_st[b][:, fc * 128:(fc + 1) * 128], pbt[:, 0:128],
                                    hi_st[b][:, fc * 128:(fc + 1) * 128],
                                    ALU.subtract).then_inc(evd, 1)
                if tc % 8 == 7:
                    feed_prepass_v(s, v, 4)

        @block.scalar
        def _(sc):
            def spill(tc):
                bb = tc % 2
                sc.wait_ge(evd, 16 * (tc + 1))
                for fc in range(16):
                    sc.dma_start(out=s.lnT_hi[fc, :, tc * 128:(tc + 1) * 128],
                                 in_=hi_st[bb][:, fc * 128:(fc + 1) * 128]).then_inc(spd, 16)
                    sc.dma_start(out=s.lnT_lo[fc, :, tc * 128:(tc + 1) * 128],
                                 in_=lo_st[bb][:, fc * 128:(fc + 1) * 128]).then_inc(spd, 16)
            for tc in range(NTC):
                sc.wait_ge(rs1, tc + 1)
                sc.activation(rstd, rvar, AF.Sqrt).then_inc(rs2, 1)
                if tc >= 1:
                    spill(tc - 1)
            spill(NTC - 1)

        @block.tensor
        def _(t):
            for tc in range(NTC):
                t.wait_ge(lnr, tc + 1)
                for fc in range(16):
                    pbt = s.pb[6 + (fc % 2)]
                    if tc * 16 + fc >= 2:
                        t.wait_ge(evd, tc * 16 + fc - 1)
                    t.transpose(pbt[:, 0:128], ln_st[tc % 2][:, fc * 128:(fc + 1) * 128],
                                s.ident_f32).then_inc(trd, 1)

        @block.gpsimd
        def _(g):
            feed_prepass(s, g, 16)
    ctx.close()


def emit_proj_pass(s, kind):
    nc = s.nc
    is_q = kind in ("qa", "qb")
    is_v = kind == "v"
    NTC = 16 if is_q else 32
    if PP_LIMIT is not None:
        NTC = min(NTC, PP_LIMIT)
        NTC = max(4, NTC)
    w_dram = {"k": s.wkT, "v": s.wvT,
              "qa": s.wqT[:, 0:1024], "qb": s.wqT[:, 1024:2048]}[kind]
    head_off = 8 if kind == "qb" else 0
    nrm_b = s.qnw_b if is_q else s.knw_b
    dst_hi = s.qn_hi if is_q else s.kn_hi
    dst_lo = s.qn_lo if is_q else s.kn_lo

    names = ["wld", "wsp", "lnld", "pdone", "pfree", "rs1", "rs2",
             "knr", "trd", "ted", "ksr", "spd"]
    if not hasattr(s, "pp_sems"):
        s.pp_sems = {n: s.sem("pp_" + n) for n in names}
        s.pp_base = {n: 0 for n in names}
    B = dict(s.pp_base)
    SM = s.pp_sems
    wld, wsp, lnld, pdone, pfree, rs1, rs2, knr, trd, ted, ksr, spd = (
        SM[n] for n in names)

    p = kind + "_"
    ctx = contextlib.ExitStack()
    sbuf = nc.sbuf_tensor
    w_hi = ctx.enter_context(sbuf(p + "whi", [128, 16 * 1024], BF16)).ap()
    w_lo = ctx.enter_context(sbuf(p + "wlo", [128, 16 * 1024], BF16)).ap()
    wtmp = [ctx.enter_context(sbuf(p + f"wtmp{i}", [128, 1024], F32)).ap() for i in range(2)]
    lnb_hi = [ctx.enter_context(sbuf(p + f"lnhi{i}", [128, 16 * 512], BF16)).ap() for i in range(2)]
    lnb_lo = [ctx.enter_context(sbuf(p + f"lnlo{i}", [128, 16 * 512], BF16)).ap() for i in range(2)]
    kf = ctx.enter_context(sbuf(p + "kf", [128, 1024], F32)).ap()
    if is_v:
        vsp_hi = [ctx.enter_context(sbuf(p + f"vsphi{i}", [128, 1024], BF16)).ap() for i in range(2)]
        vsp_lo = [ctx.enter_context(sbuf(p + f"vsplo{i}", [128, 1024], BF16)).ap() for i in range(2)]
    else:
        ksq = ctx.enter_context(sbuf(p + "ksq", [128, 1024], F32)).ap()
        red8 = ctx.enter_context(sbuf(p + "red8", [128, 8], F32)).ap()
        rv8 = ctx.enter_context(sbuf(p + "rv8", [128, 8], F32)).ap()
        rstd8 = ctx.enter_context(sbuf(p + "rstd8", [128, 8], F32)).ap()
        knf8 = ctx.enter_context(sbuf(p + "knf8", [128, 1024], F32)).ap()
        sp_st = [ctx.enter_context(sbuf(p + f"spst{i}", [128, 16 * 128], BF16)).ap() for i in range(2)]

    NTG = NTC // 4
    pbt = [s.pb[4], s.pb[5]]

    with nc.Block() as block:
        @block.sync
        def _(sy):
            for fc in range(16):
                if fc >= 2:
                    sy.wait_ge(wsp, B["wsp"] + fc - 1)
                sy.dma_start(out=wtmp[fc % 2],
                             in_=w_dram[fc * 128:(fc + 1) * 128, :]).then_inc(wld, 16)
            for tg in range(NTG):
                if tg >= 2:
                    sy.wait_ge(pdone, B["pdone"] + 4 * (tg - 1))
                ts = tg * 512
                sy.dma_start(out=lnb_hi[tg % 2],
                             in_=s.lnT_hi[:, :, ts:ts + 512].rearrange("fc f t -> f fc t")
                             ).then_inc(lnld, 16)
                sy.dma_start(out=lnb_lo[tg % 2],
                             in_=s.lnT_lo[:, :, ts:ts + 512].rearrange("fc f t -> f fc t")
                             ).then_inc(lnld, 16)
            sy.wait_ge(spd, B["spd"] + 16 * (2 * NTC if is_v else 16 * NTC))

        @block.vector
        def _(v):
            for fc in range(16):
                v.wait_ge(wld, B["wld"] + 16 * (fc + 1))
                v.tensor_copy(w_hi[:, fc * 1024:(fc + 1) * 1024], wtmp[fc % 2])
                v.tensor_tensor(w_lo[:, fc * 1024:(fc + 1) * 1024], wtmp[fc % 2],
                                w_hi[:, fc * 1024:(fc + 1) * 1024], ALU.subtract).then_inc(wsp, 1)
            for tc in range(NTC):
                v.wait_ge(pdone, B["pdone"] + tc + 1)
                pa = s.pb[2 * (tc % 2)]
                pa1 = s.pb[2 * (tc % 2) + 1]
                v.tensor_copy(kf[:, 0:512], pa)
                v.tensor_copy(kf[:, 512:1024], pa1).then_inc(pfree, 1)
                bb = tc % 2
                if is_v:
                    if tc >= 2:
                        v.wait_ge(spd, B["spd"] + 16 * 2 * (tc - 1))
                    v.tensor_copy(vsp_hi[bb], kf)
                    v.tensor_tensor(vsp_lo[bb], kf, vsp_hi[bb], ALU.subtract).then_inc(ksr, 1)
                    if tc % 8 == 7:
                        feed_prepass_v(s, v, 5)
                    continue
                v.tensor_tensor(ksq, kf, kf, ALU.mult)
                for g8 in range(8):
                    v.tensor_reduce(red8[:, g8:g8 + 1], ksq[:, g8 * 128:(g8 + 1) * 128],
                                    AX.X, ALU.add)
                v.drain()
                v.tensor_scalar(rv8, red8, 1.0 / HD, EPS_RMS, ALU.mult, ALU.add)
                v.drain()
                v.reciprocal(rv8, rv8).then_inc(rs1, 1)
                v.wait_ge(rs2, B["rs2"] + tc + 1)
                for g8 in range(8):
                    sl = slice(g8 * 128, (g8 + 1) * 128)
                    tt = v.scalar_tensor_tensor(knf8[:, sl], kf[:, sl], rstd8[:, g8:g8 + 1],
                                                nrm_b, ALU.mult, ALU.mult)
                tt.then_inc(knr, 1)
                if tc >= 2:
                    v.wait_ge(spd, B["spd"] + 16 * 16 * (tc - 1))
                for g8 in range(8):
                    v.wait_ge(trd, B["trd"] + tc * 8 + g8 + 1)
                    pbx = pbt[g8 % 2][:, 0:128]
                    hi_sl = sp_st[bb][:, (2 * g8) * 128:(2 * g8 + 1) * 128]
                    v.tensor_copy(hi_sl, pbx).then_inc(ted, 1)
                    v.tensor_tensor(sp_st[bb][:, (2 * g8 + 1) * 128:(2 * g8 + 2) * 128],
                                    pbx, hi_sl, ALU.subtract).then_inc(ted, 1)
                if tc % 8 == 7:
                    feed_prepass_v(s, v, 5)

        @block.scalar
        def _(sc):
            def spill(tc):
                bb = tc % 2
                if is_v:
                    sc.wait_ge(ksr, B["ksr"] + tc + 1)
                    sc.dma_start(out=s.vt_hi[tc * 128:(tc + 1) * 128, :],
                                 in_=vsp_hi[bb]).then_inc(spd, 16)
                    sc.dma_start(out=s.vt_lo[tc * 128:(tc + 1) * 128, :],
                                 in_=vsp_lo[bb]).then_inc(spd, 16)
                else:
                    sc.wait_ge(ted, B["ted"] + (tc + 1) * 16)
                    for g8 in range(8):
                        h = head_off + g8
                        sc.dma_start(out=dst_hi[h, tc, :, :],
                                     in_=sp_st[bb][:, (2 * g8) * 128:(2 * g8 + 1) * 128]
                                     ).then_inc(spd, 16)
                        sc.dma_start(out=dst_lo[h, tc, :, :],
                                     in_=sp_st[bb][:, (2 * g8 + 1) * 128:(2 * g8 + 2) * 128]
                                     ).then_inc(spd, 16)
            for tc in range(NTC):
                if not is_v:
                    sc.wait_ge(rs1, B["rs1"] + tc + 1)
                    sc.activation(rstd8, rv8, AF.Sqrt).then_inc(rs2, 1)
                if tc >= 1:
                    spill(tc - 1)
            spill(NTC - 1)

        @block.tensor
        def _(t):
            def emit_tr(tc):
                t.wait_ge(knr, B["knr"] + tc + 1)
                for g8 in range(8):
                    w = tc * 16 + 2 * g8
                    if w >= 2:
                        t.wait_ge(ted, B["ted"] + w - 2)
                    t.transpose(pbt[g8 % 2][:, 0:128],
                                knf8[:, g8 * 128:(g8 + 1) * 128],
                                s.ident_f32).then_inc(trd, 1)
            t.wait_ge(wsp, B["wsp"] + 16)
            for tc in range(NTC):
                tg, tcq = tc // 4, tc % 4
                t.wait_ge(lnld, B["lnld"] + 16 * 2 * (tg + 1))
                if tc >= 2:
                    t.wait_ge(pfree, B["pfree"] + tc - 1)
                pa = s.pb[2 * (tc % 2)]
                pa1 = s.pb[2 * (tc % 2) + 1]
                lh = lnb_hi[tg % 2]
                ll = lnb_lo[tg % 2]
                for fc in range(16):
                    sl = slice(fc * 512 + tcq * 128, fc * 512 + (tcq + 1) * 128)
                    st_hi = lh[:, sl]
                    st_lo = ll[:, sl]
                    first, last = fc == 0, fc == 15
                    t.matmul(pa, st_hi, w_hi[:, fc * 1024:fc * 1024 + 512],
                             start=first, stop=False)
                    t.matmul(pa1, st_hi, w_hi[:, fc * 1024 + 512:(fc + 1) * 1024],
                             start=first, stop=False)
                    t.matmul(pa, st_hi, w_lo[:, fc * 1024:fc * 1024 + 512],
                             start=False, stop=False)
                    t.matmul(pa1, st_hi, w_lo[:, fc * 1024 + 512:(fc + 1) * 1024],
                             start=False, stop=False)
                    t.matmul(pa, st_lo, w_hi[:, fc * 1024:fc * 1024 + 512],
                             start=False, stop=last)
                    mm = t.matmul(pa1, st_lo, w_hi[:, fc * 1024 + 512:(fc + 1) * 1024],
                                  start=False, stop=last)
                    if last:
                        mm.then_inc(pdone, 1)
                if not is_v and tc >= 1:
                    emit_tr(tc - 1)
            if not is_v:
                emit_tr(NTC - 1)

        @block.gpsimd
        def _(g):
            if not PP_NO_PRE:
                feed_prepass(s, g, 20 if not is_q else 10)
    ctx.close()
    s.pp_base["wld"] += 16 * 16
    s.pp_base["wsp"] += 16
    s.pp_base["lnld"] += 16 * 2 * NTG
    s.pp_base["pdone"] += NTC
    s.pp_base["pfree"] += NTC
    if is_v:
        s.pp_base["ksr"] += NTC
        s.pp_base["spd"] += 16 * 2 * NTC
    else:
        s.pp_base["rs1"] += NTC
        s.pp_base["rs2"] += NTC
        s.pp_base["knr"] += NTC
        s.pp_base["trd"] += 8 * NTC
        s.pp_base["ted"] += 16 * NTC
        s.pp_base["spd"] += 16 * 16 * NTC


def emit_wo_split(s):
    nc = s.nc
    wld = s.sem("wo_wld")
    wsp = s.sem("wo_wsp")
    spd = s.sem("wo_spd")
    ctx = contextlib.ExitStack()
    wtmp = [ctx.enter_context(nc.sbuf_tensor(f"wo_t{i}", [128, D], F32)).ap() for i in range(2)]
    whi = [ctx.enter_context(nc.sbuf_tensor(f"wo_h{i}", [128, D], BF16)).ap() for i in range(2)]
    wlo = [ctx.enter_context(nc.sbuf_tensor(f"wo_l{i}", [128, D], BF16)).ap() for i in range(2)]
    with nc.Block() as block:
        @block.sync
        def _(sy):
            for i in range(16):
                if i >= 2:
                    sy.wait_ge(wsp, i - 1)
                sy.dma_start(out=wtmp[i % 2], in_=s.woT[i * 128:(i + 1) * 128, :]).then_inc(wld, 16)
            sy.wait_ge(spd, 16 * 32)
        @block.vector
        def _(v):
            for i in range(16):
                b = i % 2
                v.wait_ge(wld, 16 * (i + 1))
                if i >= 2:
                    v.wait_ge(spd, 16 * 2 * (i - 1))
                v.tensor_copy(whi[b], wtmp[b])
                v.tensor_tensor(wlo[b], wtmp[b], whi[b], ALU.subtract).then_inc(wsp, 1)
            feed_prepass_v(s, v, len(s.pre_plan) - s.pre_nv)
        @block.scalar
        def _(sc):
            for i in range(16):
                b = i % 2
                sc.wait_ge(wsp, i + 1)
                sc.dma_start(out=s.wo_hi[i * 128:(i + 1) * 128, :], in_=whi[b]).then_inc(spd, 16)
                sc.dma_start(out=s.wo_lo[i * 128:(i + 1) * 128, :], in_=wlo[b]).then_inc(spd, 16)
        @block.gpsimd
        def _(g):
            feed_prepass(s, g, len(s.pre_plan) - s.pre_n)
    ctx.close()


def emit_m_finalize(s):
    nc = s.nc
    fin = s.sem("mfin")
    with nc.Block() as block:
        @block.tensor
        def _(t):
            t.wait_ge(s.pre_done, 1)
            t.matmul(s.pb[0][0:1, 0:1], s.ones_col, s.m_acc1, start=True, stop=True)
            t.matmul(s.pb[1][0:1, 0:1], s.ones_col, s.m_acc2,
                     start=True, stop=True).then_inc(fin, 1)
        @block.vector
        def _(v):
            v.wait_ge(fin, 1)
            v.tensor_scalar(s.m1c, s.pb[0][0:1, 0:1], 1.0 / (D * DH), EPS_Q, ALU.mult, ALU.max)
            v.tensor_scalar(s.m2c, s.pb[1][0:1, 0:1], 1.0 / (D * DH), EPS_Q, ALU.mult, ALU.max)
            v.drain().then_inc(fin, 1)
    with nc.Block() as block:
        @block.tensor
        def _(t):
            t.wait_ge(fin, 2)
            t.matmul(s.pb[0][:, 0:1], s.ones_row, s.m2c, start=True, stop=True)
            t.matmul(s.pb[1][:, 0:1], s.ones_row, s.m1c, start=True, stop=True).then_inc(fin, 1)
        @block.vector
        def _(v):
            v.wait_ge(fin, 3)
            v.tensor_copy(s.m2col, s.pb[0][:, 0:1])
            v.reciprocal(s.s1col, s.pb[1][:, 0:1])
            v.reciprocal(s.s2col, s.pb[0][:, 0:1])
            if DEBUG:
                v.drain()
                v.tensor_copy(s.dbg_st[0:1, 0:1], s.m1c)
                v.tensor_copy(s.dbg_st[0:1, 1:2], s.m2c)
                v.tensor_copy(s.dbg_st[0:1, 2:3], s.s1col[0:1, 0:1])
                v.drain().then_inc(fin, 1)
        @block.scalar
        def _(sc):
            if DEBUG:
                sc.wait_ge(fin, 4)
                sc.dma_start(out=s.dbg_m, in_=s.dbg_st[0:1, 0:4]).then_inc(fin, 16)
                sc.wait_ge(fin, 20)


def emit_p2(s):
    nc = s.nc

    kvld = s.sem("a_kvld")
    qld = s.sem("a_qld")
    ssem = s.sem("a_ssem")
    esem = s.sem("a_esem")
    usem = s.sem("a_usem")
    csem = s.sem("a_csem")
    tosem = s.sem("a_tosem")
    recsem = s.sem("a_recsem")
    recbsem = s.sem("a_recbsem")
    recbfree = s.sem("a_recbfree")
    aosem = s.sem("a_aosem")
    wold = s.sem("a_wold")
    ydone = s.sem("a_ydone")
    yev = s.sem("a_yev")
    ytr = s.sem("a_ytr")
    yevp = s.sem("a_yevp")
    ysp = s.sem("a_ysp")

    ctx = contextlib.ExitStack()
    sbuf = nc.sbuf_tensor
    K_hi = [ctx.enter_context(sbuf(f"a_khi{i}", [128, L], BF16)).ap() for i in range(2)]
    K_lo = [ctx.enter_context(sbuf(f"a_klo{i}", [128, L], BF16)).ap() for i in range(2)]
    V_hi = [ctx.enter_context(sbuf(f"a_vhi{i}", [128, L], BF16)).ap() for i in range(2)]
    V_lo = [ctx.enter_context(sbuf(f"a_vlo{i}", [128, L], BF16)).ap() for i in range(2)]
    Q_hi = [ctx.enter_context(sbuf(f"a_qhi{i}", [128, 512], BF16)).ap() for i in range(2)]
    Q_lo = [ctx.enter_context(sbuf(f"a_qlo{i}", [128, 512], BF16)).ap() for i in range(2)]
    U_f = [ctx.enter_context(sbuf(f"a_uf{i}", [128, 512], F32)).ap() for i in range(2)]
    U_hi = [ctx.enter_context(sbuf(f"a_uhi{i}", [128, 512], BF16)).ap() for i in range(2)]
    U_lo = [ctx.enter_context(sbuf(f"a_ulo{i}", [128, 512], BF16)).ap() for i in range(2)]
    mask_sb = ctx.enter_context(sbuf("a_mask", [128, 8 * 1024], BF16)).ap()
    rec_row = [ctx.enter_context(sbuf(f"a_rr{i}", [1, 512], F32)).ap() for i in range(2)]
    rec_b = ctx.enter_context(sbuf("a_recb", [128, 512], F32)).ap()
    ao_f = ctx.enter_context(sbuf("a_aof", [128, 512], F32)).ap()
    ao_hi = ctx.enter_context(sbuf("a_aohi", [128, NH * 512], BF16)).ap()
    ao_lo = ctx.enter_context(sbuf("a_aolo", [128, NH * 512], BF16)).ap()
    woc_hi = [ctx.enter_context(sbuf(f"a_wochi{i}", [128, D], BF16)).ap() for i in range(2)]
    woc_lo = [ctx.enter_context(sbuf(f"a_woclo{i}", [128, D], BF16)).ap() for i in range(2)]
    y_oc = ctx.enter_context(sbuf("a_yoc", [128, 512], F32)).ap()
    piece = [ctx.enter_context(sbuf(f"a_piece{i}", [128, 128], F32)).ap() for i in range(4)]

    psS = [s.pb[0], s.pb[1]]
    psO = s.pb[2]
    psSum = s.pb[3][0:1, :]
    psRB = s.pb[4]
    psY = [s.pb[4], s.pb[5]]
    psTR = [s.pb[6], s.pb[7]]

    with nc.Block() as block:
        @block.sync
        def _(sy):
            sy.dma_start(out=mask_sb, in_=s.mask_d.rearrange("c p q -> p c q")).then_inc(qld, 16)
            for qg in range(4):
                for g in range(8):
                    gi = qg * 8 + g
                    if gi >= 2:
                        gp = gi - 2
                        sy.wait_ge(recbsem, (gp // 8) * 16 + (gp % 8) * 2 + 2)
                    sy.dma_start(out=K_hi[g % 2],
                                 in_=s.kn_hi[g].rearrange("kc hd t -> hd kc t")).then_inc(kvld, 16)
                    sy.dma_start(out=K_lo[g % 2],
                                 in_=s.kn_lo[g].rearrange("kc hd t -> hd kc t")).then_inc(kvld, 16)
                    sy.dma_start(out=V_hi[g % 2],
                                 in_=s.vt_hi[:, g * 128:(g + 1) * 128].rearrange(
                                     "(kc p) hd -> p kc hd", p=128)).then_inc(kvld, 16)
                    sy.dma_start(out=V_lo[g % 2],
                                 in_=s.vt_lo[:, g * 128:(g + 1) * 128].rearrange(
                                     "(kc p) hd -> p kc hd", p=128)).then_inc(kvld, 16)
                    for hh in range(2):
                        U = qg * 16 + g * 2 + hh
                        h = g * 2 + hh
                        if U >= 2:
                            sy.wait_ge(tosem, U - 1)
                        sy.dma_start(out=Q_hi[U % 2],
                                     in_=s.qn_hi[h, qg * 4:(qg + 1) * 4].rearrange(
                                         "tc hd t -> hd tc t")).then_inc(qld, 16)
                        sy.dma_start(out=Q_lo[U % 2],
                                     in_=s.qn_lo[h, qg * 4:(qg + 1) * 4].rearrange(
                                         "tc hd t -> hd tc t")).then_inc(qld, 16)
                for oc in range(16):
                    oci = qg * 16 + oc
                    if oci >= 2:
                        sy.wait_ge(ydone, oci - 1)
                    sy.dma_start(out=woc_hi[oc % 2],
                                 in_=s.wo_hi[:, oc * 128:(oc + 1) * 128].rearrange(
                                     "(hc f) o -> f hc o", f=128)).then_inc(wold, 16)
                    sy.dma_start(out=woc_lo[oc % 2],
                                 in_=s.wo_lo[:, oc * 128:(oc + 1) * 128].rearrange(
                                     "(hc f) o -> f hc o", f=128)).then_inc(wold, 16)
            sy.wait_ge(ysp, 16 * 64 * 4)

        @block.tensor
        def _(t):
            def consume(U, kc, vh, vl):
                t.wait_ge(usem, U * 32 + kc + 1)
                b = kc % 2
                ks = slice(kc * 128, (kc + 1) * 128)
                t.matmul(psSum, s.ones_bf, U_hi[b], start=(kc == 0), stop=False)
                mmsum = t.matmul(psSum, s.ones_bf, U_lo[b], start=False, stop=(kc == 31))
                if kc == 31:
                    mmsum.then_inc(tosem, 1)
                t.matmul(psO, vh[:, ks], U_hi[b], start=(kc == 0), stop=False)
                t.matmul(psO, vh[:, ks], U_lo[b], start=False, stop=False)
                t.matmul(psO, vl[:, ks], U_hi[b],
                         start=False, stop=(kc == 31)).then_inc(csem, 1)

            def ytrans(qg, oc):
                t.wait_ge(yev, qg * 16 + oc + 1)
                for pc in range(4):
                    n = (qg * 16 + oc) * 4 + pc
                    if n >= 2:
                        t.wait_ge(yevp, n - 1)
                    t.transpose(psTR[pc % 2][:, 0:128],
                                y_oc[:, pc * 128:(pc + 1) * 128], s.ident_f32).then_inc(ytr, 1)

            for qg in range(4):
                for g in range(8):
                    gi = qg * 8 + g
                    t.wait_ge(kvld, 16 * 4 * (gi + 1))
                    for hh in range(2):
                        u = g * 2 + hh
                        U = qg * 16 + u
                        t.wait_ge(qld, 16 * (2 * (U + 1) + 1))
                        t.wait_ge(aosem, U)
                        t.wait_ge(recsem, U)
                        kh, kl = K_hi[g % 2], K_lo[g % 2]
                        vh, vl = V_hi[g % 2], V_lo[g % 2]
                        qh, ql = Q_hi[U % 2], Q_lo[U % 2]
                        for kc in range(32):
                            pS = psS[kc % 2]
                            t.wait_ge(esem, max(0, U * 32 + kc - 1))
                            ks = slice(kc * 128, (kc + 1) * 128)
                            masked = (8 <= kc < 16) and qg >= 2
                            t.matmul(pS, kh[:, ks], qh, start=True, stop=False)
                            t.matmul(pS, kh[:, ks], ql, start=False, stop=False)
                            mm3 = t.matmul(pS, kl[:, ks], qh, start=False, stop=not masked)
                            if masked:
                                t.matmul(pS, s.ident_bf,
                                         mask_sb[:, (kc - 8) * 1024 + (qg - 2) * 512:
                                                 (kc - 8) * 1024 + (qg - 2) * 512 + 512],
                                         start=False, stop=True).then_inc(ssem, 1)
                            else:
                                mm3.then_inc(ssem, 1)
                            if kc >= 1:
                                consume(U, kc - 1, vh, vl)
                        consume(U, 31, vh, vl)
                        t.wait_ge(recsem, U + 1)
                        if U >= 1:
                            t.wait_ge(recbfree, U)
                        if u == 0:
                            t.wait_ge(yev, U)
                        t.matmul(psRB, s.ones_row, rec_row[U % 2],
                                 start=True, stop=True).then_inc(recbsem, 1)
                t.wait_ge(aosem, qg * 16 + 16)
                for oc in range(16):
                    oci = qg * 16 + oc
                    t.wait_ge(wold, 16 * 2 * (oci + 1))
                    if oc >= 2:
                        t.wait_ge(yev, qg * 16 + oc - 1)
                    pY = psY[oc % 2]
                    wh, wl = woc_hi[oc % 2], woc_lo[oc % 2]
                    for h in range(NH):
                        ah = ao_hi[:, h * 512:(h + 1) * 512]
                        al = ao_lo[:, h * 512:(h + 1) * 512]
                        wt_h = wh[:, h * 128:(h + 1) * 128]
                        wt_l = wl[:, h * 128:(h + 1) * 128]
                        t.matmul(pY, wt_h, ah, start=(h == 0), stop=False)
                        t.matmul(pY, wt_h, al, start=False, stop=False)
                        mmy = t.matmul(pY, wt_l, ah, start=False, stop=(h == NH - 1))
                    mmy.then_inc(ydone, 1)
                    if oc >= 1:
                        ytrans(qg, oc - 1)
                ytrans(qg, 15)

        @block.scalar
        def _(sc):
            for U in range(64):
                for kc in range(32):
                    sc.wait_ge(ssem, U * 32 + kc + 1)
                    sc.wait_ge(usem, max(0, U * 32 + kc - 1))
                    sc.activation(U_f[kc % 2], psS[kc % 2], AF.Exp, scale=SCALE).then_inc(esem, 1)

        @block.vector
        def _(v):
            for qg in range(4):
                for u in range(16):
                    U = qg * 16 + u
                    if u == 0 and qg >= 1:
                        v.wait_ge(ydone, qg * 16)
                    for kc in range(32):
                        b = kc % 2
                        v.wait_ge(esem, U * 32 + kc + 1)
                        v.wait_ge(csem, max(0, U * 32 + kc - 1))
                        v.tensor_copy(U_hi[b], U_f[b])
                        v.tensor_tensor(U_lo[b], U_f[b], U_hi[b], ALU.subtract).then_inc(usem, 1)
                    v.wait_ge(tosem, U + 1)
                    v.reciprocal(rec_row[U % 2], psSum).then_inc(recsem, 1)
                    v.wait_ge(recbsem, U + 1)
                    v.tensor_copy(rec_b, psRB).then_inc(recbfree, 1)
                    v.wait_ge(csem, U * 32 + 32)
                    v.tensor_tensor(ao_f, psO, rec_b, ALU.mult)
                    v.tensor_copy(ao_hi[:, u * 512:(u + 1) * 512], ao_f)
                    v.tensor_tensor(ao_lo[:, u * 512:(u + 1) * 512], ao_f,
                                    ao_hi[:, u * 512:(u + 1) * 512], ALU.subtract).then_inc(aosem, 1)
                for oc in range(16):
                    v.wait_ge(ydone, qg * 16 + oc + 1)
                    v.tensor_copy(y_oc, psY[oc % 2]).then_inc(yev, 1)
                    for pc in range(4):
                        n = (qg * 16 + oc) * 4 + pc
                        v.wait_ge(ytr, n + 1)
                        if n >= 4:
                            v.wait_ge(ysp, 16 * (n - 3))
                        v.tensor_copy(piece[pc],
                                      psTR[pc % 2][:, 0:128]).then_inc(yevp, 1)

        @block.gpsimd
        def _(g):
            for qg in range(4):
                for oc in range(16):
                    for pc in range(4):
                        n = (qg * 16 + oc) * 4 + pc
                        g.wait_ge(yevp, n + 1)
                        tc = qg * 4 + pc
                        g.dma_start(out=s.y_sc[tc * 128:(tc + 1) * 128, oc * 128:(oc + 1) * 128],
                                    in_=piece[pc]).then_inc(ysp, 16)
            g.wait_ge(ysp, 16 * 64 * 4)
    ctx.close()


def emit_p3(s):
    nc = s.nc
    yld = s.sem("p3_yld")
    x2r = s.sem("p3_x2r")
    rs1 = s.sem("p3_rs1")
    rs2 = s.sem("p3_rs2")
    er = s.sem("p3_er")
    sg = s.sem("p3_sg")
    xqr = s.sem("p3_xqr")
    trd = s.sem("p3_trd")
    ted = s.sem("p3_ted")
    spd = s.sem("p3_spd")

    ctx = contextlib.ExitStack()
    sbuf = nc.sbuf_tensor
    y_st = [ctx.enter_context(sbuf(f"p3_y{i}", [128, D], F32)).ap() for i in range(2)]
    x_st = [ctx.enter_context(sbuf(f"p3_x{i}", [128, D], F32)).ap() for i in range(2)]
    x2_st = [ctx.enter_context(sbuf(f"p3_x2{i}", [128, D], F32)).ap() for i in range(2)]
    ln2 = ctx.enter_context(sbuf("p3_ln2", [128, D], F32)).ap()
    z_st = ctx.enter_context(sbuf("p3_z", [128, D], F32)).ap()
    sti = ctx.enter_context(sbuf("p3_sti", [128, D], I32)).ap()
    e_st = ctx.enter_context(sbuf("p3_e", [128, D], F32)).ap()
    sg1 = ctx.enter_context(sbuf("p3_sg1", [128, D], BF16)).ap()
    sg2 = ctx.enter_context(sbuf("p3_sg2", [128, D], BF16)).ap()
    cs_st = ctx.enter_context(sbuf("p3_cs", [128, D], BF16)).ap()
    xq_st = [ctx.enter_context(sbuf(f"p3_xq{i}", [128, D], F32)).ap() for i in range(2)]
    xsp = [ctx.enter_context(sbuf(f"p3_xsp{i}", [128, D], BF16)).ap() for i in range(2)]
    xsq = ctx.enter_context(sbuf("p3_xsq", [128, D], F32)).ap()
    bn2 = ctx.enter_context(sbuf("p3_bn2", [128, 2], F32)).ap()
    q2c = ctx.enter_context(sbuf("p3_q2c", [128, 1], F32)).ap()
    col = ctx.enter_context(sbuf("p3_col", [128, 1], F32)).ap()
    rstd = ctx.enter_context(sbuf("p3_rstd", [128, 1], F32)).ap()
    amx = ctx.enter_context(sbuf("p3_amx", [128, 1], F32)).ap()
    scol = ctx.enter_context(sbuf("p3_scol", [128, 1], F32)).ap()
    pbt = [s.pb[2], s.pb[3]]

    with nc.Block() as block:
        @block.sync
        def _(sy):
            for tc in range(16):
                if tc >= 2:
                    sy.wait_ge(x2r, tc - 1)
                sy.dma_start(out=y_st[tc % 2],
                             in_=s.y_sc[tc * 128:(tc + 1) * 128, :]).then_inc(yld, 16)
                sy.dma_start(out=x_st[tc % 2],
                             in_=s.x_own[tc * 128:(tc + 1) * 128, :]).then_inc(yld, 16)
            sy.wait_ge(spd, 16 * 17 * 16)

        @block.vector
        def _(v):
            for tc in range(16):
                b = tc % 2
                v.wait_ge(yld, 16 * 2 * (tc + 1))
                if tc >= 2:
                    v.wait_ge(spd, 16 * 17 * (tc - 1))
                v.tensor_tensor(x2_st[b], y_st[b], x_st[b], ALU.add).then_inc(x2r, 1)
                v.tensor_reduce(bn2[:, 0:1], x2_st[b], AX.X, ALU.add)
                v.tensor_tensor(xsq, x2_st[b], x2_st[b], ALU.mult)
                v.tensor_reduce(q2c, xsq, AX.X, ALU.add)
                v.drain()
                v.tensor_scalar(bn2[:, 0:1], bn2[:, 0:1], 1.0 / D, None, ALU.mult)
                v.drain()
                v.tensor_tensor(col, bn2[:, 0:1], bn2[:, 0:1], ALU.mult)
                v.drain()
                v.scalar_tensor_tensor(col, q2c, 1.0 / D, col, ALU.mult, ALU.subtract)
                v.drain()
                v.tensor_scalar(col, col, EPS_LN, None, ALU.add)
                v.drain()
                v.reciprocal(col, col).then_inc(rs1, 1)
                v.wait_ge(rs2, tc + 1)
                v.tensor_scalar(ln2, x2_st[b], bn2[:, 0:1], None, ALU.subtract)
                v.tensor_scalar(ln2, ln2, rstd, None, ALU.mult)
                v.tensor_reduce(amx, ln2, AX.X, ALU.max, apply_absolute_value=True)
                v.drain()
                v.tensor_scalar(amx, amx, EPS_Q, None, ALU.max)
                v.drain()
                v.reciprocal(scol, amx)
                v.drain()
                v.tensor_scalar(scol, scol, 127.0, None, ALU.mult)
                v.drain()
                v.tensor_scalar(z_st, ln2, scol, None, ALU.mult)
                v.tensor_copy(sti, z_st)
                v.tensor_tensor(e_st, z_st, sti, ALU.subtract).then_inc(er, 1)
                v.wait_ge(sg, tc + 1)
                v.tensor_tensor(cs_st, sg1, sg2, ALU.add)
                v.scalar_tensor_tensor(xq_st[b], cs_st, 0.5, sti, ALU.mult, ALU.add).then_inc(xqr, 1)
                for fc in range(16):
                    n = tc * 17 + fc
                    v.wait_ge(trd, n + 1)
                    v.tensor_copy(xsp[b][:, fc * 128:(fc + 1) * 128],
                                  pbt[fc % 2][:, 0:128]).then_inc(ted, 1)
                v.wait_ge(trd, tc * 17 + 17)
                v.tensor_copy(s.amax1_row[0:1, tc * 128:(tc + 1) * 128],
                              s.pb[1][0:1, 0:128]).then_inc(ted, 1)

        @block.scalar
        def _(sc):
            for tc in range(16):
                sc.wait_ge(rs1, tc + 1)
                sc.activation(rstd, col, AF.Sqrt).then_inc(rs2, 1)
                sc.wait_ge(er, tc + 1)
                sc.sign(sg1, e_st, bias=s.nhalfc)
                sc.sign(sg2, e_st, bias=s.halfc).then_inc(sg, 1)

        @block.tensor
        def _(t):
            for tc in range(16):
                t.wait_ge(xqr, tc + 1)
                for fc in range(16):
                    n = tc * 17 + fc
                    if n >= 2:
                        t.wait_ge(ted, n - 1)
                    t.transpose(pbt[fc % 2][:, 0:128],
                                xq_st[tc % 2][:, fc * 128:(fc + 1) * 128],
                                s.ident_f32).then_inc(trd, 1)
                if tc >= 1:
                    t.wait_ge(ted, tc * 17)
                t.transpose(s.pb[1][0:1, 0:128], amx, s.ident_f32).then_inc(trd, 1)

        @block.gpsimd
        def _(g):
            for tc in range(16):
                b = tc % 2
                g.wait_ge(x2r, tc + 1)
                g.dma_start(out=s.x2_sc[tc * 128:(tc + 1) * 128, :],
                            in_=x2_st[b]).then_inc(spd, 16)
                g.wait_ge(ted, tc * 17 + 16)
                for fc in range(16):
                    g.dma_start(out=s.xq_sc[fc, :, tc * 128:(tc + 1) * 128],
                                in_=xsp[b][:, fc * 128:(fc + 1) * 128]).then_inc(spd, 16)
            g.wait_ge(spd, 16 * 17 * 16)
    ctx.close()


def emit_c_rows(s):
    nc = s.nc
    finv = s.sem("cr_finv")
    fint = s.sem("cr_fint")
    with nc.Block() as block:
        @block.vector
        def _(v):
            v.tensor_scalar(s.amax1_row, s.amax1_row, s.m1c, 1.0 / 254.0, ALU.mult, ALU.mult)
            v.drain()
            v.drain().then_inc(finv, 1)
        @block.tensor
        def _(t):
            t.wait_ge(finv, 1)
            for i in range(4):
                t.matmul(s.pb[i][:, 0:512], s.ones_row,
                         s.amax1_row[0:1, i * 512:(i + 1) * 512], start=True, stop=True)
            t.drain().then_inc(fint, 1)
    with nc.Block() as block:
        @block.vector
        def _(v):
            v.wait_ge(fint, 1)
            for i in range(4):
                v.tensor_copy(s.c1_b[:, i * 512:(i + 1) * 512], s.pb[i][:, 0:512])


def emit_ffn_a(s):
    nc = s.nc
    xqld = s.sem("fa_xqld")
    w1ld = s.sem("fa_w1ld")
    w2ld = s.sem("fa_w2ld")
    z1r = s.sem("fa_z1r")
    z2r = s.sem("fa_z2r")
    sg1d = s.sem("fa_sg1d")
    sg2d = s.sem("fa_sg2d")
    t1r = s.sem("fa_t1r")
    t2r = s.sem("fa_t2r")
    mmd = s.sem("fa_mmd")
    hev = s.sem("fa_hev")
    gel = s.sem("fa_gel")
    mxd = s.sem("fa_mxd")
    gsp = s.sem("fa_gsp")
    t2sp = s.sem("fa_t2sp")
    amp = s.sem("fa_amp")
    amc = s.sem("fa_amc")

    ctx = contextlib.ExitStack()
    sbuf = nc.sbuf_tensor
    xqT = ctx.enter_context(sbuf("fa_xqT", [128, 16 * T], BF16)).ap()
    w1t = ctx.enter_context(sbuf("fa_w1t", [128, D], F32)).ap()
    w2t = ctx.enter_context(sbuf("fa_w2t", [128, D], F32)).ap()
    z1 = ctx.enter_context(sbuf("fa_z1", [128, D], F32)).ap()
    z2 = ctx.enter_context(sbuf("fa_z2", [128, D], F32)).ap()
    sa1 = ctx.enter_context(sbuf("fa_sa1", [128, D], BF16)).ap()
    sb1 = ctx.enter_context(sbuf("fa_sb1", [128, D], BF16)).ap()
    sa2 = ctx.enter_context(sbuf("fa_sa2", [128, D], BF16)).ap()
    sb2 = ctx.enter_context(sbuf("fa_sb2", [128, D], BF16)).ap()
    t1 = [ctx.enter_context(sbuf(f"fa_t1_{i}", [128, D], BF16)).ap() for i in range(2)]
    t2 = [ctx.enter_context(sbuf(f"fa_t2_{i}", [128, D], BF16)).ap() for i in range(2)]
    h_st = ctx.enter_context(sbuf("fa_h", [128, T], F32)).ap()
    g_st = [ctx.enter_context(sbuf(f"fa_g{i}", [128, T], F32)).ap() for i in range(2)]
    mx = ctx.enter_context(sbuf("fa_mx", [128, T], F32)).ap()
    mn = ctx.enter_context(sbuf("fa_mn", [128, T], F32)).ap()
    NDH = 64

    with nc.Block() as block:
        @block.sync
        def _(sy):
            for fc in range(16):
                sy.dma_start(out=xqT[:, fc * T:(fc + 1) * T], in_=s.xq_sc[fc]).then_inc(xqld, 16)
            for dh in range(NDH):
                if dh >= 1:
                    sy.wait_ge(z1r, dh)
                sy.dma_start(out=w1t,
                             in_=s.w1T[:, dh * 128:(dh + 1) * 128].rearrange(
                                 "(fc f) o -> f fc o", f=128)).then_inc(w1ld, 16)
                if dh >= 1:
                    sy.wait_ge(z2r, dh)
                sy.dma_start(out=w2t,
                             in_=s.w2T[dh * 128:(dh + 1) * 128, :]).then_inc(w2ld, 16)
            sy.wait_ge(gsp, 16 * NDH)
            sy.wait_ge(t2sp, 16 * NDH)

        @block.vector
        def _(v):
            v.wait_ge(s.sems["mfin"], 3)
            for dh in range(NDH):
                b = dh % 2
                v.wait_ge(w1ld, 16 * (dh + 1))
                if dh >= 1:
                    v.wait_ge(sg1d, dh)
                v.tensor_scalar(z1, w1t, s.s1col, None, ALU.mult).then_inc(z1r, 1)
                v.wait_ge(w2ld, 16 * (dh + 1))
                if dh >= 1:
                    v.wait_ge(sg2d, dh)
                v.tensor_scalar(z2, w2t, s.s2col, None, ALU.mult).then_inc(z2r, 1)
                v.wait_ge(sg1d, dh + 1)
                v.tensor_tensor(t1[b], sa1, sb1, ALU.add).then_inc(t1r, 1)
                v.wait_ge(sg2d, dh + 1)
                if dh >= 2:
                    v.wait_ge(t2sp, 16 * (dh - 1))
                v.tensor_tensor(t2[b], sa2, sb2, ALU.add).then_inc(t2r, 1)
                v.wait_ge(mmd, dh + 1)
                pset = 4 * (dh % 2)
                for i in range(4):
                    tth = v.tensor_tensor(h_st[:, i * 512:(i + 1) * 512],
                                          s.pb[pset + i], s.c1_b[:, i * 512:(i + 1) * 512],
                                          ALU.mult)
                tth.then_inc(hev, 1)
                v.wait_ge(gel, dh + 1)
                if dh == 0:
                    v.tensor_copy(mx, g_st[b])
                    v.tensor_copy(mn, g_st[b]).then_inc(mxd, 1)
                else:
                    v.tensor_tensor(mx, mx, g_st[b], ALU.max)
                    v.tensor_tensor(mn, mn, g_st[b], ALU.min).then_inc(mxd, 1)
            v.tensor_scalar(mn, mn, -1.0, None, ALU.mult)
            v.tensor_tensor(mx, mx, mn, ALU.max)
            v.drain().then_inc(amc, 1)
            v.wait_ge(amp, 1)
            for i in range(16):
                v.tensor_reduce(s.am2t[:, i:i + 1],
                                s.pb[i // 4][:, (i % 4) * 128:(i % 4 + 1) * 128],
                                AX.X, ALU.max)
            v.drain()
            v.tensor_scalar(s.am2t, s.am2t, EPS_Q, None, ALU.max)
            v.drain()
            v.tensor_tensor(s.c2_cols, s.am2t, s.m2col.broadcast_to([128, 16]), ALU.mult)
            v.drain()
            v.tensor_scalar(s.c2_cols, s.c2_cols, 1.0 / 254.0, None, ALU.mult)
            v.drain()

        @block.scalar
        def _(sc):
            for dh in range(NDH):
                b = dh % 2
                sc.wait_ge(z1r, dh + 1)
                if dh >= 1:
                    sc.wait_ge(t1r, dh)
                sc.sign(sa1, z1, bias=s.nhalfc)
                sc.sign(sb1, z1, bias=s.halfc).then_inc(sg1d, 1)
                sc.wait_ge(z2r, dh + 1)
                if dh >= 1:
                    sc.wait_ge(t2r, dh)
                sc.sign(sa2, z2, bias=s.nhalfc)
                sc.sign(sb2, z2, bias=s.halfc).then_inc(sg2d, 1)
                sc.wait_ge(hev, dh + 1)
                if dh >= 2:
                    sc.wait_ge(gsp, 16 * (dh - 1))
                sc.activation(g_st[b], h_st, AF.Gelu).then_inc(gel, 1)

        @block.tensor
        def _(t):
            t.wait_ge(xqld, 16 * 16)
            for dh in range(NDH):
                b = dh % 2
                t.wait_ge(t1r, dh + 1)
                if dh >= 2:
                    t.wait_ge(hev, dh - 1)
                pset = 4 * (dh % 2)
                for fc in range(16):
                    for i in range(4):
                        mmf = t.matmul(s.pb[pset + i], t1[b][:, fc * 128:(fc + 1) * 128],
                                       xqT[:, fc * T + i * 512: fc * T + (i + 1) * 512],
                                       start=(fc == 0), stop=(fc == 15))
                mmf.then_inc(mmd, 1)
            t.wait_ge(amc, 1)
            for i in range(16):
                t.transpose(s.pb[i // 4][:, (i % 4) * 128:(i % 4 + 1) * 128],
                            mx[:, i * 128:(i + 1) * 128], s.ident_f32)
            t.drain().then_inc(amp, 1)

        @block.gpsimd
        def _(g):
            for dh in range(NDH):
                b = dh % 2
                g.wait_ge(t2r, dh + 1)
                g.dma_start(out=s.t2_sc[dh * 128:(dh + 1) * 128, :], in_=t2[b]).then_inc(t2sp, 16)
                g.wait_ge(mxd, dh + 1)
                g.dma_start(out=s.g_sc[dh * 128:(dh + 1) * 128, :], in_=g_st[b]).then_inc(gsp, 16)
            g.wait_ge(gsp, 16 * NDH)
            g.wait_ge(t2sp, 16 * NDH)
    ctx.close()

    fpp = s.sem("fa_fpp")
    fpc = s.sem("fa_fpc")
    am16 = nc.alloc_sbuf_tensor("fa_am16", [16, 128], F32).ap()
    with nc.Block() as block:
        @block.tensor
        def _(t):
            t.transpose(s.pb[2][0:16, 0:128], s.am2t, s.ident_f32).then_inc(fpp, 1)
        @block.vector
        def _(v):
            v.wait_ge(fpp, 1)
            v.tensor_copy(am16, s.pb[2][0:16, 0:128]).then_inc(fpc, 1)
        @block.scalar
        def _(sc):
            sc.wait_ge(fpc, 1)
            sc.dma_start(out=s.amax2_row,
                         in_=am16).then_inc(fpc, 16)
            sc.wait_ge(fpc, 17)
    f2v = s.sem("fa_f2v")
    f2t = s.sem("fa_f2t")
    with nc.Block() as block:
        @block.vector
        def _(v):
            v.reciprocal(s.amax2_row, s.amax2_row)
            v.drain()
            v.tensor_scalar(s.amax2_row, s.amax2_row, 127.0, None, ALU.mult)
            v.drain()
            v.drain().then_inc(f2v, 1)
        @block.tensor
        def _(t):
            t.wait_ge(f2v, 1)
            for i in range(4):
                t.matmul(s.pb[i][:, 0:512], s.ones_row,
                         s.amax2_row[0:1, i * 512:(i + 1) * 512], start=True, stop=True)
            t.drain().then_inc(f2t, 1)
    with nc.Block() as block:
        @block.vector
        def _(v):
            v.wait_ge(f2t, 1)
            for i in range(4):
                v.tensor_copy(s.s2_b[:, i * 512:(i + 1) * 512], s.pb[i][:, 0:512])


def emit_ffn_b(s):
    nc = s.nc
    gld = s.sem("fb_gld")
    zr = s.sem("fb_zr")
    sgd = s.sem("fb_sgd")
    hqr = s.sem("fb_hqr")
    hqsp = s.sem("fb_hqsp")
    t2ld = s.sem("fb_t2ld")
    hqld = s.sem("fb_hqld")
    x2ld = s.sem("fb_x2ld")
    mmd = s.sem("fb_mmd")
    yev = s.sem("fb_yev")
    yout = s.sem("fb_yout")
    NDH = 64

    ctx = contextlib.ExitStack()
    sbuf = nc.sbuf_tensor
    g_ld = [ctx.enter_context(sbuf(f"fb_g{i}", [128, T], F32)).ap() for i in range(2)]
    z_st = ctx.enter_context(sbuf("fb_z", [128, T], F32)).ap()
    sti = ctx.enter_context(sbuf("fb_sti", [128, T], I32)).ap()
    e_st = ctx.enter_context(sbuf("fb_e", [128, T], F32)).ap()
    sg1 = ctx.enter_context(sbuf("fb_sg1", [128, T], BF16)).ap()
    sg2 = ctx.enter_context(sbuf("fb_sg2", [128, T], BF16)).ap()
    cs_st = ctx.enter_context(sbuf("fb_cs", [128, T], BF16)).ap()
    hq_st = [ctx.enter_context(sbuf(f"fb_hq{i}", [128, T], BF16)).ap() for i in range(2)]

    with nc.Block() as block:
        @block.vector
        def _(v):
            for dh in range(NDH):
                b = dh % 2
                v.wait_ge(gld, 16 * (dh + 1))
                v.tensor_tensor(z_st, g_ld[b], s.s2_b, ALU.mult)
                v.tensor_copy(sti, z_st)
                v.tensor_tensor(e_st, z_st, sti, ALU.subtract).then_inc(zr, 1)
                v.wait_ge(sgd, dh + 1)
                v.tensor_tensor(cs_st, sg1, sg2, ALU.add)
                if dh >= 2:
                    v.wait_ge(hqsp, 16 * (dh - 1))
                v.scalar_tensor_tensor(hq_st[b], cs_st, 0.5, sti,
                                       ALU.mult, ALU.add).then_inc(hqr, 1)

        @block.scalar
        def _(sc):
            for dh in range(NDH):
                sc.wait_ge(zr, dh + 1)
                sc.sign(sg1, e_st, bias=s.nhalfc)
                sc.sign(sg2, e_st, bias=s.halfc).then_inc(sgd, 1)

        @block.gpsimd
        def _(g):
            for dh in range(NDH):
                b = dh % 2
                g.wait_ge(hqr, dh + 1)
                g.dma_start(out=s.hq_sc[dh * 128:(dh + 1) * 128, :],
                            in_=hq_st[b]).then_inc(hqsp, 16)
            g.wait_ge(hqsp, 16 * NDH)

        @block.sync
        def _(sy):
            for dh in range(NDH):
                if dh >= 2:
                    sy.wait_ge(zr, dh - 1)
                sy.dma_start(out=g_ld[dh % 2],
                             in_=s.g_sc[dh * 128:(dh + 1) * 128, :]).then_inc(gld, 16)
            sy.wait_ge(hqsp, 16 * NDH)
    ctx.close()

    ctx = contextlib.ExitStack()
    T2 = ctx.enter_context(sbuf("fb_T2", [128, 64 * 512], BF16)).ap()
    hqcol = ctx.enter_context(sbuf("fb_hqcol", [128, 64 * 512], BF16)).ap()
    x2s = [ctx.enter_context(sbuf(f"fb_x2s{i}", [128, 4 * 512], F32)).ap() for i in range(2)]
    y_st = [ctx.enter_context(sbuf(f"fb_y{i}", [128, 4 * 512], F32)).ap() for i in range(2)]

    with nc.Block() as block:
        @block.sync
        def _(sy):
            for ocg in range(4):
                if ocg >= 1:
                    sy.wait_ge(mmd, ocg * 4)
                src = s.t2_sc[:, ocg * 512:(ocg + 1) * 512].rearrange("(dc p) o -> p dc o", p=128)
                for part in range(8):
                    sy.dma_start(out=T2[:, part * 8 * 512:(part + 1) * 8 * 512],
                                 in_=src[:, part * 8:(part + 1) * 8, :]).then_inc(t2ld, 16)
                for tcb in range(4):
                    n = ocg * 4 + tcb
                    bb = n % 2
                    if n >= 2:
                        sy.wait_ge(yev, 4 * (n - 1))
                    for tci in range(4):
                        tc = tcb * 4 + tci
                        sy.dma_start(out=x2s[bb][:, tci * 512:(tci + 1) * 512],
                                     in_=s.x2_sc[tc * 128:(tc + 1) * 128,
                                                 ocg * 512:(ocg + 1) * 512]).then_inc(x2ld, 16)
                    if n >= 1:
                        sy.wait_ge(mmd, n)
                    srcq = s.hq_sc[:, tcb * 512:(tcb + 1) * 512].rearrange(
                        "(dc p) o -> p dc o", p=128)
                    for part in range(8):
                        sy.dma_start(out=hqcol[:, part * 8 * 512:(part + 1) * 8 * 512],
                                     in_=srcq[:, part * 8:(part + 1) * 8, :]).then_inc(hqld, 16)
            sy.wait_ge(yout, 16 * 64)

        @block.tensor
        def _(t):
            for ocg in range(4):
                t.wait_ge(t2ld, 16 * 8 * (ocg + 1))
                for tcb in range(4):
                    n = ocg * 4 + tcb
                    pset = 4 * (tcb % 2)
                    if n >= 2:
                        t.wait_ge(yev, 4 * (n - 1))
                    t.wait_ge(hqld, 16 * 8 * (n + 1))
                    for dh in range(NDH):
                        for tci in range(4):
                            mmb = t.matmul(s.pb[pset + tci],
                                           hqcol[:, dh * 512 + tci * 128:
                                                 dh * 512 + (tci + 1) * 128],
                                           T2[:, dh * 512:(dh + 1) * 512],
                                           start=(dh == 0), stop=(dh == NDH - 1))
                    mmb.then_inc(mmd, 1)

        @block.vector
        def _(v):
            for ocg in range(4):
                for tcb in range(4):
                    n = ocg * 4 + tcb
                    v.wait_ge(mmd, n + 1)
                    v.wait_ge(x2ld, 16 * 4 * (n + 1))
                    if n >= 2:
                        v.wait_ge(yout, 16 * 4 * (n - 1))
                    bb = n % 2
                    pset = 4 * (tcb % 2)
                    for tci in range(4):
                        tc = tcb * 4 + tci
                        v.scalar_tensor_tensor(y_st[bb][:, tci * 512:(tci + 1) * 512],
                                               s.pb[pset + tci], s.c2_cols[:, tc:tc + 1],
                                               x2s[bb][:, tci * 512:(tci + 1) * 512],
                                               ALU.mult, ALU.add).then_inc(yev, 1)

        @block.gpsimd
        def _(g):
            for ocg in range(4):
                for tcb in range(4):
                    n = ocg * 4 + tcb
                    bb = n % 2
                    g.wait_ge(yev, 4 * (n + 1))
                    for tci in range(4):
                        tc = tcb * 4 + tci
                        g.dma_start(out=s.y[tc * 128:(tc + 1) * 128,
                                            ocg * 512:(ocg + 1) * 512],
                                    in_=y_st[bb][:, tci * 512:(tci + 1) * 512]).then_inc(yout, 16)
            g.wait_ge(yout, 16 * 64)
    ctx.close()


# version 03
# speedup vs baseline: 1.0370x; 1.0370x over previous
import contextlib
import numpy as np
import ml_dtypes
import concourse.bass as bass
import concourse.mybir as mybir
from concourse.bass_utils import run_bass_kernel_spmd

F32 = mybir.dt.float32
BF16 = mybir.dt.bfloat16
I32 = mybir.dt.int32
AF = mybir.ActivationFunctionType
ALU = mybir.AluOpType
AX = mybir.AxisListType

D = 2048
L = 4096
T = 2048
NH = 16
NKV = 8
HD = 128
DH = 8192
SCALE = float(HD) ** -0.5
EPS_LN = 1e-5
EPS_RMS = 1e-6
EPS_Q = 1e-5
NEG = -1.0e30

DEBUG = False
STOP_AFTER = None
RACE_DETECT = False
PP_LIMIT = None
PP_NO_PRE = False
PP_ORDER = ("k", "v", "qa", "qb")


def _bf(x):
    return np.asarray(x).astype(ml_dtypes.bfloat16)


def make_mask(h):
    m = np.zeros((8, 128, 1024), dtype=np.float32)
    if h == 1:
        for c in range(8):
            kk = c * 128 + np.arange(128)[:, None]
            qq = np.arange(1024)[None, :]
            m[c] = np.where(kk <= qq, NEG, 0.0)
    return _bf(m)


def kernel(L0_attn_hidden_states, L0_attn_Wq, L0_attn_Wk, L0_attn_Wv, L0_attn_Wo,
           L0_attn_q_norm_w, L0_attn_k_norm_w, L0_ffn_w1, L0_ffn_w2):
    x = np.ascontiguousarray(np.asarray(L0_attn_hidden_states, dtype=np.float32))
    wqT = np.ascontiguousarray(np.asarray(L0_attn_Wq, dtype=np.float32).T)
    wkT = np.ascontiguousarray(np.asarray(L0_attn_Wk, dtype=np.float32).T)
    wvT = np.ascontiguousarray(np.asarray(L0_attn_Wv, dtype=np.float32).T)
    woT = np.ascontiguousarray(np.asarray(L0_attn_Wo, dtype=np.float32).T)
    w1T = np.ascontiguousarray(np.asarray(L0_ffn_w1, dtype=np.float32).T)
    w2T = np.ascontiguousarray(np.asarray(L0_ffn_w2, dtype=np.float32).T)
    qnw = np.ascontiguousarray(np.asarray(L0_attn_q_norm_w, dtype=np.float32).reshape(1, HD))
    knw = np.ascontiguousarray(np.asarray(L0_attn_k_norm_w, dtype=np.float32).reshape(1, HD))
    ident_f32 = np.eye(128, dtype=np.float32)
    ident_bf = _bf(np.eye(128))
    masks = [make_mask(0), make_mask(1)]

    nc = build_nc()
    in_maps = []
    for c in range(8):
        b, h = c // 2, c % 2
        own = np.ascontiguousarray(x[b, h * T:(h + 1) * T, :])
        oth = np.ascontiguousarray(x[b, (1 - h) * T:(2 - h) * T, :])
        in_maps.append({
            "x_own": own, "x_oth": oth,
            "wqT": wqT, "wkT": wkT, "wvT": wvT, "woT": woT,
            "w1T": w1T, "w2T": w2T,
            "qnw_row": qnw, "knw_row": knw,
            "ident_f32": ident_f32, "ident_bf": ident_bf,
            "mask": masks[h],
        })
    res = run_bass_kernel_spmd(nc, in_maps, core_ids=list(range(8)))
    out = np.empty((4, L, D), dtype=np.float32)
    for c in range(8):
        b, h = c // 2, c % 2
        out[b, h * T:(h + 1) * T, :] = res.results[c]["y"]
    return out


class NS:
    pass


def build_nc():
    nc = bass.Bass(trn_type="TRN2", target_bir_lowering=False,
                   detect_race_conditions=RACE_DETECT)
    s = NS()
    s.nc = nc
    skind = "ExternalOutput" if DEBUG else "Internal"

    s.x_own = nc.dram_tensor("x_own", [T, D], F32, kind="ExternalInput").ap()
    s.x_oth = nc.dram_tensor("x_oth", [T, D], F32, kind="ExternalInput").ap()
    s.wqT = nc.dram_tensor("wqT", [D, D], F32, kind="ExternalInput").ap()
    s.wkT = nc.dram_tensor("wkT", [D, NKV * HD], F32, kind="ExternalInput").ap()
    s.wvT = nc.dram_tensor("wvT", [D, NKV * HD], F32, kind="ExternalInput").ap()
    s.woT = nc.dram_tensor("woT", [D, D], F32, kind="ExternalInput").ap()
    s.w1T = nc.dram_tensor("w1T", [D, DH], F32, kind="ExternalInput").ap()
    s.w2T = nc.dram_tensor("w2T", [DH, D], F32, kind="ExternalInput").ap()
    s.qnw_row_d = nc.dram_tensor("qnw_row", [1, HD], F32, kind="ExternalInput").ap()
    s.knw_row_d = nc.dram_tensor("knw_row", [1, HD], F32, kind="ExternalInput").ap()
    s.ident_f32_d = nc.dram_tensor("ident_f32", [128, 128], F32, kind="ExternalInput").ap()
    s.ident_bf_d = nc.dram_tensor("ident_bf", [128, 128], BF16, kind="ExternalInput").ap()
    s.mask_d = nc.dram_tensor("mask", [8, 128, 1024], BF16, kind="ExternalInput").ap()

    s.y = nc.dram_tensor("y", [T, D], F32, kind="ExternalOutput").ap()

    def scr(name, shape, dtype):
        return nc.dram_tensor(name, shape, dtype, kind=skind).ap()
    s.lnT_hi = scr("lnT_hi", [16, 128, L], BF16)
    s.lnT_lo = scr("lnT_lo", [16, 128, L], BF16)
    s.kn_hi = scr("kn_hi", [NKV, 32, 128, 128], BF16)
    s.kn_lo = scr("kn_lo", [NKV, 32, 128, 128], BF16)
    s.vt_hi = scr("vt_hi", [L, NKV * HD], BF16)
    s.vt_lo = scr("vt_lo", [L, NKV * HD], BF16)
    s.qn_hi = scr("qn_hi", [NH, 16, 128, 128], BF16)
    s.qn_lo = scr("qn_lo", [NH, 16, 128, 128], BF16)
    s.wo_hi = scr("wo_hi", [D, D], BF16)
    s.wo_lo = scr("wo_lo", [D, D], BF16)
    s.y_sc = scr("y_sc", [T, D], F32)
    s.x2_sc = scr("x2_sc", [T, D], F32)
    s.xq_sc = scr("xq_sc", [16, 128, T], BF16)
    s.g_sc = scr("g_sc", [DH, T], F32)
    s.hq_sc = scr("hq_sc", [DH, T], BF16)
    s.t2_sc = scr("t2_sc", [DH, D], BF16)
    s.dbg_m = scr("dbg_m", [1, 4], F32)

    def sb(name, shape, dtype):
        return nc.alloc_sbuf_tensor(name, shape, dtype).ap()
    s.ident_f32 = sb("c_ident32", [128, 128], F32)
    s.ident_bf = sb("c_identbf", [128, 128], BF16)
    s.qnw_row = sb("c_qnw", [1, HD], F32)
    s.knw_row = sb("c_knw", [1, HD], F32)
    s.qnw_b = sb("c_qnwb", [128, HD], F32)
    s.knw_b = sb("c_knwb", [128, HD], F32)
    s.ones_bf = sb("c_onesbf", [128, 1], BF16)
    s.ones_col = sb("c_onescol", [128, 1], F32)
    s.ones_row = sb("c_onesrow", [1, 128], F32)
    s.halfc = sb("c_halfc", [128, 1], F32)
    s.nhalfc = sb("c_nhalfc", [128, 1], F32)
    s.m1c = sb("c_m1c", [1, 1], F32)
    s.m2c = sb("c_m2c", [1, 1], F32)
    s.s1col = sb("c_s1col", [128, 1], F32)
    s.s2col = sb("c_s2col", [128, 1], F32)
    s.m2col = sb("c_m2col", [128, 1], F32)
    s.mid_ctx = contextlib.ExitStack()
    s.m_acc1 = s.mid_ctx.enter_context(nc.sbuf_tensor("c_macc1", [128, 1], F32)).ap()
    s.m_acc2 = s.mid_ctx.enter_context(nc.sbuf_tensor("c_macc2", [128, 1], F32)).ap()
    s.m_tmpc = s.mid_ctx.enter_context(nc.sbuf_tensor("c_mtmpc", [128, 1], F32)).ap()
    s.pre_buf = s.mid_ctx.enter_context(nc.sbuf_tensor("c_prebuf", [128, 4096], F32)).ap()
    s.dbg_st = s.mid_ctx.enter_context(nc.sbuf_tensor("c_dbgst", [1, 4], F32)).ap()

    s.pb = [nc.alloc_psum_tensor(f"pb{i}", [128, 512], F32).ap() for i in range(8)]

    s.sems = {}
    def sem(name):
        h = nc.ctx.enter_context(nc.semaphore(name))
        s.sems[name] = h
        return h
    s.sem = sem

    s.pre_gld = sem("pre_gld")
    s.pre_red = sem("pre_red")
    s.pre_done = sem("pre_done")
    s.pre_plan = []
    for i in range(D // 128):
        for j in range(DH // 2048):
            s.pre_plan.append((s.w1T, i * 128, j * 2048, s.m_acc1))
    for i in range(DH // 128):
        s.pre_plan.append((s.w2T, i * 128, 0, s.m_acc2))
    s.pre_n = 0
    s.pre_nv = 0

    emit_init(s)
    if STOP_AFTER == "init":
        return nc
    emit_p1_ln(s)
    if STOP_AFTER == "ln":
        return nc
    for kind in PP_ORDER:
        emit_proj_pass(s, kind)
        if STOP_AFTER == kind:
            return nc
    emit_wo_split(s)
    emit_m_finalize(s)
    if STOP_AFTER == "p1":
        return nc
    s.mid_ctx.close()
    s.s2_b = sb("c_s2b", [128, T], F32)
    s.c2_cols = sb("c_c2cols", [128, 16], F32)
    s.am2t = sb("c_am2t", [128, 16], F32)
    late_ctx = contextlib.ExitStack()
    s.amax1_row = late_ctx.enter_context(nc.sbuf_tensor("c_am1row", [1, T], F32)).ap()
    s.amax2_row = late_ctx.enter_context(nc.sbuf_tensor("c_am2row", [1, T], F32)).ap()
    s.c1_b = late_ctx.enter_context(nc.sbuf_tensor("c_c1b", [128, T], F32)).ap()
    emit_p2(s)
    if STOP_AFTER == "p2":
        return nc
    emit_p3(s)
    emit_c_rows(s)
    if STOP_AFTER == "p3":
        return nc
    emit_ffn_a(s)
    if STOP_AFTER == "ffa":
        late_ctx.close()
        return nc
    late_ctx.close()
    emit_ffn_b(s)
    return nc


def feed_prepass(s, g, nchunks):
    for _ in range(nchunks):
        if s.pre_n >= len(s.pre_plan):
            return
        w, r0, c0, acc = s.pre_plan[s.pre_n]
        n = s.pre_n
        if n >= 2:
            g.wait_ge(s.pre_red, n - 1)
        buf = s.pre_buf[:, (n % 2) * 2048:(n % 2 + 1) * 2048]
        g.dma_start(out=buf, in_=w[r0:r0 + 128, c0:c0 + 2048]).then_inc(s.pre_gld, 16)
        s.pre_n += 1


def feed_prepass_v(s, v, nchunks):
    for _ in range(nchunks):
        if s.pre_nv >= len(s.pre_plan):
            return
        w, r0, c0, acc = s.pre_plan[s.pre_nv]
        n = s.pre_nv
        v.wait_ge(s.pre_gld, 16 * (n + 1))
        buf = s.pre_buf[:, (n % 2) * 2048:(n % 2 + 1) * 2048]
        v.tensor_reduce(s.m_tmpc, buf, AX.X, ALU.add, apply_absolute_value=True)
        v.drain()
        v.tensor_tensor(acc, acc, s.m_tmpc, ALU.add).then_inc(s.pre_red, 1)
        s.pre_nv += 1
        if s.pre_nv == len(s.pre_plan):
            v.drain().then_inc(s.pre_done, 1)


def emit_init(s):
    nc = s.nc
    ld = s.sem("init_ld")
    ldv = s.sem("init_ldv")
    ldt = s.sem("init_ldt")
    with nc.Block() as block:
        @block.sync
        def _(sy):
            sy.dma_start(out=s.ident_f32, in_=s.ident_f32_d).then_inc(ld, 16)
            sy.dma_start(out=s.ident_bf, in_=s.ident_bf_d).then_inc(ld, 16)
            sy.dma_start(out=s.qnw_row, in_=s.qnw_row_d).then_inc(ld, 16)
            sy.dma_start(out=s.knw_row, in_=s.knw_row_d).then_inc(ld, 16)
            sy.wait_ge(ld, 16 * 4)
        @block.vector
        def _(v):
            v.memset(s.ones_bf, 1.0)
            v.memset(s.ones_col, 1.0)
            v.memset(s.ones_row, 1.0)
            v.memset(s.halfc, 0.5)
            v.memset(s.nhalfc, -0.5)
            v.memset(s.m_acc1, 0.0)
            v.memset(s.m_acc2, 0.0)
            v.memset(s.m_tmpc, 0.0)
            v.drain().then_inc(ldv, 1)
    with nc.Block() as block:
        @block.tensor
        def _(t):
            t.wait_ge(ld, 16 * 4)
            t.wait_ge(ldv, 1)
            t.matmul(s.pb[0][:, 0:HD], s.ones_row, s.qnw_row, start=True, stop=True)
            t.matmul(s.pb[1][:, 0:HD], s.ones_row, s.knw_row, start=True, stop=True).then_inc(ldt, 1)
        @block.vector
        def _(v):
            v.wait_ge(ldt, 1)
            v.tensor_copy(s.qnw_b, s.pb[0][:, 0:HD])
            v.tensor_copy(s.knw_b, s.pb[1][:, 0:HD])


def emit_p1_ln(s):
    nc = s.nc
    xld = s.sem("l_xld")
    lnr = s.sem("l_lnr")
    rs1 = s.sem("l_rs1")
    rs2 = s.sem("l_rs2")
    trd = s.sem("l_trd")
    evd = s.sem("l_evd")
    spd = s.sem("l_spd")
    ctx = contextlib.ExitStack()
    sbuf = nc.sbuf_tensor
    x_st = [ctx.enter_context(sbuf(f"l_x{i}", [128, D], F32)).ap() for i in range(2)]
    ln_st = [ctx.enter_context(sbuf(f"l_ln{i}", [128, D], F32)).ap() for i in range(2)]
    hi_st = [ctx.enter_context(sbuf(f"l_hi{i}", [128, D], BF16)).ap() for i in range(2)]
    lo_st = [ctx.enter_context(sbuf(f"l_lo{i}", [128, D], BF16)).ap() for i in range(2)]
    xsq = ctx.enter_context(sbuf("l_xsq", [128, D], F32)).ap()
    bn2 = ctx.enter_context(sbuf("l_bn2", [128, 2], F32)).ap()
    q2c = ctx.enter_context(sbuf("l_q2c", [128, 1], F32)).ap()
    veps = ctx.enter_context(sbuf("l_veps", [128, 1], F32)).ap()
    rvar = ctx.enter_context(sbuf("l_rvar", [128, 1], F32)).ap()
    rstd = ctx.enter_context(sbuf("l_rstd", [128, 1], F32)).ap()
    NTC = 32

    with nc.Block() as block:
        @block.sync
        def _(sy):
            for tc in range(NTC):
                src = s.x_own if tc < 16 else s.x_oth
                r = (tc % 16) * 128
                if tc >= 2:
                    sy.wait_ge(lnr, tc - 1)
                sy.dma_start(out=x_st[tc % 2], in_=src[r:r + 128, :]).then_inc(xld, 16)
            sy.wait_ge(spd, 16 * 32 * NTC)

        @block.vector
        def _(v):
            for tc in range(NTC):
                b = tc % 2
                v.wait_ge(xld, 16 * (tc + 1))
                v.tensor_reduce(bn2[:, 0:1], x_st[b], AX.X, ALU.add)
                v.tensor_tensor(xsq, x_st[b], x_st[b], ALU.mult)
                v.tensor_reduce(q2c, xsq, AX.X, ALU.add)
                v.drain()
                v.tensor_scalar(bn2[:, 0:1], bn2[:, 0:1], 1.0 / D, None, ALU.mult)
                v.drain()
                v.tensor_tensor(veps, bn2[:, 0:1], bn2[:, 0:1], ALU.mult)
                v.drain()
                v.scalar_tensor_tensor(veps, q2c, 1.0 / D, veps, ALU.mult, ALU.subtract)
                v.drain()
                v.tensor_scalar(veps, veps, EPS_LN, None, ALU.add)
                v.drain()
                v.reciprocal(rvar, veps).then_inc(rs1, 1)
                if tc >= 2:
                    v.wait_ge(trd, 16 * (tc - 1))
                    v.wait_ge(spd, 16 * 32 * (tc - 1))
                v.wait_ge(rs2, tc + 1)
                v.tensor_scalar(ln_st[b], x_st[b], bn2[:, 0:1], None, ALU.subtract)
                v.tensor_scalar(ln_st[b], ln_st[b], rstd, None, ALU.mult).then_inc(lnr, 1)
                for fc in range(16):
                    pbt = s.pb[6 + (fc % 2)]
                    v.wait_ge(trd, tc * 16 + fc + 1)
                    v.tensor_copy(hi_st[b][:, fc * 128:(fc + 1) * 128], pbt[:, 0:128])
                    v.tensor_tensor(lo_st[b][:, fc * 128:(fc + 1) * 128], pbt[:, 0:128],
                                    hi_st[b][:, fc * 128:(fc + 1) * 128],
                                    ALU.subtract).then_inc(evd, 1)
                if tc % 8 == 7:
                    feed_prepass_v(s, v, 4)

        @block.scalar
        def _(sc):
            def spill(tc):
                bb = tc % 2
                sc.wait_ge(evd, 16 * (tc + 1))
                for fc in range(16):
                    sc.dma_start(out=s.lnT_hi[fc, :, tc * 128:(tc + 1) * 128],
                                 in_=hi_st[bb][:, fc * 128:(fc + 1) * 128]).then_inc(spd, 16)
                    sc.dma_start(out=s.lnT_lo[fc, :, tc * 128:(tc + 1) * 128],
                                 in_=lo_st[bb][:, fc * 128:(fc + 1) * 128]).then_inc(spd, 16)
            for tc in range(NTC):
                sc.wait_ge(rs1, tc + 1)
                sc.activation(rstd, rvar, AF.Sqrt).then_inc(rs2, 1)
                if tc >= 1:
                    spill(tc - 1)
            spill(NTC - 1)

        @block.tensor
        def _(t):
            for tc in range(NTC):
                t.wait_ge(lnr, tc + 1)
                for fc in range(16):
                    pbt = s.pb[6 + (fc % 2)]
                    if tc * 16 + fc >= 2:
                        t.wait_ge(evd, tc * 16 + fc - 1)
                    t.transpose(pbt[:, 0:128], ln_st[tc % 2][:, fc * 128:(fc + 1) * 128],
                                s.ident_f32).then_inc(trd, 1)

        @block.gpsimd
        def _(g):
            feed_prepass(s, g, 16)
    ctx.close()


def emit_proj_pass(s, kind):
    nc = s.nc
    is_q = kind in ("qa", "qb")
    is_v = kind == "v"
    NTC = 16 if is_q else 32
    if PP_LIMIT is not None:
        NTC = min(NTC, PP_LIMIT)
        NTC = max(4, NTC)
    w_dram = {"k": s.wkT, "v": s.wvT,
              "qa": s.wqT[:, 0:1024], "qb": s.wqT[:, 1024:2048]}[kind]
    head_off = 8 if kind == "qb" else 0
    nrm_b = s.qnw_b if is_q else s.knw_b
    dst_hi = s.qn_hi if is_q else s.kn_hi
    dst_lo = s.qn_lo if is_q else s.kn_lo

    names = ["wld", "wsp", "lnld", "pdone", "pfree", "rs1", "rs2",
             "knr", "trd", "ted", "ksr", "spd"]
    if not hasattr(s, "pp_sems"):
        s.pp_sems = {n: s.sem("pp_" + n) for n in names}
        s.pp_base = {n: 0 for n in names}
    B = dict(s.pp_base)
    SM = s.pp_sems
    wld, wsp, lnld, pdone, pfree, rs1, rs2, knr, trd, ted, ksr, spd = (
        SM[n] for n in names)

    p = kind + "_"
    ctx = contextlib.ExitStack()
    sbuf = nc.sbuf_tensor
    w_hi = ctx.enter_context(sbuf(p + "whi", [128, 16 * 1024], BF16)).ap()
    w_lo = ctx.enter_context(sbuf(p + "wlo", [128, 16 * 1024], BF16)).ap()
    wtmp = [ctx.enter_context(sbuf(p + f"wtmp{i}", [128, 1024], F32)).ap() for i in range(2)]
    lnb_hi = [ctx.enter_context(sbuf(p + f"lnhi{i}", [128, 16 * 512], BF16)).ap() for i in range(2)]
    lnb_lo = [ctx.enter_context(sbuf(p + f"lnlo{i}", [128, 16 * 512], BF16)).ap() for i in range(2)]
    kf = ctx.enter_context(sbuf(p + "kf", [128, 1024], F32)).ap()
    if is_v:
        vsp_hi = [ctx.enter_context(sbuf(p + f"vsphi{i}", [128, 1024], BF16)).ap() for i in range(2)]
        vsp_lo = [ctx.enter_context(sbuf(p + f"vsplo{i}", [128, 1024], BF16)).ap() for i in range(2)]
    else:
        ksq = ctx.enter_context(sbuf(p + "ksq", [128, 1024], F32)).ap()
        red8 = ctx.enter_context(sbuf(p + "red8", [128, 8], F32)).ap()
        rv8 = ctx.enter_context(sbuf(p + "rv8", [128, 8], F32)).ap()
        rstd8 = ctx.enter_context(sbuf(p + "rstd8", [128, 8], F32)).ap()
        knf8 = ctx.enter_context(sbuf(p + "knf8", [128, 1024], F32)).ap()
        sp_st = [ctx.enter_context(sbuf(p + f"spst{i}", [128, 16 * 128], BF16)).ap() for i in range(2)]

    NTG = NTC // 4
    pbt = [s.pb[4], s.pb[5]]

    with nc.Block() as block:
        @block.sync
        def _(sy):
            for fc in range(16):
                if fc >= 2:
                    sy.wait_ge(wsp, B["wsp"] + fc - 1)
                sy.dma_start(out=wtmp[fc % 2],
                             in_=w_dram[fc * 128:(fc + 1) * 128, :]).then_inc(wld, 16)
            for tg in range(NTG):
                if tg >= 2:
                    sy.wait_ge(pdone, B["pdone"] + 4 * (tg - 1))
                ts = tg * 512
                sy.dma_start(out=lnb_hi[tg % 2],
                             in_=s.lnT_hi[:, :, ts:ts + 512].rearrange("fc f t -> f fc t")
                             ).then_inc(lnld, 16)
                sy.dma_start(out=lnb_lo[tg % 2],
                             in_=s.lnT_lo[:, :, ts:ts + 512].rearrange("fc f t -> f fc t")
                             ).then_inc(lnld, 16)
            sy.wait_ge(spd, B["spd"] + 16 * (2 * NTC if is_v else 16 * NTC))

        @block.vector
        def _(v):
            for fc in range(16):
                v.wait_ge(wld, B["wld"] + 16 * (fc + 1))
                v.tensor_copy(w_hi[:, fc * 1024:(fc + 1) * 1024], wtmp[fc % 2])
                v.tensor_tensor(w_lo[:, fc * 1024:(fc + 1) * 1024], wtmp[fc % 2],
                                w_hi[:, fc * 1024:(fc + 1) * 1024], ALU.subtract).then_inc(wsp, 1)
            for tc in range(NTC):
                v.wait_ge(pdone, B["pdone"] + tc + 1)
                pa = s.pb[2 * (tc % 2)]
                pa1 = s.pb[2 * (tc % 2) + 1]
                v.tensor_copy(kf[:, 0:512], pa)
                v.tensor_copy(kf[:, 512:1024], pa1).then_inc(pfree, 1)
                bb = tc % 2
                if is_v:
                    if tc >= 2:
                        v.wait_ge(spd, B["spd"] + 16 * 2 * (tc - 1))
                    v.tensor_copy(vsp_hi[bb], kf)
                    v.tensor_tensor(vsp_lo[bb], kf, vsp_hi[bb], ALU.subtract).then_inc(ksr, 1)
                    if tc % 8 == 7:
                        feed_prepass_v(s, v, 5)
                    continue
                v.tensor_tensor(ksq, kf, kf, ALU.mult)
                for g8 in range(8):
                    v.tensor_reduce(red8[:, g8:g8 + 1], ksq[:, g8 * 128:(g8 + 1) * 128],
                                    AX.X, ALU.add)
                v.drain()
                v.tensor_scalar(rv8, red8, 1.0 / HD, EPS_RMS, ALU.mult, ALU.add)
                v.drain()
                v.reciprocal(rv8, rv8).then_inc(rs1, 1)
                v.wait_ge(rs2, B["rs2"] + tc + 1)
                for g8 in range(8):
                    sl = slice(g8 * 128, (g8 + 1) * 128)
                    tt = v.scalar_tensor_tensor(knf8[:, sl], kf[:, sl], rstd8[:, g8:g8 + 1],
                                                nrm_b, ALU.mult, ALU.mult)
                tt.then_inc(knr, 1)
                if tc >= 2:
                    v.wait_ge(spd, B["spd"] + 16 * 16 * (tc - 1))
                for g8 in range(8):
                    v.wait_ge(trd, B["trd"] + tc * 8 + g8 + 1)
                    pbx = pbt[g8 % 2][:, 0:128]
                    hi_sl = sp_st[bb][:, (2 * g8) * 128:(2 * g8 + 1) * 128]
                    v.tensor_copy(hi_sl, pbx).then_inc(ted, 1)
                    v.tensor_tensor(sp_st[bb][:, (2 * g8 + 1) * 128:(2 * g8 + 2) * 128],
                                    pbx, hi_sl, ALU.subtract).then_inc(ted, 1)
                if tc % 8 == 7:
                    feed_prepass_v(s, v, 5)

        @block.scalar
        def _(sc):
            def spill(tc):
                bb = tc % 2
                if is_v:
                    sc.wait_ge(ksr, B["ksr"] + tc + 1)
                    sc.dma_start(out=s.vt_hi[tc * 128:(tc + 1) * 128, :],
                                 in_=vsp_hi[bb]).then_inc(spd, 16)
                    sc.dma_start(out=s.vt_lo[tc * 128:(tc + 1) * 128, :],
                                 in_=vsp_lo[bb]).then_inc(spd, 16)
                else:
                    sc.wait_ge(ted, B["ted"] + (tc + 1) * 16)
                    for g8 in range(8):
                        h = head_off + g8
                        sc.dma_start(out=dst_hi[h, tc, :, :],
                                     in_=sp_st[bb][:, (2 * g8) * 128:(2 * g8 + 1) * 128]
                                     ).then_inc(spd, 16)
                        sc.dma_start(out=dst_lo[h, tc, :, :],
                                     in_=sp_st[bb][:, (2 * g8 + 1) * 128:(2 * g8 + 2) * 128]
                                     ).then_inc(spd, 16)
            for tc in range(NTC):
                if not is_v:
                    sc.wait_ge(rs1, B["rs1"] + tc + 1)
                    sc.activation(rstd8, rv8, AF.Sqrt).then_inc(rs2, 1)
                if tc >= 1:
                    spill(tc - 1)
            spill(NTC - 1)

        @block.tensor
        def _(t):
            def emit_tr(tc):
                t.wait_ge(knr, B["knr"] + tc + 1)
                for g8 in range(8):
                    w = tc * 16 + 2 * g8
                    if w >= 2:
                        t.wait_ge(ted, B["ted"] + w - 2)
                    t.transpose(pbt[g8 % 2][:, 0:128],
                                knf8[:, g8 * 128:(g8 + 1) * 128],
                                s.ident_f32).then_inc(trd, 1)
            t.wait_ge(wsp, B["wsp"] + 16)
            for tc in range(NTC):
                tg, tcq = tc // 4, tc % 4
                t.wait_ge(lnld, B["lnld"] + 16 * 2 * (tg + 1))
                if tc >= 2:
                    t.wait_ge(pfree, B["pfree"] + tc - 1)
                pa = s.pb[2 * (tc % 2)]
                pa1 = s.pb[2 * (tc % 2) + 1]
                lh = lnb_hi[tg % 2]
                ll = lnb_lo[tg % 2]
                for fc in range(16):
                    sl = slice(fc * 512 + tcq * 128, fc * 512 + (tcq + 1) * 128)
                    st_hi = lh[:, sl]
                    st_lo = ll[:, sl]
                    first, last = fc == 0, fc == 15
                    t.matmul(pa, st_hi, w_hi[:, fc * 1024:fc * 1024 + 512],
                             start=first, stop=False)
                    t.matmul(pa1, st_hi, w_hi[:, fc * 1024 + 512:(fc + 1) * 1024],
                             start=first, stop=False)
                    t.matmul(pa, st_hi, w_lo[:, fc * 1024:fc * 1024 + 512],
                             start=False, stop=False)
                    t.matmul(pa1, st_hi, w_lo[:, fc * 1024 + 512:(fc + 1) * 1024],
                             start=False, stop=False)
                    t.matmul(pa, st_lo, w_hi[:, fc * 1024:fc * 1024 + 512],
                             start=False, stop=last)
                    mm = t.matmul(pa1, st_lo, w_hi[:, fc * 1024 + 512:(fc + 1) * 1024],
                                  start=False, stop=last)
                    if last:
                        mm.then_inc(pdone, 1)
                if not is_v and tc >= 1:
                    emit_tr(tc - 1)
            if not is_v:
                emit_tr(NTC - 1)

        @block.gpsimd
        def _(g):
            if not PP_NO_PRE:
                feed_prepass(s, g, 20 if not is_q else 10)
    ctx.close()
    s.pp_base["wld"] += 16 * 16
    s.pp_base["wsp"] += 16
    s.pp_base["lnld"] += 16 * 2 * NTG
    s.pp_base["pdone"] += NTC
    s.pp_base["pfree"] += NTC
    if is_v:
        s.pp_base["ksr"] += NTC
        s.pp_base["spd"] += 16 * 2 * NTC
    else:
        s.pp_base["rs1"] += NTC
        s.pp_base["rs2"] += NTC
        s.pp_base["knr"] += NTC
        s.pp_base["trd"] += 8 * NTC
        s.pp_base["ted"] += 16 * NTC
        s.pp_base["spd"] += 16 * 16 * NTC


def emit_wo_split(s):
    nc = s.nc
    wld = s.sem("wo_wld")
    wsp = s.sem("wo_wsp")
    spd = s.sem("wo_spd")
    ctx = contextlib.ExitStack()
    wtmp = [ctx.enter_context(nc.sbuf_tensor(f"wo_t{i}", [128, D], F32)).ap() for i in range(2)]
    whi = [ctx.enter_context(nc.sbuf_tensor(f"wo_h{i}", [128, D], BF16)).ap() for i in range(2)]
    wlo = [ctx.enter_context(nc.sbuf_tensor(f"wo_l{i}", [128, D], BF16)).ap() for i in range(2)]
    with nc.Block() as block:
        @block.sync
        def _(sy):
            for i in range(16):
                if i >= 2:
                    sy.wait_ge(wsp, i - 1)
                sy.dma_start(out=wtmp[i % 2], in_=s.woT[i * 128:(i + 1) * 128, :]).then_inc(wld, 16)
            sy.wait_ge(spd, 16 * 32)
        @block.vector
        def _(v):
            for i in range(16):
                b = i % 2
                v.wait_ge(wld, 16 * (i + 1))
                if i >= 2:
                    v.wait_ge(spd, 16 * 2 * (i - 1))
                v.tensor_copy(whi[b], wtmp[b])
                v.tensor_tensor(wlo[b], wtmp[b], whi[b], ALU.subtract).then_inc(wsp, 1)
            feed_prepass_v(s, v, len(s.pre_plan) - s.pre_nv)
        @block.scalar
        def _(sc):
            for i in range(16):
                b = i % 2
                sc.wait_ge(wsp, i + 1)
                sc.dma_start(out=s.wo_hi[i * 128:(i + 1) * 128, :], in_=whi[b]).then_inc(spd, 16)
                sc.dma_start(out=s.wo_lo[i * 128:(i + 1) * 128, :], in_=wlo[b]).then_inc(spd, 16)
        @block.gpsimd
        def _(g):
            feed_prepass(s, g, len(s.pre_plan) - s.pre_n)
    ctx.close()


def emit_m_finalize(s):
    nc = s.nc
    fin = s.sem("mfin")
    with nc.Block() as block:
        @block.tensor
        def _(t):
            t.wait_ge(s.pre_done, 1)
            t.matmul(s.pb[0][0:1, 0:1], s.ones_col, s.m_acc1, start=True, stop=True)
            t.matmul(s.pb[1][0:1, 0:1], s.ones_col, s.m_acc2,
                     start=True, stop=True).then_inc(fin, 1)
        @block.vector
        def _(v):
            v.wait_ge(fin, 1)
            v.tensor_scalar(s.m1c, s.pb[0][0:1, 0:1], 1.0 / (D * DH), EPS_Q, ALU.mult, ALU.max)
            v.tensor_scalar(s.m2c, s.pb[1][0:1, 0:1], 1.0 / (D * DH), EPS_Q, ALU.mult, ALU.max)
            v.drain().then_inc(fin, 1)
    with nc.Block() as block:
        @block.tensor
        def _(t):
            t.wait_ge(fin, 2)
            t.matmul(s.pb[0][:, 0:1], s.ones_row, s.m2c, start=True, stop=True)
            t.matmul(s.pb[1][:, 0:1], s.ones_row, s.m1c, start=True, stop=True).then_inc(fin, 1)
        @block.vector
        def _(v):
            v.wait_ge(fin, 3)
            v.tensor_copy(s.m2col, s.pb[0][:, 0:1])
            v.reciprocal(s.s1col, s.pb[1][:, 0:1])
            v.reciprocal(s.s2col, s.pb[0][:, 0:1])
            if DEBUG:
                v.drain()
                v.tensor_copy(s.dbg_st[0:1, 0:1], s.m1c)
                v.tensor_copy(s.dbg_st[0:1, 1:2], s.m2c)
                v.tensor_copy(s.dbg_st[0:1, 2:3], s.s1col[0:1, 0:1])
                v.drain().then_inc(fin, 1)
        @block.scalar
        def _(sc):
            if DEBUG:
                sc.wait_ge(fin, 4)
                sc.dma_start(out=s.dbg_m, in_=s.dbg_st[0:1, 0:4]).then_inc(fin, 16)
                sc.wait_ge(fin, 20)


def emit_p2(s):
    nc = s.nc

    kvld = s.sem("a_kvld")
    qld = s.sem("a_qld")
    ssem = s.sem("a_ssem")
    esem = s.sem("a_esem")
    usem = s.sem("a_usem")
    csem = s.sem("a_csem")
    tosem = s.sem("a_tosem")
    recsem = s.sem("a_recsem")
    recbsem = s.sem("a_recbsem")
    recbfree = s.sem("a_recbfree")
    aosem = s.sem("a_aosem")
    wold = s.sem("a_wold")
    ydone = s.sem("a_ydone")
    yev = s.sem("a_yev")
    ytr = s.sem("a_ytr")
    yevp = s.sem("a_yevp")
    ysp = s.sem("a_ysp")

    ctx = contextlib.ExitStack()
    sbuf = nc.sbuf_tensor
    K_hi = [ctx.enter_context(sbuf(f"a_khi{i}", [128, L], BF16)).ap() for i in range(2)]
    K_lo = [ctx.enter_context(sbuf(f"a_klo{i}", [128, L], BF16)).ap() for i in range(2)]
    V_hi = [ctx.enter_context(sbuf(f"a_vhi{i}", [128, L], BF16)).ap() for i in range(2)]
    V_lo = [ctx.enter_context(sbuf(f"a_vlo{i}", [128, L], BF16)).ap() for i in range(2)]
    Q_hi = [ctx.enter_context(sbuf(f"a_qhi{i}", [128, 512], BF16)).ap() for i in range(2)]
    Q_lo = [ctx.enter_context(sbuf(f"a_qlo{i}", [128, 512], BF16)).ap() for i in range(2)]
    U_f = [ctx.enter_context(sbuf(f"a_uf{i}", [128, 512], F32)).ap() for i in range(2)]
    U_hi = [ctx.enter_context(sbuf(f"a_uhi{i}", [128, 512], BF16)).ap() for i in range(2)]
    U_lo = [ctx.enter_context(sbuf(f"a_ulo{i}", [128, 512], BF16)).ap() for i in range(2)]
    mask_sb = ctx.enter_context(sbuf("a_mask", [128, 8 * 1024], BF16)).ap()
    rec_row = [ctx.enter_context(sbuf(f"a_rr{i}", [1, 512], F32)).ap() for i in range(2)]
    rec_b = ctx.enter_context(sbuf("a_recb", [128, 512], F32)).ap()
    ao_f = ctx.enter_context(sbuf("a_aof", [128, 512], F32)).ap()
    ao_hi = ctx.enter_context(sbuf("a_aohi", [128, NH * 512], BF16)).ap()
    ao_lo = ctx.enter_context(sbuf("a_aolo", [128, NH * 512], BF16)).ap()
    woc_hi = [ctx.enter_context(sbuf(f"a_wochi{i}", [128, D], BF16)).ap() for i in range(2)]
    woc_lo = [ctx.enter_context(sbuf(f"a_woclo{i}", [128, D], BF16)).ap() for i in range(2)]
    y_oc = ctx.enter_context(sbuf("a_yoc", [128, 512], F32)).ap()
    piece = [ctx.enter_context(sbuf(f"a_piece{i}", [128, 128], F32)).ap() for i in range(4)]

    psS = [s.pb[0], s.pb[1]]
    psO = s.pb[2]
    psSum = s.pb[3][0:1, :]
    psRB = s.pb[4]
    psY = [s.pb[4], s.pb[5]]
    psTR = [s.pb[6], s.pb[7]]

    with nc.Block() as block:
        @block.sync
        def _(sy):
            sy.dma_start(out=mask_sb, in_=s.mask_d.rearrange("c p q -> p c q")).then_inc(qld, 16)
            for qg in range(4):
                for g in range(8):
                    gi = qg * 8 + g
                    if gi >= 2:
                        gp = gi - 2
                        sy.wait_ge(recbsem, (gp // 8) * 16 + (gp % 8) * 2 + 2)
                    sy.dma_start(out=K_hi[g % 2],
                                 in_=s.kn_hi[g].rearrange("kc hd t -> hd kc t")).then_inc(kvld, 16)
                    sy.dma_start(out=K_lo[g % 2],
                                 in_=s.kn_lo[g].rearrange("kc hd t -> hd kc t")).then_inc(kvld, 16)
                    sy.dma_start(out=V_hi[g % 2],
                                 in_=s.vt_hi[:, g * 128:(g + 1) * 128].rearrange(
                                     "(kc p) hd -> p kc hd", p=128)).then_inc(kvld, 16)
                    sy.dma_start(out=V_lo[g % 2],
                                 in_=s.vt_lo[:, g * 128:(g + 1) * 128].rearrange(
                                     "(kc p) hd -> p kc hd", p=128)).then_inc(kvld, 16)
                    for hh in range(2):
                        U = qg * 16 + g * 2 + hh
                        h = g * 2 + hh
                        if U >= 2:
                            sy.wait_ge(tosem, U - 1)
                        sy.dma_start(out=Q_hi[U % 2],
                                     in_=s.qn_hi[h, qg * 4:(qg + 1) * 4].rearrange(
                                         "tc hd t -> hd tc t")).then_inc(qld, 16)
                        sy.dma_start(out=Q_lo[U % 2],
                                     in_=s.qn_lo[h, qg * 4:(qg + 1) * 4].rearrange(
                                         "tc hd t -> hd tc t")).then_inc(qld, 16)
                for oc in range(16):
                    oci = qg * 16 + oc
                    if oci >= 2:
                        sy.wait_ge(ydone, oci - 1)
                    sy.dma_start(out=woc_hi[oc % 2],
                                 in_=s.wo_hi[:, oc * 128:(oc + 1) * 128].rearrange(
                                     "(hc f) o -> f hc o", f=128)).then_inc(wold, 16)
                    sy.dma_start(out=woc_lo[oc % 2],
                                 in_=s.wo_lo[:, oc * 128:(oc + 1) * 128].rearrange(
                                     "(hc f) o -> f hc o", f=128)).then_inc(wold, 16)
            sy.wait_ge(ysp, 16 * 64 * 4)

        @block.tensor
        def _(t):
            def consume(U, kc, vh, vl):
                t.wait_ge(usem, U * 32 + kc + 1)
                b = kc % 2
                ks = slice(kc * 128, (kc + 1) * 128)
                t.matmul(psSum, s.ones_bf, U_hi[b], start=(kc == 0), stop=False)
                mmsum = t.matmul(psSum, s.ones_bf, U_lo[b], start=False, stop=(kc == 31))
                if kc == 31:
                    mmsum.then_inc(tosem, 1)
                t.matmul(psO, vh[:, ks], U_hi[b], start=(kc == 0), stop=False)
                t.matmul(psO, vh[:, ks], U_lo[b], start=False, stop=False)
                t.matmul(psO, vl[:, ks], U_hi[b],
                         start=False, stop=(kc == 31)).then_inc(csem, 1)

            def ytrans(qg, oc):
                t.wait_ge(yev, qg * 16 + oc + 1)
                for pc in range(4):
                    n = (qg * 16 + oc) * 4 + pc
                    if n >= 2:
                        t.wait_ge(yevp, n - 1)
                    t.transpose(psTR[pc % 2][:, 0:128],
                                y_oc[:, pc * 128:(pc + 1) * 128], s.ident_f32).then_inc(ytr, 1)

            for qg in range(4):
                for g in range(8):
                    gi = qg * 8 + g
                    t.wait_ge(kvld, 16 * 4 * (gi + 1))
                    for hh in range(2):
                        u = g * 2 + hh
                        U = qg * 16 + u
                        t.wait_ge(qld, 16 * (2 * (U + 1) + 1))
                        t.wait_ge(aosem, U)
                        t.wait_ge(recsem, U)
                        kh, kl = K_hi[g % 2], K_lo[g % 2]
                        vh, vl = V_hi[g % 2], V_lo[g % 2]
                        qh, ql = Q_hi[U % 2], Q_lo[U % 2]
                        for kc in range(32):
                            pS = psS[kc % 2]
                            t.wait_ge(esem, max(0, U * 32 + kc - 1))
                            ks = slice(kc * 128, (kc + 1) * 128)
                            masked = (8 <= kc < 16) and qg >= 2
                            t.matmul(pS, kh[:, ks], qh, start=True, stop=False)
                            t.matmul(pS, kh[:, ks], ql, start=False, stop=False)
                            mm3 = t.matmul(pS, kl[:, ks], qh, start=False, stop=not masked)
                            if masked:
                                t.matmul(pS, s.ident_bf,
                                         mask_sb[:, (kc - 8) * 1024 + (qg - 2) * 512:
                                                 (kc - 8) * 1024 + (qg - 2) * 512 + 512],
                                         start=False, stop=True).then_inc(ssem, 1)
                            else:
                                mm3.then_inc(ssem, 1)
                            if kc >= 1:
                                consume(U, kc - 1, vh, vl)
                        consume(U, 31, vh, vl)
                        t.wait_ge(recsem, U + 1)
                        if U >= 1:
                            t.wait_ge(recbfree, U)
                        if u == 0:
                            t.wait_ge(yev, U)
                        t.matmul(psRB, s.ones_row, rec_row[U % 2],
                                 start=True, stop=True).then_inc(recbsem, 1)
                t.wait_ge(aosem, qg * 16 + 16)
                for oc in range(16):
                    oci = qg * 16 + oc
                    t.wait_ge(wold, 16 * 2 * (oci + 1))
                    if oc >= 2:
                        t.wait_ge(yev, qg * 16 + oc - 1)
                    pY = psY[oc % 2]
                    wh, wl = woc_hi[oc % 2], woc_lo[oc % 2]
                    for h in range(NH):
                        ah = ao_hi[:, h * 512:(h + 1) * 512]
                        al = ao_lo[:, h * 512:(h + 1) * 512]
                        wt_h = wh[:, h * 128:(h + 1) * 128]
                        wt_l = wl[:, h * 128:(h + 1) * 128]
                        t.matmul(pY, wt_h, ah, start=(h == 0), stop=False)
                        t.matmul(pY, wt_h, al, start=False, stop=False)
                        mmy = t.matmul(pY, wt_l, ah, start=False, stop=(h == NH - 1))
                    mmy.then_inc(ydone, 1)
                    if oc >= 1:
                        ytrans(qg, oc - 1)
                ytrans(qg, 15)

        @block.scalar
        def _(sc):
            for U in range(64):
                for kc in range(32):
                    sc.wait_ge(ssem, U * 32 + kc + 1)
                    sc.wait_ge(usem, max(0, U * 32 + kc - 1))
                    sc.activation(U_f[kc % 2], psS[kc % 2], AF.Exp, scale=SCALE).then_inc(esem, 1)

        @block.vector
        def _(v):
            for qg in range(4):
                for u in range(16):
                    U = qg * 16 + u
                    if u == 0 and qg >= 1:
                        v.wait_ge(ydone, qg * 16)
                    for kc in range(32):
                        b = kc % 2
                        v.wait_ge(esem, U * 32 + kc + 1)
                        v.wait_ge(csem, max(0, U * 32 + kc - 1))
                        v.tensor_copy(U_hi[b], U_f[b])
                        v.tensor_tensor(U_lo[b], U_f[b], U_hi[b], ALU.subtract).then_inc(usem, 1)
                    v.wait_ge(tosem, U + 1)
                    v.reciprocal(rec_row[U % 2], psSum).then_inc(recsem, 1)
                    v.wait_ge(recbsem, U + 1)
                    v.tensor_copy(rec_b, psRB).then_inc(recbfree, 1)
                    v.wait_ge(csem, U * 32 + 32)
                    v.tensor_tensor(ao_f, psO, rec_b, ALU.mult)
                    v.tensor_copy(ao_hi[:, u * 512:(u + 1) * 512], ao_f)
                    v.tensor_tensor(ao_lo[:, u * 512:(u + 1) * 512], ao_f,
                                    ao_hi[:, u * 512:(u + 1) * 512], ALU.subtract).then_inc(aosem, 1)
                for oc in range(16):
                    v.wait_ge(ydone, qg * 16 + oc + 1)
                    v.tensor_copy(y_oc, psY[oc % 2]).then_inc(yev, 1)
                    for pc in range(4):
                        n = (qg * 16 + oc) * 4 + pc
                        v.wait_ge(ytr, n + 1)
                        if n >= 4:
                            v.wait_ge(ysp, 16 * (n - 3))
                        v.tensor_copy(piece[pc],
                                      psTR[pc % 2][:, 0:128]).then_inc(yevp, 1)

        @block.gpsimd
        def _(g):
            for qg in range(4):
                for oc in range(16):
                    for pc in range(4):
                        n = (qg * 16 + oc) * 4 + pc
                        g.wait_ge(yevp, n + 1)
                        tc = qg * 4 + pc
                        g.dma_start(out=s.y_sc[tc * 128:(tc + 1) * 128, oc * 128:(oc + 1) * 128],
                                    in_=piece[pc]).then_inc(ysp, 16)
            g.wait_ge(ysp, 16 * 64 * 4)
    ctx.close()


def emit_p3(s):
    nc = s.nc
    yld = s.sem("p3_yld")
    x2r = s.sem("p3_x2r")
    rs1 = s.sem("p3_rs1")
    rs2 = s.sem("p3_rs2")
    er = s.sem("p3_er")
    sg = s.sem("p3_sg")
    xqr = s.sem("p3_xqr")
    trd = s.sem("p3_trd")
    ted = s.sem("p3_ted")
    spd = s.sem("p3_spd")

    ctx = contextlib.ExitStack()
    sbuf = nc.sbuf_tensor
    y_st = [ctx.enter_context(sbuf(f"p3_y{i}", [128, D], F32)).ap() for i in range(2)]
    x_st = [ctx.enter_context(sbuf(f"p3_x{i}", [128, D], F32)).ap() for i in range(2)]
    x2_st = [ctx.enter_context(sbuf(f"p3_x2{i}", [128, D], F32)).ap() for i in range(2)]
    ln2 = ctx.enter_context(sbuf("p3_ln2", [128, D], F32)).ap()
    z_st = ctx.enter_context(sbuf("p3_z", [128, D], F32)).ap()
    sti = ctx.enter_context(sbuf("p3_sti", [128, D], I32)).ap()
    e_st = ctx.enter_context(sbuf("p3_e", [128, D], F32)).ap()
    sg1 = ctx.enter_context(sbuf("p3_sg1", [128, D], BF16)).ap()
    sg2 = ctx.enter_context(sbuf("p3_sg2", [128, D], BF16)).ap()
    cs_st = ctx.enter_context(sbuf("p3_cs", [128, D], BF16)).ap()
    xq_st = [ctx.enter_context(sbuf(f"p3_xq{i}", [128, D], F32)).ap() for i in range(2)]
    xsp = [ctx.enter_context(sbuf(f"p3_xsp{i}", [128, D], BF16)).ap() for i in range(2)]
    xsq = ctx.enter_context(sbuf("p3_xsq", [128, D], F32)).ap()
    bn2 = ctx.enter_context(sbuf("p3_bn2", [128, 2], F32)).ap()
    q2c = ctx.enter_context(sbuf("p3_q2c", [128, 1], F32)).ap()
    col = ctx.enter_context(sbuf("p3_col", [128, 1], F32)).ap()
    rstd = ctx.enter_context(sbuf("p3_rstd", [128, 1], F32)).ap()
    amx = ctx.enter_context(sbuf("p3_amx", [128, 1], F32)).ap()
    scol = ctx.enter_context(sbuf("p3_scol", [128, 1], F32)).ap()
    pbt = [s.pb[2], s.pb[3]]

    with nc.Block() as block:
        @block.sync
        def _(sy):
            for tc in range(16):
                if tc >= 2:
                    sy.wait_ge(x2r, tc - 1)
                sy.dma_start(out=y_st[tc % 2],
                             in_=s.y_sc[tc * 128:(tc + 1) * 128, :]).then_inc(yld, 16)
                sy.dma_start(out=x_st[tc % 2],
                             in_=s.x_own[tc * 128:(tc + 1) * 128, :]).then_inc(yld, 16)
            sy.wait_ge(spd, 16 * 17 * 16)

        @block.vector
        def _(v):
            for tc in range(16):
                b = tc % 2
                v.wait_ge(yld, 16 * 2 * (tc + 1))
                if tc >= 2:
                    v.wait_ge(spd, 16 * 17 * (tc - 1))
                v.tensor_tensor(x2_st[b], y_st[b], x_st[b], ALU.add).then_inc(x2r, 1)
                v.tensor_reduce(bn2[:, 0:1], x2_st[b], AX.X, ALU.add)
                v.tensor_tensor(xsq, x2_st[b], x2_st[b], ALU.mult)
                v.tensor_reduce(q2c, xsq, AX.X, ALU.add)
                v.drain()
                v.tensor_scalar(bn2[:, 0:1], bn2[:, 0:1], 1.0 / D, None, ALU.mult)
                v.drain()
                v.tensor_tensor(col, bn2[:, 0:1], bn2[:, 0:1], ALU.mult)
                v.drain()
                v.scalar_tensor_tensor(col, q2c, 1.0 / D, col, ALU.mult, ALU.subtract)
                v.drain()
                v.tensor_scalar(col, col, EPS_LN, None, ALU.add)
                v.drain()
                v.reciprocal(col, col).then_inc(rs1, 1)
                v.wait_ge(rs2, tc + 1)
                v.tensor_scalar(ln2, x2_st[b], bn2[:, 0:1], None, ALU.subtract)
                v.tensor_scalar(ln2, ln2, rstd, None, ALU.mult)
                v.tensor_reduce(amx, ln2, AX.X, ALU.max, apply_absolute_value=True)
                v.drain()
                v.tensor_scalar(amx, amx, EPS_Q, None, ALU.max)
                v.drain()
                v.reciprocal(scol, amx)
                v.drain()
                v.tensor_scalar(scol, scol, 127.0, None, ALU.mult)
                v.drain()
                v.tensor_scalar(z_st, ln2, scol, None, ALU.mult)
                v.tensor_copy(sti, z_st)
                v.tensor_tensor(e_st, z_st, sti, ALU.subtract).then_inc(er, 1)
                v.wait_ge(sg, tc + 1)
                v.tensor_tensor(cs_st, sg1, sg2, ALU.add)
                v.scalar_tensor_tensor(xq_st[b], cs_st, 0.5, sti, ALU.mult, ALU.add).then_inc(xqr, 1)
                for fc in range(16):
                    n = tc * 17 + fc
                    v.wait_ge(trd, n + 1)
                    v.tensor_copy(xsp[b][:, fc * 128:(fc + 1) * 128],
                                  pbt[fc % 2][:, 0:128]).then_inc(ted, 1)
                v.wait_ge(trd, tc * 17 + 17)
                v.tensor_copy(s.amax1_row[0:1, tc * 128:(tc + 1) * 128],
                              s.pb[1][0:1, 0:128]).then_inc(ted, 1)

        @block.scalar
        def _(sc):
            for tc in range(16):
                sc.wait_ge(rs1, tc + 1)
                sc.activation(rstd, col, AF.Sqrt).then_inc(rs2, 1)
                sc.wait_ge(er, tc + 1)
                sc.sign(sg1, e_st, bias=s.nhalfc)
                sc.sign(sg2, e_st, bias=s.halfc).then_inc(sg, 1)

        @block.tensor
        def _(t):
            for tc in range(16):
                t.wait_ge(xqr, tc + 1)
                for fc in range(16):
                    n = tc * 17 + fc
                    if n >= 2:
                        t.wait_ge(ted, n - 1)
                    t.transpose(pbt[fc % 2][:, 0:128],
                                xq_st[tc % 2][:, fc * 128:(fc + 1) * 128],
                                s.ident_f32).then_inc(trd, 1)
                if tc >= 1:
                    t.wait_ge(ted, tc * 17)
                t.transpose(s.pb[1][0:1, 0:128], amx, s.ident_f32).then_inc(trd, 1)

        @block.gpsimd
        def _(g):
            for tc in range(16):
                b = tc % 2
                g.wait_ge(x2r, tc + 1)
                g.dma_start(out=s.x2_sc[tc * 128:(tc + 1) * 128, :],
                            in_=x2_st[b]).then_inc(spd, 16)
                g.wait_ge(ted, tc * 17 + 16)
                for fc in range(16):
                    g.dma_start(out=s.xq_sc[fc, :, tc * 128:(tc + 1) * 128],
                                in_=xsp[b][:, fc * 128:(fc + 1) * 128]).then_inc(spd, 16)
            g.wait_ge(spd, 16 * 17 * 16)
    ctx.close()


def emit_c_rows(s):
    nc = s.nc
    finv = s.sem("cr_finv")
    fint = s.sem("cr_fint")
    with nc.Block() as block:
        @block.vector
        def _(v):
            v.tensor_scalar(s.amax1_row, s.amax1_row, s.m1c, 1.0 / 254.0, ALU.mult, ALU.mult)
            v.drain()
            v.drain().then_inc(finv, 1)
        @block.tensor
        def _(t):
            t.wait_ge(finv, 1)
            for i in range(4):
                t.matmul(s.pb[i][:, 0:512], s.ones_row,
                         s.amax1_row[0:1, i * 512:(i + 1) * 512], start=True, stop=True)
            t.drain().then_inc(fint, 1)
    with nc.Block() as block:
        @block.vector
        def _(v):
            v.wait_ge(fint, 1)
            for i in range(4):
                v.tensor_copy(s.c1_b[:, i * 512:(i + 1) * 512], s.pb[i][:, 0:512])


def emit_ffn_a(s):
    nc = s.nc
    xqld = s.sem("fa_xqld")
    w1ld = s.sem("fa_w1ld")
    w2ld = s.sem("fa_w2ld")
    z1r = s.sem("fa_z1r")
    z2r = s.sem("fa_z2r")
    sg1d = s.sem("fa_sg1d")
    sg2d = s.sem("fa_sg2d")
    t1r = s.sem("fa_t1r")
    t2r = s.sem("fa_t2r")
    mmd = s.sem("fa_mmd")
    hev = s.sem("fa_hev")
    gel = s.sem("fa_gel")
    mxd = s.sem("fa_mxd")
    gsp = s.sem("fa_gsp")
    t2sp = s.sem("fa_t2sp")
    amp = s.sem("fa_amp")
    amc = s.sem("fa_amc")

    ctx = contextlib.ExitStack()
    sbuf = nc.sbuf_tensor
    xqT = ctx.enter_context(sbuf("fa_xqT", [128, 16 * T], BF16)).ap()
    w1t = ctx.enter_context(sbuf("fa_w1t", [128, D], F32)).ap()
    w2t = ctx.enter_context(sbuf("fa_w2t", [128, D], F32)).ap()
    z1 = ctx.enter_context(sbuf("fa_z1", [128, D], F32)).ap()
    z2 = ctx.enter_context(sbuf("fa_z2", [128, D], F32)).ap()
    sa1 = ctx.enter_context(sbuf("fa_sa1", [128, D], BF16)).ap()
    sb1 = ctx.enter_context(sbuf("fa_sb1", [128, D], BF16)).ap()
    sa2 = ctx.enter_context(sbuf("fa_sa2", [128, D], BF16)).ap()
    sb2 = ctx.enter_context(sbuf("fa_sb2", [128, D], BF16)).ap()
    t1 = [ctx.enter_context(sbuf(f"fa_t1_{i}", [128, D], BF16)).ap() for i in range(2)]
    t2 = [ctx.enter_context(sbuf(f"fa_t2_{i}", [128, D], BF16)).ap() for i in range(2)]
    h_st = ctx.enter_context(sbuf("fa_h", [128, T], F32)).ap()
    g_st = [ctx.enter_context(sbuf(f"fa_g{i}", [128, T], F32)).ap() for i in range(2)]
    mx = ctx.enter_context(sbuf("fa_mx", [128, T], F32)).ap()
    mn = ctx.enter_context(sbuf("fa_mn", [128, T], F32)).ap()
    NDH = 64

    with nc.Block() as block:
        @block.sync
        def _(sy):
            for fc in range(16):
                sy.dma_start(out=xqT[:, fc * T:(fc + 1) * T], in_=s.xq_sc[fc]).then_inc(xqld, 16)
            for dh in range(NDH):
                if dh >= 1:
                    sy.wait_ge(z1r, dh)
                sy.dma_start(out=w1t,
                             in_=s.w1T[:, dh * 128:(dh + 1) * 128].rearrange(
                                 "(fc f) o -> f fc o", f=128)).then_inc(w1ld, 16)
                if dh >= 1:
                    sy.wait_ge(z2r, dh)
                sy.dma_start(out=w2t,
                             in_=s.w2T[dh * 128:(dh + 1) * 128, :]).then_inc(w2ld, 16)
            sy.wait_ge(gsp, 16 * NDH)
            sy.wait_ge(t2sp, 16 * NDH)

        @block.vector
        def _(v):
            v.wait_ge(s.sems["mfin"], 3)
            for dh in range(NDH):
                b = dh % 2
                v.wait_ge(w1ld, 16 * (dh + 1))
                if dh >= 1:
                    v.wait_ge(sg1d, dh)
                v.tensor_scalar(z1, w1t, s.s1col, None, ALU.mult).then_inc(z1r, 1)
                v.wait_ge(w2ld, 16 * (dh + 1))
                if dh >= 1:
                    v.wait_ge(sg2d, dh)
                v.tensor_scalar(z2, w2t, s.s2col, None, ALU.mult).then_inc(z2r, 1)
                v.wait_ge(sg1d, dh + 1)
                v.tensor_tensor(t1[b], sa1, sb1, ALU.add).then_inc(t1r, 1)
                v.wait_ge(sg2d, dh + 1)
                if dh >= 2:
                    v.wait_ge(t2sp, 16 * (dh - 1))
                v.tensor_tensor(t2[b], sa2, sb2, ALU.add).then_inc(t2r, 1)
                v.wait_ge(mmd, dh + 1)
                pset = 4 * (dh % 2)
                for i in range(4):
                    tth = v.tensor_tensor(h_st[:, i * 512:(i + 1) * 512],
                                          s.pb[pset + i], s.c1_b[:, i * 512:(i + 1) * 512],
                                          ALU.mult)
                tth.then_inc(hev, 1)
                v.wait_ge(gel, dh + 1)
                if dh == 0:
                    v.tensor_copy(mx, g_st[b])
                    v.tensor_copy(mn, g_st[b]).then_inc(mxd, 1)
                else:
                    v.tensor_tensor(mx, mx, g_st[b], ALU.max)
                    v.tensor_tensor(mn, mn, g_st[b], ALU.min).then_inc(mxd, 1)
            v.tensor_scalar(mn, mn, -1.0, None, ALU.mult)
            v.tensor_tensor(mx, mx, mn, ALU.max)
            v.drain().then_inc(amc, 1)
            v.wait_ge(amp, 1)
            for i in range(16):
                v.tensor_reduce(s.am2t[:, i:i + 1],
                                s.pb[i // 4][:, (i % 4) * 128:(i % 4 + 1) * 128],
                                AX.X, ALU.max)
            v.drain()
            v.tensor_scalar(s.am2t, s.am2t, EPS_Q, None, ALU.max)
            v.drain()
            v.tensor_tensor(s.c2_cols, s.am2t, s.m2col.broadcast_to([128, 16]), ALU.mult)
            v.drain()
            v.tensor_scalar(s.c2_cols, s.c2_cols, 1.0 / 254.0, None, ALU.mult)
            v.drain()

        @block.scalar
        def _(sc):
            for dh in range(NDH):
                b = dh % 2
                sc.wait_ge(z1r, dh + 1)
                if dh >= 1:
                    sc.wait_ge(t1r, dh)
                sc.sign(sa1, z1, bias=s.nhalfc)
                sc.sign(sb1, z1, bias=s.halfc).then_inc(sg1d, 1)
                sc.wait_ge(z2r, dh + 1)
                if dh >= 1:
                    sc.wait_ge(t2r, dh)
                sc.sign(sa2, z2, bias=s.nhalfc)
                sc.sign(sb2, z2, bias=s.halfc).then_inc(sg2d, 1)
                sc.wait_ge(hev, dh + 1)
                if dh >= 2:
                    sc.wait_ge(gsp, 16 * (dh - 1))
                sc.activation(g_st[b], h_st, AF.Gelu).then_inc(gel, 1)

        @block.tensor
        def _(t):
            t.wait_ge(xqld, 16 * 16)
            for dh in range(NDH):
                b = dh % 2
                t.wait_ge(t1r, dh + 1)
                if dh >= 2:
                    t.wait_ge(hev, dh - 1)
                pset = 4 * (dh % 2)
                for fc in range(16):
                    for i in range(4):
                        mmf = t.matmul(s.pb[pset + i], t1[b][:, fc * 128:(fc + 1) * 128],
                                       xqT[:, fc * T + i * 512: fc * T + (i + 1) * 512],
                                       start=(fc == 0), stop=(fc == 15))
                mmf.then_inc(mmd, 1)
            t.wait_ge(amc, 1)
            for i in range(16):
                t.transpose(s.pb[i // 4][:, (i % 4) * 128:(i % 4 + 1) * 128],
                            mx[:, i * 128:(i + 1) * 128], s.ident_f32)
            t.drain().then_inc(amp, 1)

        @block.gpsimd
        def _(g):
            for dh in range(NDH):
                b = dh % 2
                g.wait_ge(t2r, dh + 1)
                g.dma_start(out=s.t2_sc[dh * 128:(dh + 1) * 128, :], in_=t2[b]).then_inc(t2sp, 16)
                g.wait_ge(mxd, dh + 1)
                g.dma_start(out=s.g_sc[dh * 128:(dh + 1) * 128, :], in_=g_st[b]).then_inc(gsp, 16)
            g.wait_ge(gsp, 16 * NDH)
            g.wait_ge(t2sp, 16 * NDH)
    ctx.close()

    fpp = s.sem("fa_fpp")
    fpc = s.sem("fa_fpc")
    am16 = s.qnw_b[0:16, 0:128]
    with nc.Block() as block:
        @block.tensor
        def _(t):
            t.transpose(s.pb[2][0:16, 0:128], s.am2t, s.ident_f32).then_inc(fpp, 1)
        @block.vector
        def _(v):
            v.wait_ge(fpp, 1)
            v.tensor_copy(am16, s.pb[2][0:16, 0:128]).then_inc(fpc, 1)
        @block.scalar
        def _(sc):
            sc.wait_ge(fpc, 1)
            sc.dma_start(out=s.amax2_row,
                         in_=am16).then_inc(fpc, 16)
            sc.wait_ge(fpc, 17)
    f2v = s.sem("fa_f2v")
    f2t = s.sem("fa_f2t")
    with nc.Block() as block:
        @block.vector
        def _(v):
            v.reciprocal(s.amax2_row, s.amax2_row)
            v.drain()
            v.tensor_scalar(s.amax2_row, s.amax2_row, 127.0, None, ALU.mult)
            v.drain()
            v.drain().then_inc(f2v, 1)
        @block.tensor
        def _(t):
            t.wait_ge(f2v, 1)
            for i in range(4):
                t.matmul(s.pb[i][:, 0:512], s.ones_row,
                         s.amax2_row[0:1, i * 512:(i + 1) * 512], start=True, stop=True)
            t.drain().then_inc(f2t, 1)
    with nc.Block() as block:
        @block.vector
        def _(v):
            v.wait_ge(f2t, 1)
            for i in range(4):
                v.tensor_copy(s.s2_b[:, i * 512:(i + 1) * 512], s.pb[i][:, 0:512])


def emit_ffn_b(s):
    nc = s.nc
    gld = s.sem("fb_gld")
    zr = s.sem("fb_zr")
    sgd = s.sem("fb_sgd")
    hqr = s.sem("fb_hqr")
    hqsp = s.sem("fb_hqsp")
    t2ld = s.sem("fb_t2ld")
    hqld = s.sem("fb_hqld")
    x2ld = s.sem("fb_x2ld")
    mmh = s.sem("fb_mmh")
    yev = s.sem("fb_yev")
    yout = s.sem("fb_yout")
    NDH = 64
    NJ = 256

    ctx = contextlib.ExitStack()
    sbuf = nc.sbuf_tensor
    g_ld = [ctx.enter_context(sbuf(f"fb_g{i}", [128, 512], F32)).ap() for i in range(2)]
    z_st = ctx.enter_context(sbuf("fb_z", [128, 512], F32)).ap()
    sti = ctx.enter_context(sbuf("fb_sti", [128, 512], I32)).ap()
    e_st = ctx.enter_context(sbuf("fb_e", [128, 512], F32)).ap()
    sg1 = ctx.enter_context(sbuf("fb_sg1", [128, 512], BF16)).ap()
    sg2 = ctx.enter_context(sbuf("fb_sg2", [128, 512], BF16)).ap()
    cs_st = ctx.enter_context(sbuf("fb_cs", [128, 512], BF16)).ap()
    hq_st = [ctx.enter_context(sbuf(f"fb_hq{i}", [128, 512], BF16)).ap() for i in range(2)]
    T2 = ctx.enter_context(sbuf("fb_T2", [128, 64 * 512], BF16)).ap()
    hqb = [ctx.enter_context(sbuf(f"fb_hqb{i}", [128, 32 * 512], BF16)).ap() for i in range(2)]
    x2s = [ctx.enter_context(sbuf(f"fb_x2s{i}", [128, 4 * 512], F32)).ap() for i in range(2)]
    y_st = [ctx.enter_context(sbuf(f"fb_y{i}", [128, 4 * 512], F32)).ap() for i in range(2)]
    EV_AT = {96: 0, 152: 1, 218: 2}

    with nc.Block() as block:
        @block.sync
        def _(sy):
            for j in range(NJ):
                tq, dh = j // 64, j % 64
                if j >= 2:
                    sy.wait_ge(zr, j - 1)
                sy.dma_start(out=g_ld[j % 2],
                             in_=s.g_sc[dh * 128:(dh + 1) * 128,
                                        tq * 512:(tq + 1) * 512]).then_inc(gld, 16)
            sy.wait_ge(yout, 16 * 64)

        @block.scalar
        def _(sc):
            def x2_load(n):
                ocg, tcb = n // 4, n % 4
                if n >= 2:
                    sc.wait_ge(yev, 4 * (n - 1))
                for tci in range(4):
                    tc = tcb * 4 + tci
                    sc.dma_start(out=x2s[n % 2][:, tci * 512:(tci + 1) * 512],
                                 in_=s.x2_sc[tc * 128:(tc + 1) * 128,
                                             ocg * 512:(ocg + 1) * 512]).then_inc(x2ld, 16)
            X2_AT = {130: 2, 200: 3}
            x2_load(0)
            x2_load(1)
            for j in range(NJ):
                sc.wait_ge(zr, j + 1)
                sc.sign(sg1, e_st, bias=s.nhalfc)
                sc.sign(sg2, e_st, bias=s.halfc).then_inc(sgd, 1)
                if j in X2_AT:
                    x2_load(X2_AT[j])
            for n in range(4, 16):
                x2_load(n)

        @block.gpsimd
        def _(g):
            def hq_halves(n):
                tcb = n % 4
                srcq = s.hq_sc[:, tcb * 512:(tcb + 1) * 512].rearrange(
                    "(dc p) o -> p dc o", p=128)
                for H in range(2):
                    gidx = 2 * n + H
                    if gidx >= 2:
                        g.wait_ge(mmh, gidx - 1)
                    for part in range(4):
                        dc0 = H * 32 + part * 8
                        g.dma_start(out=hqb[H][:, part * 8 * 512:(part + 1) * 8 * 512],
                                    in_=srcq[:, dc0:dc0 + 8, :]).then_inc(hqld, 16)
            def t2_parts(ocg):
                src = s.t2_sc[:, ocg * 512:(ocg + 1) * 512].rearrange(
                    "(dc p) o -> p dc o", p=128)
                for part in range(8):
                    g.dma_start(out=T2[:, part * 8 * 512:(part + 1) * 8 * 512],
                                in_=src[:, part * 8:(part + 1) * 8, :]).then_inc(t2ld, 16)
            def y_out(n):
                ocg, tcb = n // 4, n % 4
                g.wait_ge(yev, 4 * (n + 1))
                for tci in range(4):
                    tc = tcb * 4 + tci
                    g.dma_start(out=s.y[tc * 128:(tc + 1) * 128,
                                        ocg * 512:(ocg + 1) * 512],
                                in_=y_st[n % 2][:, tci * 512:(tci + 1) * 512]).then_inc(yout, 16)

            t2_parts(0)
            for tq in range(4):
                for dh in range(NDH):
                    j = tq * 64 + dh
                    g.wait_ge(hqr, j + 1)
                    g.dma_start(out=s.hq_sc[dh * 128:(dh + 1) * 128,
                                            tq * 512:(tq + 1) * 512],
                                in_=hq_st[j % 2]).then_inc(hqsp, 16)
                g.wait_ge(hqsp, 16 * 64 * (tq + 1))
                hq_halves(tq)
                if tq >= 1:
                    y_out(tq - 1)
            for n in range(3, 16):
                ocg, tcb = n // 4, n % 4
                if tcb == 0:
                    g.wait_ge(mmh, 8 * ocg)
                    t2_parts(ocg)
                if n > 3:
                    hq_halves(n)
                y_out(n - 1 if n > 3 else 3)
            y_out(15)
            g.wait_ge(yout, 16 * 64)

        @block.tensor
        def _(t):
            for n in range(16):
                ocg, tcb = n // 4, n % 4
                pset = 4 * (tcb % 2)
                t.wait_ge(t2ld, 16 * 8 * (ocg + 1))
                if n >= 2:
                    t.wait_ge(yev, 4 * (n - 1))
                for H in range(2):
                    t.wait_ge(hqld, 16 * 4 * (2 * n + H + 1))
                    for dq in range(32):
                        dh = H * 32 + dq
                        for tci in range(4):
                            mmb = t.matmul(s.pb[pset + tci],
                                           hqb[H][:, dq * 512 + tci * 128:
                                                 dq * 512 + (tci + 1) * 128],
                                           T2[:, dh * 512:(dh + 1) * 512],
                                           start=(dh == 0), stop=(dh == NDH - 1))
                    mmb.then_inc(mmh, 1)

        @block.vector
        def _(v):
            def evac(n):
                tcb = n % 4
                pset = 4 * (tcb % 2)
                v.wait_ge(mmh, 2 * (n + 1))
                v.wait_ge(x2ld, 16 * 4 * (n + 1))
                if n >= 2:
                    v.wait_ge(yout, 16 * 4 * (n - 1))
                for tci in range(4):
                    tc = tcb * 4 + tci
                    v.scalar_tensor_tensor(y_st[n % 2][:, tci * 512:(tci + 1) * 512],
                                           s.pb[pset + tci], s.c2_cols[:, tc:tc + 1],
                                           x2s[n % 2][:, tci * 512:(tci + 1) * 512],
                                           ALU.mult, ALU.add).then_inc(yev, 1)

            nev = 0
            for j in range(NJ):
                tq = j // 64
                v.wait_ge(gld, 16 * (j + 1))
                v.tensor_tensor(z_st, g_ld[j % 2],
                                s.s2_b[:, tq * 512:(tq + 1) * 512], ALU.mult)
                v.tensor_copy(sti, z_st)
                v.tensor_tensor(e_st, z_st, sti, ALU.subtract).then_inc(zr, 1)
                v.wait_ge(sgd, j + 1)
                v.tensor_tensor(cs_st, sg1, sg2, ALU.add)
                if j >= 2:
                    v.wait_ge(hqsp, 16 * (j - 1))
                v.scalar_tensor_tensor(hq_st[j % 2], cs_st, 0.5, sti,
                                       ALU.mult, ALU.add).then_inc(hqr, 1)
                if j in EV_AT:
                    evac(EV_AT[j])
                    nev += 1
            for n in range(nev, 16):
                evac(n)
    ctx.close()


# version 04
# speedup vs baseline: 1.0618x; 1.0239x over previous
import contextlib
import numpy as np
import ml_dtypes
import concourse.bass as bass
import concourse.mybir as mybir
from concourse.bass_utils import run_bass_kernel_spmd

F32 = mybir.dt.float32
BF16 = mybir.dt.bfloat16
I32 = mybir.dt.int32
AF = mybir.ActivationFunctionType
ALU = mybir.AluOpType
AX = mybir.AxisListType

D = 2048
L = 4096
T = 2048
NH = 16
NKV = 8
HD = 128
DH = 8192
SCALE = float(HD) ** -0.5
EPS_LN = 1e-5
EPS_RMS = 1e-6
EPS_Q = 1e-5
NEG = -1.0e30

DEBUG = False
STOP_AFTER = None
RACE_DETECT = False
PP_LIMIT = None
PP_NO_PRE = False
PP_ORDER = ("k", "v", "qa", "qb")


def _bf(x):
    return np.asarray(x).astype(ml_dtypes.bfloat16)


def make_mask(h):
    m = np.zeros((8, 128, 1024), dtype=np.float32)
    if h == 1:
        for c in range(8):
            kk = c * 128 + np.arange(128)[:, None]
            qq = np.arange(1024)[None, :]
            m[c] = np.where(kk <= qq, NEG, 0.0)
    return _bf(m)


def kernel(L0_attn_hidden_states, L0_attn_Wq, L0_attn_Wk, L0_attn_Wv, L0_attn_Wo,
           L0_attn_q_norm_w, L0_attn_k_norm_w, L0_ffn_w1, L0_ffn_w2):
    x = np.ascontiguousarray(np.asarray(L0_attn_hidden_states, dtype=np.float32))
    wqT = np.ascontiguousarray(np.asarray(L0_attn_Wq, dtype=np.float32).T)
    wkT = np.ascontiguousarray(np.asarray(L0_attn_Wk, dtype=np.float32).T)
    wvT = np.ascontiguousarray(np.asarray(L0_attn_Wv, dtype=np.float32).T)
    woT = np.ascontiguousarray(np.asarray(L0_attn_Wo, dtype=np.float32).T)
    w1T = np.ascontiguousarray(np.asarray(L0_ffn_w1, dtype=np.float32).T)
    w2T = np.ascontiguousarray(np.asarray(L0_ffn_w2, dtype=np.float32).T)
    qnw = np.ascontiguousarray(np.asarray(L0_attn_q_norm_w, dtype=np.float32).reshape(1, HD))
    knw = np.ascontiguousarray(np.asarray(L0_attn_k_norm_w, dtype=np.float32).reshape(1, HD))
    ident_f32 = np.eye(128, dtype=np.float32)
    ident_bf = _bf(np.eye(128))
    masks = [make_mask(0), make_mask(1)]

    nc = build_nc()
    in_maps = []
    for c in range(8):
        b, h = c // 2, c % 2
        own = np.ascontiguousarray(x[b, h * T:(h + 1) * T, :])
        oth = np.ascontiguousarray(x[b, (1 - h) * T:(2 - h) * T, :])
        in_maps.append({
            "x_own": own, "x_oth": oth,
            "wqT": wqT, "wkT": wkT, "wvT": wvT, "woT": woT,
            "w1T": w1T, "w2T": w2T,
            "qnw_row": qnw, "knw_row": knw,
            "ident_f32": ident_f32, "ident_bf": ident_bf,
            "mask": masks[h],
        })
    res = run_bass_kernel_spmd(nc, in_maps, core_ids=list(range(8)))
    out = np.empty((4, L, D), dtype=np.float32)
    for c in range(8):
        b, h = c // 2, c % 2
        out[b, h * T:(h + 1) * T, :] = res.results[c]["y"]
    return out


class NS:
    pass


def build_nc():
    nc = bass.Bass(trn_type="TRN2", target_bir_lowering=False,
                   detect_race_conditions=RACE_DETECT)
    s = NS()
    s.nc = nc
    skind = "ExternalOutput" if DEBUG else "Internal"

    s.x_own = nc.dram_tensor("x_own", [T, D], F32, kind="ExternalInput").ap()
    s.x_oth = nc.dram_tensor("x_oth", [T, D], F32, kind="ExternalInput").ap()
    s.wqT = nc.dram_tensor("wqT", [D, D], F32, kind="ExternalInput").ap()
    s.wkT = nc.dram_tensor("wkT", [D, NKV * HD], F32, kind="ExternalInput").ap()
    s.wvT = nc.dram_tensor("wvT", [D, NKV * HD], F32, kind="ExternalInput").ap()
    s.woT = nc.dram_tensor("woT", [D, D], F32, kind="ExternalInput").ap()
    s.w1T = nc.dram_tensor("w1T", [D, DH], F32, kind="ExternalInput").ap()
    s.w2T = nc.dram_tensor("w2T", [DH, D], F32, kind="ExternalInput").ap()
    s.qnw_row_d = nc.dram_tensor("qnw_row", [1, HD], F32, kind="ExternalInput").ap()
    s.knw_row_d = nc.dram_tensor("knw_row", [1, HD], F32, kind="ExternalInput").ap()
    s.ident_f32_d = nc.dram_tensor("ident_f32", [128, 128], F32, kind="ExternalInput").ap()
    s.ident_bf_d = nc.dram_tensor("ident_bf", [128, 128], BF16, kind="ExternalInput").ap()
    s.mask_d = nc.dram_tensor("mask", [8, 128, 1024], BF16, kind="ExternalInput").ap()

    s.y = nc.dram_tensor("y", [T, D], F32, kind="ExternalOutput").ap()

    def scr(name, shape, dtype):
        return nc.dram_tensor(name, shape, dtype, kind=skind).ap()
    s.lnT_hi = scr("lnT_hi", [16, 128, L], BF16)
    s.lnT_lo = scr("lnT_lo", [16, 128, L], BF16)
    s.kn_hi = scr("kn_hi", [NKV, 32, 128, 128], BF16)
    s.kn_lo = scr("kn_lo", [NKV, 32, 128, 128], BF16)
    s.vt_hi = scr("vt_hi", [L, NKV * HD], BF16)
    s.vt_lo = scr("vt_lo", [L, NKV * HD], BF16)
    s.qn_hi = scr("qn_hi", [NH, 16, 128, 128], BF16)
    s.qn_lo = scr("qn_lo", [NH, 16, 128, 128], BF16)
    s.wo_hi = scr("wo_hi", [D, D], BF16)
    s.wo_lo = scr("wo_lo", [D, D], BF16)
    s.y_sc = scr("y_sc", [T, D], F32)
    s.x2_sc = scr("x2_sc", [T, D], F32)
    s.xq_sc = scr("xq_sc", [16, 128, T], BF16)
    s.g_sc = scr("g_sc", [DH, T], F32)
    s.hq_sc = scr("hq_sc", [DH, T], BF16)
    s.t2_sc = scr("t2_sc", [DH, D], BF16)
    s.dbg_m = scr("dbg_m", [1, 4], F32)

    def sb(name, shape, dtype):
        return nc.alloc_sbuf_tensor(name, shape, dtype).ap()
    s.ident_f32 = sb("c_ident32", [128, 128], F32)
    s.ident_bf = sb("c_identbf", [128, 128], BF16)
    s.qnw_row = sb("c_qnw", [1, HD], F32)
    s.knw_row = sb("c_knw", [1, HD], F32)
    s.qnw_b = sb("c_qnwb", [128, HD], F32)
    s.knw_b = sb("c_knwb", [128, HD], F32)
    s.ones_bf = sb("c_onesbf", [128, 1], BF16)
    s.ones_col = sb("c_onescol", [128, 1], F32)
    s.ones_row = sb("c_onesrow", [1, 128], F32)
    s.halfc = sb("c_halfc", [128, 1], F32)
    s.nhalfc = sb("c_nhalfc", [128, 1], F32)
    s.m1c = sb("c_m1c", [1, 1], F32)
    s.m2c = sb("c_m2c", [1, 1], F32)
    s.s1col = sb("c_s1col", [128, 1], F32)
    s.s2col = sb("c_s2col", [128, 1], F32)
    s.m2col = sb("c_m2col", [128, 1], F32)
    s.mid_ctx = contextlib.ExitStack()
    s.m_acc1 = s.mid_ctx.enter_context(nc.sbuf_tensor("c_macc1", [128, 1], F32)).ap()
    s.m_acc2 = s.mid_ctx.enter_context(nc.sbuf_tensor("c_macc2", [128, 1], F32)).ap()
    s.m_tmpc = s.mid_ctx.enter_context(nc.sbuf_tensor("c_mtmpc", [128, 1], F32)).ap()
    s.pre_buf = s.mid_ctx.enter_context(nc.sbuf_tensor("c_prebuf", [128, 4096], F32)).ap()
    s.dbg_st = s.mid_ctx.enter_context(nc.sbuf_tensor("c_dbgst", [1, 4], F32)).ap()

    s.pb = [nc.alloc_psum_tensor(f"pb{i}", [128, 512], F32).ap() for i in range(8)]

    s.sems = {}
    def sem(name):
        h = nc.ctx.enter_context(nc.semaphore(name))
        s.sems[name] = h
        return h
    s.sem = sem

    s.pre_gld = sem("pre_gld")
    s.pre_red = sem("pre_red")
    s.pre_done = sem("pre_done")
    s.pre_plan = []
    for i in range(D // 128):
        for j in range(DH // 2048):
            s.pre_plan.append((s.w1T, i * 128, j * 2048, s.m_acc1))
    for i in range(DH // 128):
        s.pre_plan.append((s.w2T, i * 128, 0, s.m_acc2))
    s.pre_n = 0
    s.pre_nv = 0

    emit_init(s)
    if STOP_AFTER == "init":
        return nc
    emit_p1_ln(s)
    if STOP_AFTER == "ln":
        return nc
    for kind in PP_ORDER:
        emit_proj_pass(s, kind)
        if STOP_AFTER == kind:
            return nc
    emit_wo_split(s)
    emit_m_finalize(s)
    if STOP_AFTER == "p1":
        return nc
    s.mid_ctx.close()
    s.s2_b = sb("c_s2b", [128, T], F32)
    s.c2_cols = sb("c_c2cols", [128, 16], F32)
    s.am2t = sb("c_am2t", [128, 16], F32)
    late_ctx = contextlib.ExitStack()
    s.amax1_row = late_ctx.enter_context(nc.sbuf_tensor("c_am1row", [1, T], F32)).ap()
    s.amax2_row = late_ctx.enter_context(nc.sbuf_tensor("c_am2row", [1, T], F32)).ap()
    s.c1_b = late_ctx.enter_context(nc.sbuf_tensor("c_c1b", [128, T], F32)).ap()
    emit_p2(s)
    if STOP_AFTER == "p2":
        return nc
    emit_p3(s)
    emit_c_rows(s)
    if STOP_AFTER == "p3":
        return nc
    emit_ffn_a(s)
    if STOP_AFTER == "ffa":
        late_ctx.close()
        return nc
    late_ctx.close()
    emit_ffn_b(s)
    return nc


def feed_prepass(s, g, nchunks):
    for _ in range(nchunks):
        if s.pre_n >= len(s.pre_plan):
            return
        w, r0, c0, acc = s.pre_plan[s.pre_n]
        n = s.pre_n
        if n >= 2:
            g.wait_ge(s.pre_red, n - 1)
        buf = s.pre_buf[:, (n % 2) * 2048:(n % 2 + 1) * 2048]
        g.dma_start(out=buf, in_=w[r0:r0 + 128, c0:c0 + 2048]).then_inc(s.pre_gld, 16)
        s.pre_n += 1


def feed_prepass_v(s, v, nchunks):
    for _ in range(nchunks):
        if s.pre_nv >= len(s.pre_plan):
            return
        w, r0, c0, acc = s.pre_plan[s.pre_nv]
        n = s.pre_nv
        v.wait_ge(s.pre_gld, 16 * (n + 1))
        buf = s.pre_buf[:, (n % 2) * 2048:(n % 2 + 1) * 2048]
        v.tensor_reduce(s.m_tmpc, buf, AX.X, ALU.add, apply_absolute_value=True)
        v.drain()
        v.tensor_tensor(acc, acc, s.m_tmpc, ALU.add).then_inc(s.pre_red, 1)
        s.pre_nv += 1
        if s.pre_nv == len(s.pre_plan):
            v.drain().then_inc(s.pre_done, 1)


def emit_init(s):
    nc = s.nc
    ld = s.sem("init_ld")
    ldv = s.sem("init_ldv")
    ldt = s.sem("init_ldt")
    with nc.Block() as block:
        @block.sync
        def _(sy):
            sy.dma_start(out=s.ident_f32, in_=s.ident_f32_d).then_inc(ld, 16)
            sy.dma_start(out=s.ident_bf, in_=s.ident_bf_d).then_inc(ld, 16)
            sy.dma_start(out=s.qnw_row, in_=s.qnw_row_d).then_inc(ld, 16)
            sy.dma_start(out=s.knw_row, in_=s.knw_row_d).then_inc(ld, 16)
            sy.wait_ge(ld, 16 * 4)
        @block.vector
        def _(v):
            v.memset(s.ones_bf, 1.0)
            v.memset(s.ones_col, 1.0)
            v.memset(s.ones_row, 1.0)
            v.memset(s.halfc, 0.5)
            v.memset(s.nhalfc, -0.5)
            v.memset(s.m_acc1, 0.0)
            v.memset(s.m_acc2, 0.0)
            v.memset(s.m_tmpc, 0.0)
            v.drain().then_inc(ldv, 1)
    with nc.Block() as block:
        @block.tensor
        def _(t):
            t.wait_ge(ld, 16 * 4)
            t.wait_ge(ldv, 1)
            t.matmul(s.pb[0][:, 0:HD], s.ones_row, s.qnw_row, start=True, stop=True)
            t.matmul(s.pb[1][:, 0:HD], s.ones_row, s.knw_row, start=True, stop=True).then_inc(ldt, 1)
        @block.vector
        def _(v):
            v.wait_ge(ldt, 1)
            v.tensor_copy(s.qnw_b, s.pb[0][:, 0:HD])
            v.tensor_copy(s.knw_b, s.pb[1][:, 0:HD])


def emit_p1_ln(s):
    nc = s.nc
    xld = s.sem("l_xld")
    lnr = s.sem("l_lnr")
    rs1 = s.sem("l_rs1")
    rs2 = s.sem("l_rs2")
    trd = s.sem("l_trd")
    evd = s.sem("l_evd")
    spd = s.sem("l_spd")
    ctx = contextlib.ExitStack()
    sbuf = nc.sbuf_tensor
    x_st = [ctx.enter_context(sbuf(f"l_x{i}", [128, D], F32)).ap() for i in range(2)]
    ln_st = [ctx.enter_context(sbuf(f"l_ln{i}", [128, D], F32)).ap() for i in range(2)]
    hi_st = [ctx.enter_context(sbuf(f"l_hi{i}", [128, D], BF16)).ap() for i in range(2)]
    lo_st = [ctx.enter_context(sbuf(f"l_lo{i}", [128, D], BF16)).ap() for i in range(2)]
    xsq = ctx.enter_context(sbuf("l_xsq", [128, D], F32)).ap()
    bn2 = ctx.enter_context(sbuf("l_bn2", [128, 2], F32)).ap()
    q2c = ctx.enter_context(sbuf("l_q2c", [128, 1], F32)).ap()
    veps = ctx.enter_context(sbuf("l_veps", [128, 1], F32)).ap()
    rvar = ctx.enter_context(sbuf("l_rvar", [128, 1], F32)).ap()
    rstd = ctx.enter_context(sbuf("l_rstd", [128, 1], F32)).ap()
    NTC = 32

    with nc.Block() as block:
        @block.sync
        def _(sy):
            for tc in range(NTC):
                src = s.x_own if tc < 16 else s.x_oth
                r = (tc % 16) * 128
                if tc >= 2:
                    sy.wait_ge(lnr, tc - 1)
                sy.dma_start(out=x_st[tc % 2], in_=src[r:r + 128, :]).then_inc(xld, 16)
            sy.wait_ge(spd, 16 * 32 * NTC)

        @block.vector
        def _(v):
            for tc in range(NTC):
                b = tc % 2
                v.wait_ge(xld, 16 * (tc + 1))
                v.tensor_reduce(bn2[:, 0:1], x_st[b], AX.X, ALU.add)
                v.tensor_tensor(xsq, x_st[b], x_st[b], ALU.mult)
                v.tensor_reduce(q2c, xsq, AX.X, ALU.add)
                v.drain()
                v.tensor_scalar(bn2[:, 0:1], bn2[:, 0:1], 1.0 / D, None, ALU.mult)
                v.drain()
                v.tensor_tensor(veps, bn2[:, 0:1], bn2[:, 0:1], ALU.mult)
                v.drain()
                v.scalar_tensor_tensor(veps, q2c, 1.0 / D, veps, ALU.mult, ALU.subtract)
                v.drain()
                v.tensor_scalar(veps, veps, EPS_LN, None, ALU.add)
                v.drain()
                v.reciprocal(rvar, veps).then_inc(rs1, 1)
                if tc >= 2:
                    v.wait_ge(trd, 16 * (tc - 1))
                    v.wait_ge(spd, 16 * 32 * (tc - 1))
                v.wait_ge(rs2, tc + 1)
                v.tensor_scalar(ln_st[b], x_st[b], bn2[:, 0:1], None, ALU.subtract)
                v.tensor_scalar(ln_st[b], ln_st[b], rstd, None, ALU.mult).then_inc(lnr, 1)
                for fc in range(16):
                    pbt = s.pb[6 + (fc % 2)]
                    v.wait_ge(trd, tc * 16 + fc + 1)
                    v.tensor_copy(hi_st[b][:, fc * 128:(fc + 1) * 128], pbt[:, 0:128])
                    v.tensor_tensor(lo_st[b][:, fc * 128:(fc + 1) * 128], pbt[:, 0:128],
                                    hi_st[b][:, fc * 128:(fc + 1) * 128],
                                    ALU.subtract).then_inc(evd, 1)
                if tc % 8 == 7:
                    feed_prepass_v(s, v, 4)

        @block.scalar
        def _(sc):
            def spill(tc):
                bb = tc % 2
                sc.wait_ge(evd, 16 * (tc + 1))
                for fc in range(16):
                    sc.dma_start(out=s.lnT_hi[fc, :, tc * 128:(tc + 1) * 128],
                                 in_=hi_st[bb][:, fc * 128:(fc + 1) * 128]).then_inc(spd, 16)
                    sc.dma_start(out=s.lnT_lo[fc, :, tc * 128:(tc + 1) * 128],
                                 in_=lo_st[bb][:, fc * 128:(fc + 1) * 128]).then_inc(spd, 16)
            for tc in range(NTC):
                sc.wait_ge(rs1, tc + 1)
                sc.activation(rstd, rvar, AF.Sqrt).then_inc(rs2, 1)
                if tc >= 1:
                    spill(tc - 1)
            spill(NTC - 1)

        @block.tensor
        def _(t):
            for tc in range(NTC):
                t.wait_ge(lnr, tc + 1)
                for fc in range(16):
                    pbt = s.pb[6 + (fc % 2)]
                    if tc * 16 + fc >= 2:
                        t.wait_ge(evd, tc * 16 + fc - 1)
                    t.transpose(pbt[:, 0:128], ln_st[tc % 2][:, fc * 128:(fc + 1) * 128],
                                s.ident_f32).then_inc(trd, 1)

        @block.gpsimd
        def _(g):
            feed_prepass(s, g, 16)
    ctx.close()


def emit_proj_pass(s, kind):
    nc = s.nc
    is_q = kind in ("qa", "qb")
    is_v = kind == "v"
    NTC = 16 if is_q else 32
    if PP_LIMIT is not None:
        NTC = min(NTC, PP_LIMIT)
        NTC = max(4, NTC)
    w_dram = {"k": s.wkT, "v": s.wvT,
              "qa": s.wqT[:, 0:1024], "qb": s.wqT[:, 1024:2048]}[kind]
    head_off = 8 if kind == "qb" else 0
    nrm_b = s.qnw_b if is_q else s.knw_b
    dst_hi = s.qn_hi if is_q else s.kn_hi
    dst_lo = s.qn_lo if is_q else s.kn_lo

    names = ["wld", "wsp", "lnld", "pdone", "pfree", "rs1", "rs2",
             "knr", "trd", "ted", "ksr", "spd"]
    if not hasattr(s, "pp_sems"):
        s.pp_sems = {n: s.sem("pp_" + n) for n in names}
        s.pp_base = {n: 0 for n in names}
    B = dict(s.pp_base)
    SM = s.pp_sems
    wld, wsp, lnld, pdone, pfree, rs1, rs2, knr, trd, ted, ksr, spd = (
        SM[n] for n in names)

    p = kind + "_"
    ctx = contextlib.ExitStack()
    sbuf = nc.sbuf_tensor
    w_hi = ctx.enter_context(sbuf(p + "whi", [128, 16 * 1024], BF16)).ap()
    w_lo = ctx.enter_context(sbuf(p + "wlo", [128, 16 * 1024], BF16)).ap()
    wtmp = [ctx.enter_context(sbuf(p + f"wtmp{i}", [128, 1024], F32)).ap() for i in range(2)]
    lnb_hi = [ctx.enter_context(sbuf(p + f"lnhi{i}", [128, 16 * 512], BF16)).ap() for i in range(2)]
    lnb_lo = [ctx.enter_context(sbuf(p + f"lnlo{i}", [128, 16 * 512], BF16)).ap() for i in range(2)]
    kf = ctx.enter_context(sbuf(p + "kf", [128, 1024], F32)).ap()
    if is_v:
        vsp_hi = [ctx.enter_context(sbuf(p + f"vsphi{i}", [128, 1024], BF16)).ap() for i in range(2)]
        vsp_lo = [ctx.enter_context(sbuf(p + f"vsplo{i}", [128, 1024], BF16)).ap() for i in range(2)]
    else:
        ksq = ctx.enter_context(sbuf(p + "ksq", [128, 1024], F32)).ap()
        red8 = ctx.enter_context(sbuf(p + "red8", [128, 8], F32)).ap()
        rv8 = ctx.enter_context(sbuf(p + "rv8", [128, 8], F32)).ap()
        rstd8 = ctx.enter_context(sbuf(p + "rstd8", [128, 8], F32)).ap()
        knf8 = ctx.enter_context(sbuf(p + "knf8", [128, 1024], F32)).ap()
        sp_st = [ctx.enter_context(sbuf(p + f"spst{i}", [128, 16 * 128], BF16)).ap() for i in range(2)]

    NTG = NTC // 4
    pbt = [s.pb[4], s.pb[5]]

    with nc.Block() as block:
        @block.sync
        def _(sy):
            for fc in range(16):
                if fc >= 2:
                    sy.wait_ge(wsp, B["wsp"] + fc - 1)
                sy.dma_start(out=wtmp[fc % 2],
                             in_=w_dram[fc * 128:(fc + 1) * 128, :]).then_inc(wld, 16)
            for tg in range(NTG):
                if tg >= 2:
                    sy.wait_ge(pdone, B["pdone"] + 4 * (tg - 1))
                ts = tg * 512
                sy.dma_start(out=lnb_hi[tg % 2],
                             in_=s.lnT_hi[:, :, ts:ts + 512].rearrange("fc f t -> f fc t")
                             ).then_inc(lnld, 16)
                sy.dma_start(out=lnb_lo[tg % 2],
                             in_=s.lnT_lo[:, :, ts:ts + 512].rearrange("fc f t -> f fc t")
                             ).then_inc(lnld, 16)
            sy.wait_ge(spd, B["spd"] + 16 * (2 * NTC if is_v else 16 * NTC))

        @block.vector
        def _(v):
            for fc in range(16):
                v.wait_ge(wld, B["wld"] + 16 * (fc + 1))
                v.tensor_copy(w_hi[:, fc * 1024:(fc + 1) * 1024], wtmp[fc % 2])
                v.tensor_tensor(w_lo[:, fc * 1024:(fc + 1) * 1024], wtmp[fc % 2],
                                w_hi[:, fc * 1024:(fc + 1) * 1024], ALU.subtract).then_inc(wsp, 1)
            for tc in range(NTC):
                v.wait_ge(pdone, B["pdone"] + tc + 1)
                pa = s.pb[2 * (tc % 2)]
                pa1 = s.pb[2 * (tc % 2) + 1]
                v.tensor_copy(kf[:, 0:512], pa)
                v.tensor_copy(kf[:, 512:1024], pa1).then_inc(pfree, 1)
                bb = tc % 2
                if is_v:
                    if tc >= 2:
                        v.wait_ge(spd, B["spd"] + 16 * 2 * (tc - 1))
                    v.tensor_copy(vsp_hi[bb], kf)
                    v.tensor_tensor(vsp_lo[bb], kf, vsp_hi[bb], ALU.subtract).then_inc(ksr, 1)
                    if tc % 8 == 7:
                        feed_prepass_v(s, v, 5)
                    continue
                v.tensor_tensor(ksq, kf, kf, ALU.mult)
                for g8 in range(8):
                    v.tensor_reduce(red8[:, g8:g8 + 1], ksq[:, g8 * 128:(g8 + 1) * 128],
                                    AX.X, ALU.add)
                v.drain()
                v.tensor_scalar(rv8, red8, 1.0 / HD, EPS_RMS, ALU.mult, ALU.add)
                v.drain()
                v.reciprocal(rv8, rv8).then_inc(rs1, 1)
                v.wait_ge(rs2, B["rs2"] + tc + 1)
                for g8 in range(8):
                    sl = slice(g8 * 128, (g8 + 1) * 128)
                    tt = v.scalar_tensor_tensor(knf8[:, sl], kf[:, sl], rstd8[:, g8:g8 + 1],
                                                nrm_b, ALU.mult, ALU.mult)
                tt.then_inc(knr, 1)
                if tc >= 2:
                    v.wait_ge(spd, B["spd"] + 16 * 16 * (tc - 1))
                for g8 in range(8):
                    v.wait_ge(trd, B["trd"] + tc * 8 + g8 + 1)
                    pbx = pbt[g8 % 2][:, 0:128]
                    hi_sl = sp_st[bb][:, (2 * g8) * 128:(2 * g8 + 1) * 128]
                    v.tensor_copy(hi_sl, pbx).then_inc(ted, 1)
                    v.tensor_tensor(sp_st[bb][:, (2 * g8 + 1) * 128:(2 * g8 + 2) * 128],
                                    pbx, hi_sl, ALU.subtract).then_inc(ted, 1)
                if tc % 8 == 7:
                    feed_prepass_v(s, v, 5)

        @block.scalar
        def _(sc):
            def spill(tc):
                bb = tc % 2
                if is_v:
                    sc.wait_ge(ksr, B["ksr"] + tc + 1)
                    sc.dma_start(out=s.vt_hi[tc * 128:(tc + 1) * 128, :],
                                 in_=vsp_hi[bb]).then_inc(spd, 16)
                    sc.dma_start(out=s.vt_lo[tc * 128:(tc + 1) * 128, :],
                                 in_=vsp_lo[bb]).then_inc(spd, 16)
                else:
                    sc.wait_ge(ted, B["ted"] + (tc + 1) * 16)
                    for g8 in range(8):
                        h = head_off + g8
                        sc.dma_start(out=dst_hi[h, tc, :, :],
                                     in_=sp_st[bb][:, (2 * g8) * 128:(2 * g8 + 1) * 128]
                                     ).then_inc(spd, 16)
                        sc.dma_start(out=dst_lo[h, tc, :, :],
                                     in_=sp_st[bb][:, (2 * g8 + 1) * 128:(2 * g8 + 2) * 128]
                                     ).then_inc(spd, 16)
            for tc in range(NTC):
                if not is_v:
                    sc.wait_ge(rs1, B["rs1"] + tc + 1)
                    sc.activation(rstd8, rv8, AF.Sqrt).then_inc(rs2, 1)
                if tc >= 1:
                    spill(tc - 1)
            spill(NTC - 1)

        @block.tensor
        def _(t):
            def emit_tr(tc):
                t.wait_ge(knr, B["knr"] + tc + 1)
                for g8 in range(8):
                    w = tc * 16 + 2 * g8
                    if w >= 2:
                        t.wait_ge(ted, B["ted"] + w - 2)
                    t.transpose(pbt[g8 % 2][:, 0:128],
                                knf8[:, g8 * 128:(g8 + 1) * 128],
                                s.ident_f32).then_inc(trd, 1)
            t.wait_ge(wsp, B["wsp"] + 16)
            for tc in range(NTC):
                tg, tcq = tc // 4, tc % 4
                t.wait_ge(lnld, B["lnld"] + 16 * 2 * (tg + 1))
                if tc >= 2:
                    t.wait_ge(pfree, B["pfree"] + tc - 1)
                pa = s.pb[2 * (tc % 2)]
                pa1 = s.pb[2 * (tc % 2) + 1]
                lh = lnb_hi[tg % 2]
                ll = lnb_lo[tg % 2]
                for fc in range(16):
                    sl = slice(fc * 512 + tcq * 128, fc * 512 + (tcq + 1) * 128)
                    st_hi = lh[:, sl]
                    st_lo = ll[:, sl]
                    first, last = fc == 0, fc == 15
                    t.matmul(pa, st_hi, w_hi[:, fc * 1024:fc * 1024 + 512],
                             start=first, stop=False)
                    t.matmul(pa1, st_hi, w_hi[:, fc * 1024 + 512:(fc + 1) * 1024],
                             start=first, stop=False)
                    t.matmul(pa, st_hi, w_lo[:, fc * 1024:fc * 1024 + 512],
                             start=False, stop=False)
                    t.matmul(pa1, st_hi, w_lo[:, fc * 1024 + 512:(fc + 1) * 1024],
                             start=False, stop=False)
                    t.matmul(pa, st_lo, w_hi[:, fc * 1024:fc * 1024 + 512],
                             start=False, stop=last)
                    mm = t.matmul(pa1, st_lo, w_hi[:, fc * 1024 + 512:(fc + 1) * 1024],
                                  start=False, stop=last)
                    if last:
                        mm.then_inc(pdone, 1)
                if not is_v and tc >= 1:
                    emit_tr(tc - 1)
            if not is_v:
                emit_tr(NTC - 1)

        @block.gpsimd
        def _(g):
            if not PP_NO_PRE:
                feed_prepass(s, g, 20 if not is_q else 10)
    ctx.close()
    s.pp_base["wld"] += 16 * 16
    s.pp_base["wsp"] += 16
    s.pp_base["lnld"] += 16 * 2 * NTG
    s.pp_base["pdone"] += NTC
    s.pp_base["pfree"] += NTC
    if is_v:
        s.pp_base["ksr"] += NTC
        s.pp_base["spd"] += 16 * 2 * NTC
    else:
        s.pp_base["rs1"] += NTC
        s.pp_base["rs2"] += NTC
        s.pp_base["knr"] += NTC
        s.pp_base["trd"] += 8 * NTC
        s.pp_base["ted"] += 16 * NTC
        s.pp_base["spd"] += 16 * 16 * NTC


def emit_wo_split(s):
    nc = s.nc
    wld = s.sem("wo_wld")
    wsp = s.sem("wo_wsp")
    spd = s.sem("wo_spd")
    ctx = contextlib.ExitStack()
    wtmp = [ctx.enter_context(nc.sbuf_tensor(f"wo_t{i}", [128, D], F32)).ap() for i in range(2)]
    whi = [ctx.enter_context(nc.sbuf_tensor(f"wo_h{i}", [128, D], BF16)).ap() for i in range(2)]
    wlo = [ctx.enter_context(nc.sbuf_tensor(f"wo_l{i}", [128, D], BF16)).ap() for i in range(2)]
    with nc.Block() as block:
        @block.sync
        def _(sy):
            for i in range(16):
                if i >= 2:
                    sy.wait_ge(wsp, i - 1)
                sy.dma_start(out=wtmp[i % 2], in_=s.woT[i * 128:(i + 1) * 128, :]).then_inc(wld, 16)
            sy.wait_ge(spd, 16 * 32)
        @block.vector
        def _(v):
            for i in range(16):
                b = i % 2
                v.wait_ge(wld, 16 * (i + 1))
                if i >= 2:
                    v.wait_ge(spd, 16 * 2 * (i - 1))
                v.tensor_copy(whi[b], wtmp[b])
                v.tensor_tensor(wlo[b], wtmp[b], whi[b], ALU.subtract).then_inc(wsp, 1)
            feed_prepass_v(s, v, len(s.pre_plan) - s.pre_nv)
        @block.scalar
        def _(sc):
            for i in range(16):
                b = i % 2
                sc.wait_ge(wsp, i + 1)
                sc.dma_start(out=s.wo_hi[i * 128:(i + 1) * 128, :], in_=whi[b]).then_inc(spd, 16)
                sc.dma_start(out=s.wo_lo[i * 128:(i + 1) * 128, :], in_=wlo[b]).then_inc(spd, 16)
        @block.gpsimd
        def _(g):
            feed_prepass(s, g, len(s.pre_plan) - s.pre_n)
    ctx.close()


def emit_m_finalize(s):
    nc = s.nc
    fin = s.sem("mfin")
    with nc.Block() as block:
        @block.tensor
        def _(t):
            t.wait_ge(s.pre_done, 1)
            t.matmul(s.pb[0][0:1, 0:1], s.ones_col, s.m_acc1, start=True, stop=True)
            t.matmul(s.pb[1][0:1, 0:1], s.ones_col, s.m_acc2,
                     start=True, stop=True).then_inc(fin, 1)
        @block.vector
        def _(v):
            v.wait_ge(fin, 1)
            v.tensor_scalar(s.m1c, s.pb[0][0:1, 0:1], 1.0 / (D * DH), EPS_Q, ALU.mult, ALU.max)
            v.tensor_scalar(s.m2c, s.pb[1][0:1, 0:1], 1.0 / (D * DH), EPS_Q, ALU.mult, ALU.max)
            v.drain().then_inc(fin, 1)
    with nc.Block() as block:
        @block.tensor
        def _(t):
            t.wait_ge(fin, 2)
            t.matmul(s.pb[0][:, 0:1], s.ones_row, s.m2c, start=True, stop=True)
            t.matmul(s.pb[1][:, 0:1], s.ones_row, s.m1c, start=True, stop=True).then_inc(fin, 1)
        @block.vector
        def _(v):
            v.wait_ge(fin, 3)
            v.tensor_copy(s.m2col, s.pb[0][:, 0:1])
            v.reciprocal(s.s1col, s.pb[1][:, 0:1])
            v.reciprocal(s.s2col, s.pb[0][:, 0:1])
            if DEBUG:
                v.drain()
                v.tensor_copy(s.dbg_st[0:1, 0:1], s.m1c)
                v.tensor_copy(s.dbg_st[0:1, 1:2], s.m2c)
                v.tensor_copy(s.dbg_st[0:1, 2:3], s.s1col[0:1, 0:1])
                v.drain().then_inc(fin, 1)
        @block.scalar
        def _(sc):
            if DEBUG:
                sc.wait_ge(fin, 4)
                sc.dma_start(out=s.dbg_m, in_=s.dbg_st[0:1, 0:4]).then_inc(fin, 16)
                sc.wait_ge(fin, 20)


def emit_p2(s):
    nc = s.nc

    kvld = s.sem("a_kvld")
    qld = s.sem("a_qld")
    ssem = s.sem("a_ssem")
    esem = s.sem("a_esem")
    usem = s.sem("a_usem")
    csem = s.sem("a_csem")
    tosem = s.sem("a_tosem")
    recsem = s.sem("a_recsem")
    recbsem = s.sem("a_recbsem")
    recbfree = s.sem("a_recbfree")
    aosem = s.sem("a_aosem")
    wold = s.sem("a_wold")
    ydone = s.sem("a_ydone")
    yev = s.sem("a_yev")
    ytr = s.sem("a_ytr")
    yevp = s.sem("a_yevp")
    ysp = s.sem("a_ysp")

    ctx = contextlib.ExitStack()
    sbuf = nc.sbuf_tensor
    K_hi = [ctx.enter_context(sbuf(f"a_khi{i}", [128, L], BF16)).ap() for i in range(2)]
    K_lo = [ctx.enter_context(sbuf(f"a_klo{i}", [128, L], BF16)).ap() for i in range(2)]
    V_hi = [ctx.enter_context(sbuf(f"a_vhi{i}", [128, L], BF16)).ap() for i in range(2)]
    V_lo = [ctx.enter_context(sbuf(f"a_vlo{i}", [128, L], BF16)).ap() for i in range(2)]
    Q_hi = [ctx.enter_context(sbuf(f"a_qhi{i}", [128, 512], BF16)).ap() for i in range(2)]
    Q_lo = [ctx.enter_context(sbuf(f"a_qlo{i}", [128, 512], BF16)).ap() for i in range(2)]
    U_f = [ctx.enter_context(sbuf(f"a_uf{i}", [128, 512], F32)).ap() for i in range(2)]
    U_hi = [ctx.enter_context(sbuf(f"a_uhi{i}", [128, 512], BF16)).ap() for i in range(2)]
    U_lo = [ctx.enter_context(sbuf(f"a_ulo{i}", [128, 512], BF16)).ap() for i in range(2)]
    mask_sb = ctx.enter_context(sbuf("a_mask", [128, 8 * 1024], BF16)).ap()
    rec_row = [ctx.enter_context(sbuf(f"a_rr{i}", [1, 512], F32)).ap() for i in range(2)]
    rec_b = ctx.enter_context(sbuf("a_recb", [128, 512], F32)).ap()
    ao_f = ctx.enter_context(sbuf("a_aof", [128, 512], F32)).ap()
    ao_hi = ctx.enter_context(sbuf("a_aohi", [128, NH * 512], BF16)).ap()
    ao_lo = ctx.enter_context(sbuf("a_aolo", [128, NH * 512], BF16)).ap()
    woc_hi = [ctx.enter_context(sbuf(f"a_wochi{i}", [128, D], BF16)).ap() for i in range(2)]
    woc_lo = [ctx.enter_context(sbuf(f"a_woclo{i}", [128, D], BF16)).ap() for i in range(2)]
    y_oc = ctx.enter_context(sbuf("a_yoc", [128, 512], F32)).ap()
    piece = [ctx.enter_context(sbuf(f"a_piece{i}", [128, 128], F32)).ap() for i in range(4)]

    psS = [s.pb[0], s.pb[1]]
    psO = s.pb[2]
    psSum = s.pb[3][0:1, :]
    psRB = s.pb[4]
    psY = [s.pb[4], s.pb[5]]
    psTR = [s.pb[6], s.pb[7]]

    with nc.Block() as block:
        @block.sync
        def _(sy):
            sy.dma_start(out=mask_sb, in_=s.mask_d.rearrange("c p q -> p c q")).then_inc(qld, 16)
            for qg in range(4):
                for g in range(8):
                    gi = qg * 8 + g
                    if gi >= 2:
                        gp = gi - 2
                        sy.wait_ge(recbsem, (gp // 8) * 16 + (gp % 8) * 2 + 2)
                    sy.dma_start(out=K_hi[g % 2],
                                 in_=s.kn_hi[g].rearrange("kc hd t -> hd kc t")).then_inc(kvld, 16)
                    sy.dma_start(out=K_lo[g % 2],
                                 in_=s.kn_lo[g].rearrange("kc hd t -> hd kc t")).then_inc(kvld, 16)
                    sy.dma_start(out=V_hi[g % 2],
                                 in_=s.vt_hi[:, g * 128:(g + 1) * 128].rearrange(
                                     "(kc p) hd -> p kc hd", p=128)).then_inc(kvld, 16)
                    sy.dma_start(out=V_lo[g % 2],
                                 in_=s.vt_lo[:, g * 128:(g + 1) * 128].rearrange(
                                     "(kc p) hd -> p kc hd", p=128)).then_inc(kvld, 16)
                    for hh in range(2):
                        U = qg * 16 + g * 2 + hh
                        h = g * 2 + hh
                        if U >= 2:
                            sy.wait_ge(tosem, U - 1)
                        sy.dma_start(out=Q_hi[U % 2],
                                     in_=s.qn_hi[h, qg * 4:(qg + 1) * 4].rearrange(
                                         "tc hd t -> hd tc t")).then_inc(qld, 16)
                        sy.dma_start(out=Q_lo[U % 2],
                                     in_=s.qn_lo[h, qg * 4:(qg + 1) * 4].rearrange(
                                         "tc hd t -> hd tc t")).then_inc(qld, 16)
                for oc in range(16):
                    oci = qg * 16 + oc
                    if oci >= 2:
                        sy.wait_ge(ydone, oci - 1)
                    sy.dma_start(out=woc_hi[oc % 2],
                                 in_=s.wo_hi[:, oc * 128:(oc + 1) * 128].rearrange(
                                     "(hc f) o -> f hc o", f=128)).then_inc(wold, 16)
                    sy.dma_start(out=woc_lo[oc % 2],
                                 in_=s.wo_lo[:, oc * 128:(oc + 1) * 128].rearrange(
                                     "(hc f) o -> f hc o", f=128)).then_inc(wold, 16)
            sy.wait_ge(ysp, 16 * 64 * 4)

        @block.tensor
        def _(t):
            def consume(U, kc, vh, vl):
                t.wait_ge(usem, U * 32 + kc + 1)
                b = kc % 2
                ks = slice(kc * 128, (kc + 1) * 128)
                t.matmul(psSum, s.ones_bf, U_hi[b], start=(kc == 0), stop=False)
                mmsum = t.matmul(psSum, s.ones_bf, U_lo[b], start=False, stop=(kc == 31))
                if kc == 31:
                    mmsum.then_inc(tosem, 1)
                t.matmul(psO, vh[:, ks], U_hi[b], start=(kc == 0), stop=False)
                t.matmul(psO, vh[:, ks], U_lo[b], start=False, stop=False)
                t.matmul(psO, vl[:, ks], U_hi[b],
                         start=False, stop=(kc == 31)).then_inc(csem, 1)

            def ytrans(qg, oc):
                t.wait_ge(yev, qg * 16 + oc + 1)
                for pc in range(4):
                    n = (qg * 16 + oc) * 4 + pc
                    if n >= 2:
                        t.wait_ge(yevp, n - 1)
                    t.transpose(psTR[pc % 2][:, 0:128],
                                y_oc[:, pc * 128:(pc + 1) * 128], s.ident_f32).then_inc(ytr, 1)

            for qg in range(4):
                for g in range(8):
                    gi = qg * 8 + g
                    t.wait_ge(kvld, 16 * 4 * (gi + 1))
                    for hh in range(2):
                        u = g * 2 + hh
                        U = qg * 16 + u
                        t.wait_ge(qld, 16 * (2 * (U + 1) + 1))
                        t.wait_ge(aosem, U)
                        t.wait_ge(recsem, U)
                        kh, kl = K_hi[g % 2], K_lo[g % 2]
                        vh, vl = V_hi[g % 2], V_lo[g % 2]
                        qh, ql = Q_hi[U % 2], Q_lo[U % 2]
                        for kc in range(32):
                            pS = psS[kc % 2]
                            t.wait_ge(esem, max(0, U * 32 + kc - 1))
                            ks = slice(kc * 128, (kc + 1) * 128)
                            masked = (8 <= kc < 16) and qg >= 2
                            t.matmul(pS, kh[:, ks], qh, start=True, stop=False)
                            t.matmul(pS, kh[:, ks], ql, start=False, stop=False)
                            mm3 = t.matmul(pS, kl[:, ks], qh, start=False, stop=not masked)
                            if masked:
                                t.matmul(pS, s.ident_bf,
                                         mask_sb[:, (kc - 8) * 1024 + (qg - 2) * 512:
                                                 (kc - 8) * 1024 + (qg - 2) * 512 + 512],
                                         start=False, stop=True).then_inc(ssem, 1)
                            else:
                                mm3.then_inc(ssem, 1)
                            if kc >= 1:
                                consume(U, kc - 1, vh, vl)
                        consume(U, 31, vh, vl)
                        t.wait_ge(recsem, U + 1)
                        if U >= 1:
                            t.wait_ge(recbfree, U)
                        if u == 0:
                            t.wait_ge(yev, U)
                        t.matmul(psRB, s.ones_row, rec_row[U % 2],
                                 start=True, stop=True).then_inc(recbsem, 1)
                t.wait_ge(aosem, qg * 16 + 16)
                for oc in range(16):
                    oci = qg * 16 + oc
                    t.wait_ge(wold, 16 * 2 * (oci + 1))
                    if oc >= 2:
                        t.wait_ge(yev, qg * 16 + oc - 1)
                    pY = psY[oc % 2]
                    wh, wl = woc_hi[oc % 2], woc_lo[oc % 2]
                    for h in range(NH):
                        ah = ao_hi[:, h * 512:(h + 1) * 512]
                        al = ao_lo[:, h * 512:(h + 1) * 512]
                        wt_h = wh[:, h * 128:(h + 1) * 128]
                        wt_l = wl[:, h * 128:(h + 1) * 128]
                        t.matmul(pY, wt_h, ah, start=(h == 0), stop=False)
                        t.matmul(pY, wt_h, al, start=False, stop=False)
                        mmy = t.matmul(pY, wt_l, ah, start=False, stop=(h == NH - 1))
                    mmy.then_inc(ydone, 1)
                    if oc >= 1:
                        ytrans(qg, oc - 1)
                ytrans(qg, 15)

        @block.scalar
        def _(sc):
            for U in range(64):
                for kc in range(32):
                    sc.wait_ge(ssem, U * 32 + kc + 1)
                    sc.wait_ge(usem, max(0, U * 32 + kc - 1))
                    sc.activation(U_f[kc % 2], psS[kc % 2], AF.Exp, scale=SCALE).then_inc(esem, 1)

        @block.vector
        def _(v):
            for qg in range(4):
                for u in range(16):
                    U = qg * 16 + u
                    if u == 0 and qg >= 1:
                        v.wait_ge(ydone, qg * 16)
                    for kc in range(32):
                        b = kc % 2
                        v.wait_ge(esem, U * 32 + kc + 1)
                        v.wait_ge(csem, max(0, U * 32 + kc - 1))
                        v.tensor_copy(U_hi[b], U_f[b])
                        v.tensor_tensor(U_lo[b], U_f[b], U_hi[b], ALU.subtract).then_inc(usem, 1)
                    v.wait_ge(tosem, U + 1)
                    v.reciprocal(rec_row[U % 2], psSum).then_inc(recsem, 1)
                    v.wait_ge(recbsem, U + 1)
                    v.tensor_copy(rec_b, psRB).then_inc(recbfree, 1)
                    v.wait_ge(csem, U * 32 + 32)
                    v.tensor_tensor(ao_f, psO, rec_b, ALU.mult)
                    v.tensor_copy(ao_hi[:, u * 512:(u + 1) * 512], ao_f)
                    v.tensor_tensor(ao_lo[:, u * 512:(u + 1) * 512], ao_f,
                                    ao_hi[:, u * 512:(u + 1) * 512], ALU.subtract).then_inc(aosem, 1)
                for oc in range(16):
                    v.wait_ge(ydone, qg * 16 + oc + 1)
                    v.tensor_copy(y_oc, psY[oc % 2]).then_inc(yev, 1)
                    for pc in range(4):
                        n = (qg * 16 + oc) * 4 + pc
                        v.wait_ge(ytr, n + 1)
                        if n >= 4:
                            v.wait_ge(ysp, 16 * (n - 3))
                        v.tensor_copy(piece[pc],
                                      psTR[pc % 2][:, 0:128]).then_inc(yevp, 1)

        @block.gpsimd
        def _(g):
            for qg in range(4):
                for oc in range(16):
                    for pc in range(4):
                        n = (qg * 16 + oc) * 4 + pc
                        g.wait_ge(yevp, n + 1)
                        tc = qg * 4 + pc
                        g.dma_start(out=s.y_sc[tc * 128:(tc + 1) * 128, oc * 128:(oc + 1) * 128],
                                    in_=piece[pc]).then_inc(ysp, 16)
            g.wait_ge(ysp, 16 * 64 * 4)
    ctx.close()


def emit_p3(s):
    nc = s.nc
    yld = s.sem("p3_yld")
    x2r = s.sem("p3_x2r")
    rs1 = s.sem("p3_rs1")
    rs2 = s.sem("p3_rs2")
    er = s.sem("p3_er")
    sg = s.sem("p3_sg")
    xqr = s.sem("p3_xqr")
    trd = s.sem("p3_trd")
    ted = s.sem("p3_ted")
    spd = s.sem("p3_spd")

    ctx = contextlib.ExitStack()
    sbuf = nc.sbuf_tensor
    y_st = [ctx.enter_context(sbuf(f"p3_y{i}", [128, D], F32)).ap() for i in range(2)]
    x_st = [ctx.enter_context(sbuf(f"p3_x{i}", [128, D], F32)).ap() for i in range(2)]
    x2_st = [ctx.enter_context(sbuf(f"p3_x2{i}", [128, D], F32)).ap() for i in range(2)]
    ln2 = ctx.enter_context(sbuf("p3_ln2", [128, D], F32)).ap()
    z_st = ctx.enter_context(sbuf("p3_z", [128, D], F32)).ap()
    sti = ctx.enter_context(sbuf("p3_sti", [128, D], I32)).ap()
    e_st = ctx.enter_context(sbuf("p3_e", [128, D], F32)).ap()
    sg1 = ctx.enter_context(sbuf("p3_sg1", [128, D], BF16)).ap()
    sg2 = ctx.enter_context(sbuf("p3_sg2", [128, D], BF16)).ap()
    cs_st = ctx.enter_context(sbuf("p3_cs", [128, D], BF16)).ap()
    xq_st = [ctx.enter_context(sbuf(f"p3_xq{i}", [128, D], F32)).ap() for i in range(2)]
    xsp = [ctx.enter_context(sbuf(f"p3_xsp{i}", [128, D], BF16)).ap() for i in range(2)]
    xsq = ctx.enter_context(sbuf("p3_xsq", [128, D], F32)).ap()
    bn2 = ctx.enter_context(sbuf("p3_bn2", [128, 2], F32)).ap()
    q2c = ctx.enter_context(sbuf("p3_q2c", [128, 1], F32)).ap()
    col = ctx.enter_context(sbuf("p3_col", [128, 1], F32)).ap()
    rstd = ctx.enter_context(sbuf("p3_rstd", [128, 1], F32)).ap()
    amx = ctx.enter_context(sbuf("p3_amx", [128, 1], F32)).ap()
    scol = ctx.enter_context(sbuf("p3_scol", [128, 1], F32)).ap()
    pbt = [s.pb[2], s.pb[3]]

    with nc.Block() as block:
        @block.sync
        def _(sy):
            for tc in range(16):
                if tc >= 2:
                    sy.wait_ge(x2r, tc - 1)
                sy.dma_start(out=y_st[tc % 2],
                             in_=s.y_sc[tc * 128:(tc + 1) * 128, :]).then_inc(yld, 16)
                sy.dma_start(out=x_st[tc % 2],
                             in_=s.x_own[tc * 128:(tc + 1) * 128, :]).then_inc(yld, 16)
            sy.wait_ge(spd, 16 * 17 * 16)

        @block.vector
        def _(v):
            for tc in range(16):
                b = tc % 2
                v.wait_ge(yld, 16 * 2 * (tc + 1))
                if tc >= 2:
                    v.wait_ge(spd, 16 * 17 * (tc - 1))
                v.tensor_tensor(x2_st[b], y_st[b], x_st[b], ALU.add).then_inc(x2r, 1)
                v.tensor_reduce(bn2[:, 0:1], x2_st[b], AX.X, ALU.add)
                v.tensor_tensor(xsq, x2_st[b], x2_st[b], ALU.mult)
                v.tensor_reduce(q2c, xsq, AX.X, ALU.add)
                v.drain()
                v.tensor_scalar(bn2[:, 0:1], bn2[:, 0:1], 1.0 / D, None, ALU.mult)
                v.drain()
                v.tensor_tensor(col, bn2[:, 0:1], bn2[:, 0:1], ALU.mult)
                v.drain()
                v.scalar_tensor_tensor(col, q2c, 1.0 / D, col, ALU.mult, ALU.subtract)
                v.drain()
                v.tensor_scalar(col, col, EPS_LN, None, ALU.add)
                v.drain()
                v.reciprocal(col, col).then_inc(rs1, 1)
                v.wait_ge(rs2, tc + 1)
                v.tensor_scalar(ln2, x2_st[b], bn2[:, 0:1], None, ALU.subtract)
                v.tensor_scalar(ln2, ln2, rstd, None, ALU.mult)
                v.tensor_reduce(amx, ln2, AX.X, ALU.max, apply_absolute_value=True)
                v.drain()
                v.tensor_scalar(amx, amx, EPS_Q, None, ALU.max)
                v.drain()
                v.reciprocal(scol, amx)
                v.drain()
                v.tensor_scalar(scol, scol, 127.0, None, ALU.mult)
                v.drain()
                v.tensor_scalar(z_st, ln2, scol, None, ALU.mult)
                v.tensor_copy(sti, z_st)
                v.tensor_tensor(e_st, z_st, sti, ALU.subtract).then_inc(er, 1)
                v.wait_ge(sg, tc + 1)
                v.tensor_tensor(cs_st, sg1, sg2, ALU.add)
                v.scalar_tensor_tensor(xq_st[b], cs_st, 0.5, sti, ALU.mult, ALU.add).then_inc(xqr, 1)
                for fc in range(16):
                    n = tc * 17 + fc
                    v.wait_ge(trd, n + 1)
                    v.tensor_copy(xsp[b][:, fc * 128:(fc + 1) * 128],
                                  pbt[fc % 2][:, 0:128]).then_inc(ted, 1)
                v.wait_ge(trd, tc * 17 + 17)
                v.tensor_copy(s.amax1_row[0:1, tc * 128:(tc + 1) * 128],
                              s.pb[1][0:1, 0:128]).then_inc(ted, 1)

        @block.scalar
        def _(sc):
            for tc in range(16):
                sc.wait_ge(rs1, tc + 1)
                sc.activation(rstd, col, AF.Sqrt).then_inc(rs2, 1)
                sc.wait_ge(er, tc + 1)
                sc.sign(sg1, e_st, bias=s.nhalfc)
                sc.sign(sg2, e_st, bias=s.halfc).then_inc(sg, 1)

        @block.tensor
        def _(t):
            for tc in range(16):
                t.wait_ge(xqr, tc + 1)
                for fc in range(16):
                    n = tc * 17 + fc
                    if n >= 2:
                        t.wait_ge(ted, n - 1)
                    t.transpose(pbt[fc % 2][:, 0:128],
                                xq_st[tc % 2][:, fc * 128:(fc + 1) * 128],
                                s.ident_f32).then_inc(trd, 1)
                if tc >= 1:
                    t.wait_ge(ted, tc * 17)
                t.transpose(s.pb[1][0:1, 0:128], amx, s.ident_f32).then_inc(trd, 1)

        @block.gpsimd
        def _(g):
            for tc in range(16):
                b = tc % 2
                g.wait_ge(x2r, tc + 1)
                g.dma_start(out=s.x2_sc[tc * 128:(tc + 1) * 128, :],
                            in_=x2_st[b]).then_inc(spd, 16)
                g.wait_ge(ted, tc * 17 + 16)
                for fc in range(16):
                    g.dma_start(out=s.xq_sc[fc, :, tc * 128:(tc + 1) * 128],
                                in_=xsp[b][:, fc * 128:(fc + 1) * 128]).then_inc(spd, 16)
            g.wait_ge(spd, 16 * 17 * 16)
    ctx.close()


def emit_c_rows(s):
    nc = s.nc
    finv = s.sem("cr_finv")
    fint = s.sem("cr_fint")
    with nc.Block() as block:
        @block.vector
        def _(v):
            v.tensor_scalar(s.amax1_row, s.amax1_row, s.m1c, 1.0 / 254.0, ALU.mult, ALU.mult)
            v.drain()
            v.drain().then_inc(finv, 1)
        @block.tensor
        def _(t):
            t.wait_ge(finv, 1)
            for i in range(4):
                t.matmul(s.pb[i][:, 0:512], s.ones_row,
                         s.amax1_row[0:1, i * 512:(i + 1) * 512], start=True, stop=True)
            t.drain().then_inc(fint, 1)
    with nc.Block() as block:
        @block.vector
        def _(v):
            v.wait_ge(fint, 1)
            for i in range(4):
                v.tensor_copy(s.c1_b[:, i * 512:(i + 1) * 512], s.pb[i][:, 0:512])


def emit_ffn_a(s):
    nc = s.nc
    xqld = s.sem("fa_xqld")
    xqld2 = s.sem("fa_xqld2")
    w1ld = s.sem("fa_w1ld")
    w2ld = s.sem("fa_w2ld")
    sg1d = s.sem("fa_sg1d")
    sg2d = s.sem("fa_sg2d")
    t1r = s.sem("fa_t1r")
    t2r = s.sem("fa_t2r")
    mmd = s.sem("fa_mmd")
    hev = s.sem("fa_hev")
    gel = s.sem("fa_gel")
    mxd = s.sem("fa_mxd")
    gsp = s.sem("fa_gsp")
    t2sp = s.sem("fa_t2sp")
    amp = s.sem("fa_amp")
    amc = s.sem("fa_amc")

    ctx = contextlib.ExitStack()
    sbuf = nc.sbuf_tensor
    xqT = ctx.enter_context(sbuf("fa_xqT", [128, 16 * T], BF16)).ap()
    w1t = [ctx.enter_context(sbuf(f"fa_w1t{i}", [128, D], F32)).ap() for i in range(2)]
    w2t = ctx.enter_context(sbuf("fa_w2t", [128, D], F32)).ap()
    sa1 = ctx.enter_context(sbuf("fa_sa1", [128, D], BF16)).ap()
    sb1 = ctx.enter_context(sbuf("fa_sb1", [128, D], BF16)).ap()
    sa2 = ctx.enter_context(sbuf("fa_sa2", [128, D], BF16)).ap()
    sb2 = ctx.enter_context(sbuf("fa_sb2", [128, D], BF16)).ap()
    t1 = [ctx.enter_context(sbuf(f"fa_t1_{i}", [128, D], BF16)).ap() for i in range(2)]
    t2 = [ctx.enter_context(sbuf(f"fa_t2_{i}", [128, D], BF16)).ap() for i in range(2)]
    h_st = [ctx.enter_context(sbuf(f"fa_h{i}", [128, T], F32)).ap() for i in range(2)]
    g_st = [ctx.enter_context(sbuf(f"fa_g{i}", [128, T], F32)).ap() for i in range(2)]
    mx = ctx.enter_context(sbuf("fa_mx", [128, T], F32)).ap()
    mn = ctx.enter_context(sbuf("fa_mn", [128, T], F32)).ap()
    NDH = 64

    with nc.Block() as block:
        @block.sync
        def _(sy):
            for dh in range(2):
                sy.dma_start(out=w1t[dh],
                             in_=s.w1T[:, dh * 128:(dh + 1) * 128].rearrange(
                                 "(fc f) o -> f fc o", f=128)).then_inc(w1ld, 16)
            for fc in range(8, 16):
                sy.dma_start(out=xqT[:, fc * T:(fc + 1) * T],
                             in_=s.xq_sc[fc]).then_inc(xqld, 16)
            for dh in range(2, NDH):
                sy.wait_ge(sg1d, dh - 1)
                sy.dma_start(out=w1t[dh % 2],
                             in_=s.w1T[:, dh * 128:(dh + 1) * 128].rearrange(
                                 "(fc f) o -> f fc o", f=128)).then_inc(w1ld, 16)
            sy.wait_ge(gsp, 16 * NDH)
            sy.wait_ge(t2sp, 16 * NDH)

        @block.scalar
        def _(sc):
            sc.wait_ge(s.sems["mfin"], 3)
            sc.dma_start(out=w2t, in_=s.w2T[0:128, :]).then_inc(w2ld, 16)
            for dh in range(NDH):
                b = dh % 2
                sc.wait_ge(w1ld, 16 * (dh + 1))
                if dh >= 1:
                    sc.wait_ge(t1r, dh)
                sc.activation(sa1, w1t[b], AF.Sign, bias=s.nhalfc, scale=s.s1col)
                sc.activation(sb1, w1t[b], AF.Sign,
                              bias=s.halfc, scale=s.s1col).then_inc(sg1d, 1)
                sc.wait_ge(w2ld, 16 * (dh + 1))
                if dh >= 1:
                    sc.wait_ge(t2r, dh)
                sc.activation(sa2, w2t, AF.Sign, bias=s.nhalfc, scale=s.s2col)
                sc.activation(sb2, w2t, AF.Sign,
                              bias=s.halfc, scale=s.s2col).then_inc(sg2d, 1)
                if dh + 1 < NDH:
                    sc.wait_ge(sg2d, dh + 1)
                    sc.dma_start(out=w2t,
                                 in_=s.w2T[(dh + 1) * 128:(dh + 2) * 128, :]
                                 ).then_inc(w2ld, 16)
                sc.wait_ge(hev, dh + 1)
                if dh >= 2:
                    sc.wait_ge(gsp, 16 * (dh - 1))
                sc.activation(g_st[b], h_st[b], AF.Gelu).then_inc(gel, 1)

        @block.vector
        def _(v):
            for dh in range(NDH):
                b = dh % 2
                v.wait_ge(sg1d, dh + 1)
                v.tensor_tensor(t1[b], sa1, sb1, ALU.add).then_inc(t1r, 1)
                v.wait_ge(sg2d, dh + 1)
                if dh >= 2:
                    v.wait_ge(t2sp, 16 * (dh - 1))
                v.tensor_tensor(t2[b], sa2, sb2, ALU.add).then_inc(t2r, 1)
                v.wait_ge(mmd, dh + 1)
                pset = 4 * (dh % 2)
                for i in range(4):
                    tth = v.tensor_tensor(h_st[b][:, i * 512:(i + 1) * 512],
                                          s.pb[pset + i], s.c1_b[:, i * 512:(i + 1) * 512],
                                          ALU.mult)
                tth.then_inc(hev, 1)
                if dh >= 1:
                    v.wait_ge(gel, dh)
                    gp = g_st[(dh - 1) % 2]
                    if dh == 1:
                        v.tensor_copy(mx, gp)
                        v.tensor_copy(mn, gp).then_inc(mxd, 1)
                    else:
                        v.tensor_tensor(mx, mx, gp, ALU.max)
                        v.tensor_tensor(mn, mn, gp, ALU.min).then_inc(mxd, 1)
            v.wait_ge(gel, NDH)
            v.tensor_tensor(mx, mx, g_st[(NDH - 1) % 2], ALU.max)
            v.tensor_tensor(mn, mn, g_st[(NDH - 1) % 2], ALU.min).then_inc(mxd, 1)
            v.tensor_scalar(mn, mn, -1.0, None, ALU.mult)
            v.tensor_tensor(mx, mx, mn, ALU.max)
            v.drain().then_inc(amc, 1)
            v.wait_ge(amp, 1)
            for i in range(16):
                v.tensor_reduce(s.am2t[:, i:i + 1],
                                s.pb[i // 4][:, (i % 4) * 128:(i % 4 + 1) * 128],
                                AX.X, ALU.max)
            v.drain()
            v.tensor_scalar(s.am2t, s.am2t, EPS_Q, None, ALU.max)
            v.drain()
            v.tensor_tensor(s.c2_cols, s.am2t, s.m2col.broadcast_to([128, 16]), ALU.mult)
            v.drain()
            v.tensor_scalar(s.c2_cols, s.c2_cols, 1.0 / 254.0, None, ALU.mult)
            v.drain()

        @block.tensor
        def _(t):
            t.wait_ge(xqld, 16 * 8)
            t.wait_ge(xqld2, 16 * 8)
            for dh in range(NDH):
                b = dh % 2
                t.wait_ge(t1r, dh + 1)
                if dh >= 2:
                    t.wait_ge(hev, dh - 1)
                pset = 4 * (dh % 2)
                for fc in range(16):
                    for i in range(4):
                        mmf = t.matmul(s.pb[pset + i], t1[b][:, fc * 128:(fc + 1) * 128],
                                       xqT[:, fc * T + i * 512: fc * T + (i + 1) * 512],
                                       start=(fc == 0), stop=(fc == 15))
                mmf.then_inc(mmd, 1)
            t.wait_ge(amc, 1)
            for i in range(16):
                t.transpose(s.pb[i // 4][:, (i % 4) * 128:(i % 4 + 1) * 128],
                            mx[:, i * 128:(i + 1) * 128], s.ident_f32)
            t.drain().then_inc(amp, 1)

        @block.gpsimd
        def _(g):
            for fc in range(8):
                g.dma_start(out=xqT[:, fc * T:(fc + 1) * T],
                            in_=s.xq_sc[fc]).then_inc(xqld2, 16)
            for dh in range(NDH):
                b = dh % 2
                g.wait_ge(t2r, dh + 1)
                g.dma_start(out=s.t2_sc[dh * 128:(dh + 1) * 128, :], in_=t2[b]).then_inc(t2sp, 16)
                g.wait_ge(mxd, dh + 1)
                g.dma_start(out=s.g_sc[dh * 128:(dh + 1) * 128, :], in_=g_st[b]).then_inc(gsp, 16)
            g.wait_ge(gsp, 16 * NDH)
            g.wait_ge(t2sp, 16 * NDH)
    ctx.close()

    fpp = s.sem("fa_fpp")
    fpc = s.sem("fa_fpc")
    am16 = s.qnw_b[0:16, 0:128]
    with nc.Block() as block:
        @block.tensor
        def _(t):
            t.transpose(s.pb[2][0:16, 0:128], s.am2t, s.ident_f32).then_inc(fpp, 1)
        @block.vector
        def _(v):
            v.wait_ge(fpp, 1)
            v.tensor_copy(am16, s.pb[2][0:16, 0:128]).then_inc(fpc, 1)
        @block.scalar
        def _(sc):
            sc.wait_ge(fpc, 1)
            sc.dma_start(out=s.amax2_row,
                         in_=am16).then_inc(fpc, 16)
            sc.wait_ge(fpc, 17)
    f2v = s.sem("fa_f2v")
    f2t = s.sem("fa_f2t")
    with nc.Block() as block:
        @block.vector
        def _(v):
            v.reciprocal(s.amax2_row, s.amax2_row)
            v.drain()
            v.tensor_scalar(s.amax2_row, s.amax2_row, 127.0, None, ALU.mult)
            v.drain()
            v.drain().then_inc(f2v, 1)
        @block.tensor
        def _(t):
            t.wait_ge(f2v, 1)
            for i in range(4):
                t.matmul(s.pb[i][:, 0:512], s.ones_row,
                         s.amax2_row[0:1, i * 512:(i + 1) * 512], start=True, stop=True)
            t.drain().then_inc(f2t, 1)
    with nc.Block() as block:
        @block.vector
        def _(v):
            v.wait_ge(f2t, 1)
            for i in range(4):
                v.tensor_copy(s.s2_b[:, i * 512:(i + 1) * 512], s.pb[i][:, 0:512])


def emit_ffn_b(s):
    nc = s.nc
    gld = s.sem("fb_gld")
    zr = s.sem("fb_zr")
    sgd = s.sem("fb_sgd")
    hqr = s.sem("fb_hqr")
    hqsp = s.sem("fb_hqsp")
    t2ld = s.sem("fb_t2ld")
    hqld = s.sem("fb_hqld")
    x2ld = s.sem("fb_x2ld")
    mmh = s.sem("fb_mmh")
    yev = s.sem("fb_yev")
    yout = s.sem("fb_yout")
    NDH = 64
    NJ = 256

    ctx = contextlib.ExitStack()
    sbuf = nc.sbuf_tensor
    g_ld = [ctx.enter_context(sbuf(f"fb_g{i}", [128, 512], F32)).ap() for i in range(2)]
    z_st = ctx.enter_context(sbuf("fb_z", [128, 512], F32)).ap()
    sti = ctx.enter_context(sbuf("fb_sti", [128, 512], I32)).ap()
    e_st = ctx.enter_context(sbuf("fb_e", [128, 512], F32)).ap()
    sg1 = ctx.enter_context(sbuf("fb_sg1", [128, 512], BF16)).ap()
    sg2 = ctx.enter_context(sbuf("fb_sg2", [128, 512], BF16)).ap()
    cs_st = ctx.enter_context(sbuf("fb_cs", [128, 512], BF16)).ap()
    hq_st = [ctx.enter_context(sbuf(f"fb_hq{i}", [128, 512], BF16)).ap() for i in range(2)]
    T2 = ctx.enter_context(sbuf("fb_T2", [128, 64 * 512], BF16)).ap()
    hqb = [ctx.enter_context(sbuf(f"fb_hqb{i}", [128, 32 * 512], BF16)).ap() for i in range(2)]
    x2s = [ctx.enter_context(sbuf(f"fb_x2s{i}", [128, 4 * 512], F32)).ap() for i in range(2)]
    y_st = [ctx.enter_context(sbuf(f"fb_y{i}", [128, 4 * 512], F32)).ap() for i in range(2)]
    EV_AT = {96: 0, 152: 1, 218: 2}

    with nc.Block() as block:
        @block.sync
        def _(sy):
            for j in range(NJ):
                tq, dh = j // 64, j % 64
                if j >= 2:
                    sy.wait_ge(zr, j - 1)
                sy.dma_start(out=g_ld[j % 2],
                             in_=s.g_sc[dh * 128:(dh + 1) * 128,
                                        tq * 512:(tq + 1) * 512]).then_inc(gld, 16)
            sy.wait_ge(yout, 16 * 64)

        @block.scalar
        def _(sc):
            def x2_load(n):
                ocg, tcb = n // 4, n % 4
                if n >= 2:
                    sc.wait_ge(yev, 4 * (n - 1))
                for tci in range(4):
                    tc = tcb * 4 + tci
                    sc.dma_start(out=x2s[n % 2][:, tci * 512:(tci + 1) * 512],
                                 in_=s.x2_sc[tc * 128:(tc + 1) * 128,
                                             ocg * 512:(ocg + 1) * 512]).then_inc(x2ld, 16)
            X2_AT = {130: 2, 200: 3}
            x2_load(0)
            x2_load(1)
            for j in range(NJ):
                sc.wait_ge(zr, j + 1)
                sc.sign(sg1, e_st, bias=s.nhalfc)
                sc.sign(sg2, e_st, bias=s.halfc).then_inc(sgd, 1)
                if j in X2_AT:
                    x2_load(X2_AT[j])
            for n in range(4, 16):
                x2_load(n)

        @block.gpsimd
        def _(g):
            def hq_halves(n):
                tcb = n % 4
                srcq = s.hq_sc[:, tcb * 512:(tcb + 1) * 512].rearrange(
                    "(dc p) o -> p dc o", p=128)
                for H in range(2):
                    gidx = 2 * n + H
                    if gidx >= 2:
                        g.wait_ge(mmh, gidx - 1)
                    for part in range(4):
                        dc0 = H * 32 + part * 8
                        g.dma_start(out=hqb[H][:, part * 8 * 512:(part + 1) * 8 * 512],
                                    in_=srcq[:, dc0:dc0 + 8, :]).then_inc(hqld, 16)
            def t2_parts(ocg):
                src = s.t2_sc[:, ocg * 512:(ocg + 1) * 512].rearrange(
                    "(dc p) o -> p dc o", p=128)
                for part in range(8):
                    g.dma_start(out=T2[:, part * 8 * 512:(part + 1) * 8 * 512],
                                in_=src[:, part * 8:(part + 1) * 8, :]).then_inc(t2ld, 16)
            def y_out(n):
                ocg, tcb = n // 4, n % 4
                g.wait_ge(yev, 4 * (n + 1))
                for tci in range(4):
                    tc = tcb * 4 + tci
                    g.dma_start(out=s.y[tc * 128:(tc + 1) * 128,
                                        ocg * 512:(ocg + 1) * 512],
                                in_=y_st[n % 2][:, tci * 512:(tci + 1) * 512]).then_inc(yout, 16)

            t2_parts(0)
            for tq in range(4):
                for dh in range(NDH):
                    j = tq * 64 + dh
                    g.wait_ge(hqr, j + 1)
                    g.dma_start(out=s.hq_sc[dh * 128:(dh + 1) * 128,
                                            tq * 512:(tq + 1) * 512],
                                in_=hq_st[j % 2]).then_inc(hqsp, 16)
                g.wait_ge(hqsp, 16 * 64 * (tq + 1))
                hq_halves(tq)
                if tq >= 1:
                    y_out(tq - 1)
            for n in range(3, 16):
                ocg, tcb = n // 4, n % 4
                if tcb == 0:
                    g.wait_ge(mmh, 8 * ocg)
                    t2_parts(ocg)
                if n > 3:
                    hq_halves(n)
                y_out(n - 1 if n > 3 else 3)
            y_out(15)
            g.wait_ge(yout, 16 * 64)

        @block.tensor
        def _(t):
            for n in range(16):
                ocg, tcb = n // 4, n % 4
                pset = 4 * (tcb % 2)
                t.wait_ge(t2ld, 16 * 8 * (ocg + 1))
                if n >= 2:
                    t.wait_ge(yev, 4 * (n - 1))
                for H in range(2):
                    t.wait_ge(hqld, 16 * 4 * (2 * n + H + 1))
                    for dq in range(32):
                        dh = H * 32 + dq
                        for tci in range(4):
                            mmb = t.matmul(s.pb[pset + tci],
                                           hqb[H][:, dq * 512 + tci * 128:
                                                 dq * 512 + (tci + 1) * 128],
                                           T2[:, dh * 512:(dh + 1) * 512],
                                           start=(dh == 0), stop=(dh == NDH - 1))
                    mmb.then_inc(mmh, 1)

        @block.vector
        def _(v):
            def evac(n):
                tcb = n % 4
                pset = 4 * (tcb % 2)
                v.wait_ge(mmh, 2 * (n + 1))
                v.wait_ge(x2ld, 16 * 4 * (n + 1))
                if n >= 2:
                    v.wait_ge(yout, 16 * 4 * (n - 1))
                for tci in range(4):
                    tc = tcb * 4 + tci
                    v.scalar_tensor_tensor(y_st[n % 2][:, tci * 512:(tci + 1) * 512],
                                           s.pb[pset + tci], s.c2_cols[:, tc:tc + 1],
                                           x2s[n % 2][:, tci * 512:(tci + 1) * 512],
                                           ALU.mult, ALU.add).then_inc(yev, 1)

            nev = 0
            for j in range(NJ):
                tq = j // 64
                v.wait_ge(gld, 16 * (j + 1))
                v.tensor_tensor(z_st, g_ld[j % 2],
                                s.s2_b[:, tq * 512:(tq + 1) * 512], ALU.mult)
                v.tensor_copy(sti, z_st)
                v.tensor_tensor(e_st, z_st, sti, ALU.subtract).then_inc(zr, 1)
                v.wait_ge(sgd, j + 1)
                v.tensor_tensor(cs_st, sg1, sg2, ALU.add)
                if j >= 2:
                    v.wait_ge(hqsp, 16 * (j - 1))
                v.scalar_tensor_tensor(hq_st[j % 2], cs_st, 0.5, sti,
                                       ALU.mult, ALU.add).then_inc(hqr, 1)
                if j in EV_AT:
                    evac(EV_AT[j])
                    nev += 1
            for n in range(nev, 16):
                evac(n)
    ctx.close()


# version 05 (final)
# speedup vs baseline: 1.1115x; 1.0468x over previous
import contextlib
import numpy as np
import ml_dtypes
import concourse.bass as bass
import concourse.mybir as mybir
from concourse.bass_utils import run_bass_kernel_spmd

F32 = mybir.dt.float32
BF16 = mybir.dt.bfloat16
I32 = mybir.dt.int32
AF = mybir.ActivationFunctionType
ALU = mybir.AluOpType
AX = mybir.AxisListType

D = 2048
L = 4096
T = 2048
NH = 16
NKV = 8
HD = 128
DH = 8192
SCALE = float(HD) ** -0.5
EPS_LN = 1e-5
EPS_RMS = 1e-6
EPS_Q = 1e-5
NEG = -1.0e30

DEBUG = False
STOP_AFTER = None
RACE_DETECT = False
PP_LIMIT = None
PP_NO_PRE = False
PP_ORDER = ("k", "v", "qa", "qb")


def _bf(x):
    return np.asarray(x).astype(ml_dtypes.bfloat16)


def make_mask(h):
    m = np.zeros((8, 128, 1024), dtype=np.float32)
    if h == 1:
        for c in range(8):
            kk = c * 128 + np.arange(128)[:, None]
            qq = np.arange(1024)[None, :]
            m[c] = np.where(kk <= qq, NEG, 0.0)
    return _bf(m)


def kernel(L0_attn_hidden_states, L0_attn_Wq, L0_attn_Wk, L0_attn_Wv, L0_attn_Wo,
           L0_attn_q_norm_w, L0_attn_k_norm_w, L0_ffn_w1, L0_ffn_w2):
    x = np.ascontiguousarray(np.asarray(L0_attn_hidden_states, dtype=np.float32))
    wqT = np.ascontiguousarray(np.asarray(L0_attn_Wq, dtype=np.float32).T)
    wkT = np.ascontiguousarray(np.asarray(L0_attn_Wk, dtype=np.float32).T)
    wvT = np.ascontiguousarray(np.asarray(L0_attn_Wv, dtype=np.float32).T)
    woT = np.ascontiguousarray(np.asarray(L0_attn_Wo, dtype=np.float32).T)
    w1T = np.ascontiguousarray(np.asarray(L0_ffn_w1, dtype=np.float32).T)
    w2T = np.ascontiguousarray(np.asarray(L0_ffn_w2, dtype=np.float32).T)
    qnw = np.ascontiguousarray(np.asarray(L0_attn_q_norm_w, dtype=np.float32).reshape(1, HD))
    knw = np.ascontiguousarray(np.asarray(L0_attn_k_norm_w, dtype=np.float32).reshape(1, HD))
    ident_f32 = np.eye(128, dtype=np.float32)
    ident_bf = _bf(np.eye(128))
    masks = [make_mask(0), make_mask(1)]

    nc = build_nc()
    in_maps = []
    for c in range(8):
        b, h = c // 2, c % 2
        own = np.ascontiguousarray(x[b, h * T:(h + 1) * T, :])
        oth = np.ascontiguousarray(x[b, (1 - h) * T:(2 - h) * T, :])
        in_maps.append({
            "x_own": own, "x_oth": oth,
            "wqT": wqT, "wkT": wkT, "wvT": wvT, "woT": woT,
            "w1T": w1T, "w2T": w2T,
            "qnw_row": qnw, "knw_row": knw,
            "ident_f32": ident_f32, "ident_bf": ident_bf,
            "mask": masks[h],
        })
    res = run_bass_kernel_spmd(nc, in_maps, core_ids=list(range(8)))
    out = np.empty((4, L, D), dtype=np.float32)
    for c in range(8):
        b, h = c // 2, c % 2
        out[b, h * T:(h + 1) * T, :] = res.results[c]["y"]
    return out


class NS:
    pass


def build_nc():
    nc = bass.Bass(trn_type="TRN2", target_bir_lowering=False,
                   detect_race_conditions=RACE_DETECT)
    s = NS()
    s.nc = nc
    skind = "ExternalOutput" if DEBUG else "Internal"

    s.x_own = nc.dram_tensor("x_own", [T, D], F32, kind="ExternalInput").ap()
    s.x_oth = nc.dram_tensor("x_oth", [T, D], F32, kind="ExternalInput").ap()
    s.wqT = nc.dram_tensor("wqT", [D, D], F32, kind="ExternalInput").ap()
    s.wkT = nc.dram_tensor("wkT", [D, NKV * HD], F32, kind="ExternalInput").ap()
    s.wvT = nc.dram_tensor("wvT", [D, NKV * HD], F32, kind="ExternalInput").ap()
    s.woT = nc.dram_tensor("woT", [D, D], F32, kind="ExternalInput").ap()
    s.w1T = nc.dram_tensor("w1T", [D, DH], F32, kind="ExternalInput").ap()
    s.w2T = nc.dram_tensor("w2T", [DH, D], F32, kind="ExternalInput").ap()
    s.qnw_row_d = nc.dram_tensor("qnw_row", [1, HD], F32, kind="ExternalInput").ap()
    s.knw_row_d = nc.dram_tensor("knw_row", [1, HD], F32, kind="ExternalInput").ap()
    s.ident_f32_d = nc.dram_tensor("ident_f32", [128, 128], F32, kind="ExternalInput").ap()
    s.ident_bf_d = nc.dram_tensor("ident_bf", [128, 128], BF16, kind="ExternalInput").ap()
    s.mask_d = nc.dram_tensor("mask", [8, 128, 1024], BF16, kind="ExternalInput").ap()

    s.y = nc.dram_tensor("y", [T, D], F32, kind="ExternalOutput").ap()

    def scr(name, shape, dtype):
        return nc.dram_tensor(name, shape, dtype, kind=skind).ap()
    s.lnT_hi = scr("lnT_hi", [16, 128, L], BF16)
    s.lnT_lo = scr("lnT_lo", [16, 128, L], BF16)
    s.kn_hi = scr("kn_hi", [NKV, 32, 128, 128], BF16)
    s.kn_lo = scr("kn_lo", [NKV, 32, 128, 128], BF16)
    s.vt_hi = scr("vt_hi", [L, NKV * HD], BF16)
    s.vt_lo = scr("vt_lo", [L, NKV * HD], BF16)
    s.qn_hi = scr("qn_hi", [NH, 16, 128, 128], BF16)
    s.qn_lo = scr("qn_lo", [NH, 16, 128, 128], BF16)
    s.wo_hi = scr("wo_hi", [D, D], BF16)
    s.wo_lo = scr("wo_lo", [D, D], BF16)
    s.y_sc = scr("y_sc", [T, D], F32)
    s.x2_sc = scr("x2_sc", [T, D], F32)
    s.xq_sc = scr("xq_sc", [16, 128, T], BF16)
    s.g_sc = scr("g_sc", [4, DH, 512], F32)
    s.hq_sc = scr("hq_sc", [DH, T], BF16)
    s.t2_sc = scr("t2_sc", [DH, D], BF16)
    s.dbg_m = scr("dbg_m", [1, 4], F32)

    def sb(name, shape, dtype):
        return nc.alloc_sbuf_tensor(name, shape, dtype).ap()
    s.ident_f32 = sb("c_ident32", [128, 128], F32)
    s.ident_bf = sb("c_identbf", [128, 128], BF16)
    s.qnw_row = sb("c_qnw", [1, HD], F32)
    s.knw_row = sb("c_knw", [1, HD], F32)
    s.qnw_b = sb("c_qnwb", [128, HD], F32)
    s.knw_b = sb("c_knwb", [128, HD], F32)
    s.ones_bf = sb("c_onesbf", [128, 1], BF16)
    s.ones_col = sb("c_onescol", [128, 1], F32)
    s.ones_row = sb("c_onesrow", [1, 128], F32)
    s.halfc = sb("c_halfc", [128, 1], F32)
    s.nhalfc = sb("c_nhalfc", [128, 1], F32)
    s.m1c = sb("c_m1c", [1, 1], F32)
    s.m2c = sb("c_m2c", [1, 1], F32)
    s.s1col = sb("c_s1col", [128, 1], F32)
    s.s2col = sb("c_s2col", [128, 1], F32)
    s.m2col = sb("c_m2col", [128, 1], F32)
    s.mid_ctx = contextlib.ExitStack()
    s.m_acc1 = s.mid_ctx.enter_context(nc.sbuf_tensor("c_macc1", [128, 1], F32)).ap()
    s.m_acc2 = s.mid_ctx.enter_context(nc.sbuf_tensor("c_macc2", [128, 1], F32)).ap()
    s.m_tmpc = s.mid_ctx.enter_context(nc.sbuf_tensor("c_mtmpc", [128, 1], F32)).ap()
    s.pre_buf = s.mid_ctx.enter_context(nc.sbuf_tensor("c_prebuf", [128, 4096], F32)).ap()
    s.dbg_st = s.mid_ctx.enter_context(nc.sbuf_tensor("c_dbgst", [1, 4], F32)).ap()

    s.pb = [nc.alloc_psum_tensor(f"pb{i}", [128, 512], F32).ap() for i in range(8)]

    s.sems = {}
    def sem(name):
        h = nc.ctx.enter_context(nc.semaphore(name))
        s.sems[name] = h
        return h
    s.sem = sem

    s.pre_gld = sem("pre_gld")
    s.pre_red = sem("pre_red")
    s.pre_done = sem("pre_done")
    s.pre_plan = []
    for i in range(D // 128):
        for j in range(DH // 2048):
            s.pre_plan.append((s.w1T, i * 128, j * 2048, s.m_acc1))
    for i in range(DH // 128):
        s.pre_plan.append((s.w2T, i * 128, 0, s.m_acc2))
    s.pre_n = 0
    s.pre_nv = 0

    emit_init(s)
    if STOP_AFTER == "init":
        return nc
    emit_p1_ln(s)
    if STOP_AFTER == "ln":
        return nc
    for kind in PP_ORDER:
        emit_proj_pass(s, kind)
        if STOP_AFTER == kind:
            return nc
    emit_wo_split(s)
    emit_m_finalize(s)
    if STOP_AFTER == "p1":
        return nc
    s.mid_ctx.close()
    s.s2_b = sb("c_s2b", [128, T], F32)
    s.c2_cols = sb("c_c2cols", [128, 16], F32)
    s.am2t = sb("c_am2t", [128, 16], F32)
    late_ctx = contextlib.ExitStack()
    s.amax1_row = late_ctx.enter_context(nc.sbuf_tensor("c_am1row", [1, T], F32)).ap()
    s.amax2_row = late_ctx.enter_context(nc.sbuf_tensor("c_am2row", [1, T], F32)).ap()
    s.c1_b = late_ctx.enter_context(nc.sbuf_tensor("c_c1b", [128, T], F32)).ap()
    emit_p2(s)
    if STOP_AFTER == "p2":
        return nc
    emit_p3(s)
    emit_c_rows(s)
    if STOP_AFTER == "p3":
        return nc
    emit_ffn_a(s)
    if STOP_AFTER == "ffa":
        late_ctx.close()
        return nc
    late_ctx.close()
    emit_ffn_b(s)
    return nc


def feed_prepass(s, g, nchunks):
    for _ in range(nchunks):
        if s.pre_n >= len(s.pre_plan):
            return
        w, r0, c0, acc = s.pre_plan[s.pre_n]
        n = s.pre_n
        if n >= 2:
            g.wait_ge(s.pre_red, n - 1)
        buf = s.pre_buf[:, (n % 2) * 2048:(n % 2 + 1) * 2048]
        g.dma_start(out=buf, in_=w[r0:r0 + 128, c0:c0 + 2048]).then_inc(s.pre_gld, 16)
        s.pre_n += 1


def feed_prepass_v(s, v, nchunks):
    for _ in range(nchunks):
        if s.pre_nv >= len(s.pre_plan):
            return
        w, r0, c0, acc = s.pre_plan[s.pre_nv]
        n = s.pre_nv
        v.wait_ge(s.pre_gld, 16 * (n + 1))
        buf = s.pre_buf[:, (n % 2) * 2048:(n % 2 + 1) * 2048]
        v.tensor_reduce(s.m_tmpc, buf, AX.X, ALU.add, apply_absolute_value=True)
        v.drain()
        v.tensor_tensor(acc, acc, s.m_tmpc, ALU.add).then_inc(s.pre_red, 1)
        s.pre_nv += 1
        if s.pre_nv == len(s.pre_plan):
            v.drain().then_inc(s.pre_done, 1)


def emit_init(s):
    nc = s.nc
    ld = s.sem("init_ld")
    ldv = s.sem("init_ldv")
    ldt = s.sem("init_ldt")
    with nc.Block() as block:
        @block.sync
        def _(sy):
            sy.dma_start(out=s.ident_f32, in_=s.ident_f32_d).then_inc(ld, 16)
            sy.dma_start(out=s.ident_bf, in_=s.ident_bf_d).then_inc(ld, 16)
            sy.dma_start(out=s.qnw_row, in_=s.qnw_row_d).then_inc(ld, 16)
            sy.dma_start(out=s.knw_row, in_=s.knw_row_d).then_inc(ld, 16)
            sy.wait_ge(ld, 16 * 4)
        @block.vector
        def _(v):
            v.memset(s.ones_bf, 1.0)
            v.memset(s.ones_col, 1.0)
            v.memset(s.ones_row, 1.0)
            v.memset(s.halfc, 0.5)
            v.memset(s.nhalfc, -0.5)
            v.memset(s.m_acc1, 0.0)
            v.memset(s.m_acc2, 0.0)
            v.memset(s.m_tmpc, 0.0)
            v.drain().then_inc(ldv, 1)
    with nc.Block() as block:
        @block.tensor
        def _(t):
            t.wait_ge(ld, 16 * 4)
            t.wait_ge(ldv, 1)
            t.matmul(s.pb[0][:, 0:HD], s.ones_row, s.qnw_row, start=True, stop=True)
            t.matmul(s.pb[1][:, 0:HD], s.ones_row, s.knw_row, start=True, stop=True).then_inc(ldt, 1)
        @block.vector
        def _(v):
            v.wait_ge(ldt, 1)
            v.tensor_copy(s.qnw_b, s.pb[0][:, 0:HD])
            v.tensor_copy(s.knw_b, s.pb[1][:, 0:HD])


def emit_p1_ln(s):
    nc = s.nc
    xld = s.sem("l_xld")
    lnr = s.sem("l_lnr")
    rs1 = s.sem("l_rs1")
    rs2 = s.sem("l_rs2")
    trd = s.sem("l_trd")
    evd = s.sem("l_evd")
    spd = s.sem("l_spd")
    ctx = contextlib.ExitStack()
    sbuf = nc.sbuf_tensor
    x_st = [ctx.enter_context(sbuf(f"l_x{i}", [128, D], F32)).ap() for i in range(2)]
    ln_st = [ctx.enter_context(sbuf(f"l_ln{i}", [128, D], F32)).ap() for i in range(2)]
    hi_st = [ctx.enter_context(sbuf(f"l_hi{i}", [128, D], BF16)).ap() for i in range(2)]
    lo_st = [ctx.enter_context(sbuf(f"l_lo{i}", [128, D], BF16)).ap() for i in range(2)]
    xsq = ctx.enter_context(sbuf("l_xsq", [128, D], F32)).ap()
    bn2 = ctx.enter_context(sbuf("l_bn2", [128, 2], F32)).ap()
    q2c = ctx.enter_context(sbuf("l_q2c", [128, 1], F32)).ap()
    veps = ctx.enter_context(sbuf("l_veps", [128, 1], F32)).ap()
    rvar = ctx.enter_context(sbuf("l_rvar", [128, 1], F32)).ap()
    rstd = ctx.enter_context(sbuf("l_rstd", [128, 1], F32)).ap()
    NTC = 32

    with nc.Block() as block:
        @block.sync
        def _(sy):
            for tc in range(NTC):
                src = s.x_own if tc < 16 else s.x_oth
                r = (tc % 16) * 128
                if tc >= 2:
                    sy.wait_ge(lnr, tc - 1)
                sy.dma_start(out=x_st[tc % 2], in_=src[r:r + 128, :]).then_inc(xld, 16)
            sy.wait_ge(spd, 16 * 32 * NTC)

        @block.vector
        def _(v):
            for tc in range(NTC):
                b = tc % 2
                v.wait_ge(xld, 16 * (tc + 1))
                v.tensor_reduce(bn2[:, 0:1], x_st[b], AX.X, ALU.add)
                v.tensor_tensor(xsq, x_st[b], x_st[b], ALU.mult)
                v.tensor_reduce(q2c, xsq, AX.X, ALU.add)
                v.drain()
                v.tensor_scalar(bn2[:, 0:1], bn2[:, 0:1], 1.0 / D, None, ALU.mult)
                v.drain()
                v.tensor_tensor(veps, bn2[:, 0:1], bn2[:, 0:1], ALU.mult)
                v.drain()
                v.scalar_tensor_tensor(veps, q2c, 1.0 / D, veps, ALU.mult, ALU.subtract)
                v.drain()
                v.tensor_scalar(veps, veps, EPS_LN, None, ALU.add)
                v.drain()
                v.reciprocal(rvar, veps).then_inc(rs1, 1)
                if tc >= 2:
                    v.wait_ge(trd, 16 * (tc - 1))
                    v.wait_ge(spd, 16 * 32 * (tc - 1))
                v.wait_ge(rs2, tc + 1)
                v.tensor_scalar(ln_st[b], x_st[b], bn2[:, 0:1], None, ALU.subtract)
                v.tensor_scalar(ln_st[b], ln_st[b], rstd, None, ALU.mult).then_inc(lnr, 1)
                for fc in range(16):
                    pbt = s.pb[6 + (fc % 2)]
                    v.wait_ge(trd, tc * 16 + fc + 1)
                    v.tensor_copy(hi_st[b][:, fc * 128:(fc + 1) * 128], pbt[:, 0:128])
                    v.tensor_tensor(lo_st[b][:, fc * 128:(fc + 1) * 128], pbt[:, 0:128],
                                    hi_st[b][:, fc * 128:(fc + 1) * 128],
                                    ALU.subtract).then_inc(evd, 1)
                if tc % 8 == 7:
                    feed_prepass_v(s, v, 4)

        @block.scalar
        def _(sc):
            def spill(tc):
                bb = tc % 2
                sc.wait_ge(evd, 16 * (tc + 1))
                for fc in range(16):
                    sc.dma_start(out=s.lnT_hi[fc, :, tc * 128:(tc + 1) * 128],
                                 in_=hi_st[bb][:, fc * 128:(fc + 1) * 128]).then_inc(spd, 16)
                    sc.dma_start(out=s.lnT_lo[fc, :, tc * 128:(tc + 1) * 128],
                                 in_=lo_st[bb][:, fc * 128:(fc + 1) * 128]).then_inc(spd, 16)
            for tc in range(NTC):
                sc.wait_ge(rs1, tc + 1)
                sc.activation(rstd, rvar, AF.Sqrt).then_inc(rs2, 1)
                if tc >= 1:
                    spill(tc - 1)
            spill(NTC - 1)

        @block.tensor
        def _(t):
            for tc in range(NTC):
                t.wait_ge(lnr, tc + 1)
                for fc in range(16):
                    pbt = s.pb[6 + (fc % 2)]
                    if tc * 16 + fc >= 2:
                        t.wait_ge(evd, tc * 16 + fc - 1)
                    t.transpose(pbt[:, 0:128], ln_st[tc % 2][:, fc * 128:(fc + 1) * 128],
                                s.ident_f32).then_inc(trd, 1)

        @block.gpsimd
        def _(g):
            feed_prepass(s, g, 16)
    ctx.close()


def emit_proj_pass(s, kind):
    nc = s.nc
    is_q = kind in ("qa", "qb")
    is_v = kind == "v"
    NTC = 16 if is_q else 32
    if PP_LIMIT is not None:
        NTC = min(NTC, PP_LIMIT)
        NTC = max(4, NTC)
    w_dram = {"k": s.wkT, "v": s.wvT,
              "qa": s.wqT[:, 0:1024], "qb": s.wqT[:, 1024:2048]}[kind]
    head_off = 8 if kind == "qb" else 0
    nrm_b = s.qnw_b if is_q else s.knw_b
    dst_hi = s.qn_hi if is_q else s.kn_hi
    dst_lo = s.qn_lo if is_q else s.kn_lo

    names = ["wld", "wsp", "lnld", "pdone", "pfree", "rs1", "rs2",
             "knr", "trd", "ted", "ksr", "spd"]
    if not hasattr(s, "pp_sems"):
        s.pp_sems = {n: s.sem("pp_" + n) for n in names}
        s.pp_base = {n: 0 for n in names}
    B = dict(s.pp_base)
    SM = s.pp_sems
    wld, wsp, lnld, pdone, pfree, rs1, rs2, knr, trd, ted, ksr, spd = (
        SM[n] for n in names)

    p = kind + "_"
    ctx = contextlib.ExitStack()
    sbuf = nc.sbuf_tensor
    w_hi = ctx.enter_context(sbuf(p + "whi", [128, 16 * 1024], BF16)).ap()
    w_lo = ctx.enter_context(sbuf(p + "wlo", [128, 16 * 1024], BF16)).ap()
    wtmp = [ctx.enter_context(sbuf(p + f"wtmp{i}", [128, 1024], F32)).ap() for i in range(2)]
    lnb_hi = [ctx.enter_context(sbuf(p + f"lnhi{i}", [128, 16 * 512], BF16)).ap() for i in range(2)]
    lnb_lo = [ctx.enter_context(sbuf(p + f"lnlo{i}", [128, 16 * 512], BF16)).ap() for i in range(2)]
    kf = ctx.enter_context(sbuf(p + "kf", [128, 1024], F32)).ap()
    if is_v:
        vsp_hi = [ctx.enter_context(sbuf(p + f"vsphi{i}", [128, 1024], BF16)).ap() for i in range(2)]
        vsp_lo = [ctx.enter_context(sbuf(p + f"vsplo{i}", [128, 1024], BF16)).ap() for i in range(2)]
    else:
        ksq = ctx.enter_context(sbuf(p + "ksq", [128, 1024], F32)).ap()
        red8 = ctx.enter_context(sbuf(p + "red8", [128, 8], F32)).ap()
        rv8 = ctx.enter_context(sbuf(p + "rv8", [128, 8], F32)).ap()
        rstd8 = ctx.enter_context(sbuf(p + "rstd8", [128, 8], F32)).ap()
        knf8 = ctx.enter_context(sbuf(p + "knf8", [128, 1024], F32)).ap()
        sp_st = [ctx.enter_context(sbuf(p + f"spst{i}", [128, 16 * 128], BF16)).ap() for i in range(2)]

    NTG = NTC // 4
    pbt = [s.pb[4], s.pb[5]]

    with nc.Block() as block:
        @block.sync
        def _(sy):
            for fc in range(16):
                if fc >= 2:
                    sy.wait_ge(wsp, B["wsp"] + fc - 1)
                sy.dma_start(out=wtmp[fc % 2],
                             in_=w_dram[fc * 128:(fc + 1) * 128, :]).then_inc(wld, 16)
            for tg in range(NTG):
                if tg >= 2:
                    sy.wait_ge(pdone, B["pdone"] + 4 * (tg - 1))
                ts = tg * 512
                sy.dma_start(out=lnb_hi[tg % 2],
                             in_=s.lnT_hi[:, :, ts:ts + 512].rearrange("fc f t -> f fc t")
                             ).then_inc(lnld, 16)
                sy.dma_start(out=lnb_lo[tg % 2],
                             in_=s.lnT_lo[:, :, ts:ts + 512].rearrange("fc f t -> f fc t")
                             ).then_inc(lnld, 16)
            sy.wait_ge(spd, B["spd"] + 16 * (2 * NTC if is_v else 16 * NTC))

        @block.vector
        def _(v):
            for fc in range(16):
                v.wait_ge(wld, B["wld"] + 16 * (fc + 1))
                v.tensor_copy(w_hi[:, fc * 1024:(fc + 1) * 1024], wtmp[fc % 2])
                v.tensor_tensor(w_lo[:, fc * 1024:(fc + 1) * 1024], wtmp[fc % 2],
                                w_hi[:, fc * 1024:(fc + 1) * 1024], ALU.subtract).then_inc(wsp, 1)
            for tc in range(NTC):
                v.wait_ge(pdone, B["pdone"] + tc + 1)
                pa = s.pb[2 * (tc % 2)]
                pa1 = s.pb[2 * (tc % 2) + 1]
                v.tensor_copy(kf[:, 0:512], pa)
                v.tensor_copy(kf[:, 512:1024], pa1).then_inc(pfree, 1)
                bb = tc % 2
                if is_v:
                    if tc >= 2:
                        v.wait_ge(spd, B["spd"] + 16 * 2 * (tc - 1))
                    v.tensor_copy(vsp_hi[bb], kf)
                    v.tensor_tensor(vsp_lo[bb], kf, vsp_hi[bb], ALU.subtract).then_inc(ksr, 1)
                    if tc % 8 == 7:
                        feed_prepass_v(s, v, 5)
                    continue
                v.tensor_tensor(ksq, kf, kf, ALU.mult)
                for g8 in range(8):
                    v.tensor_reduce(red8[:, g8:g8 + 1], ksq[:, g8 * 128:(g8 + 1) * 128],
                                    AX.X, ALU.add)
                v.drain()
                v.tensor_scalar(rv8, red8, 1.0 / HD, EPS_RMS, ALU.mult, ALU.add)
                v.drain()
                v.reciprocal(rv8, rv8).then_inc(rs1, 1)
                v.wait_ge(rs2, B["rs2"] + tc + 1)
                for g8 in range(8):
                    sl = slice(g8 * 128, (g8 + 1) * 128)
                    tt = v.scalar_tensor_tensor(knf8[:, sl], kf[:, sl], rstd8[:, g8:g8 + 1],
                                                nrm_b, ALU.mult, ALU.mult)
                tt.then_inc(knr, 1)
                if tc >= 2:
                    v.wait_ge(spd, B["spd"] + 16 * 16 * (tc - 1))
                for g8 in range(8):
                    v.wait_ge(trd, B["trd"] + tc * 8 + g8 + 1)
                    pbx = pbt[g8 % 2][:, 0:128]
                    hi_sl = sp_st[bb][:, (2 * g8) * 128:(2 * g8 + 1) * 128]
                    v.tensor_copy(hi_sl, pbx).then_inc(ted, 1)
                    v.tensor_tensor(sp_st[bb][:, (2 * g8 + 1) * 128:(2 * g8 + 2) * 128],
                                    pbx, hi_sl, ALU.subtract).then_inc(ted, 1)
                if tc % 8 == 7:
                    feed_prepass_v(s, v, 5)

        @block.scalar
        def _(sc):
            def spill(tc):
                bb = tc % 2
                if is_v:
                    sc.wait_ge(ksr, B["ksr"] + tc + 1)
                    sc.dma_start(out=s.vt_hi[tc * 128:(tc + 1) * 128, :],
                                 in_=vsp_hi[bb]).then_inc(spd, 16)
                    sc.dma_start(out=s.vt_lo[tc * 128:(tc + 1) * 128, :],
                                 in_=vsp_lo[bb]).then_inc(spd, 16)
                else:
                    sc.wait_ge(ted, B["ted"] + (tc + 1) * 16)
                    for g8 in range(8):
                        h = head_off + g8
                        sc.dma_start(out=dst_hi[h, tc, :, :],
                                     in_=sp_st[bb][:, (2 * g8) * 128:(2 * g8 + 1) * 128]
                                     ).then_inc(spd, 16)
                        sc.dma_start(out=dst_lo[h, tc, :, :],
                                     in_=sp_st[bb][:, (2 * g8 + 1) * 128:(2 * g8 + 2) * 128]
                                     ).then_inc(spd, 16)
            for tc in range(NTC):
                if not is_v:
                    sc.wait_ge(rs1, B["rs1"] + tc + 1)
                    sc.activation(rstd8, rv8, AF.Sqrt).then_inc(rs2, 1)
                if tc >= 1:
                    spill(tc - 1)
            spill(NTC - 1)

        @block.tensor
        def _(t):
            def emit_tr(tc):
                t.wait_ge(knr, B["knr"] + tc + 1)
                for g8 in range(8):
                    w = tc * 16 + 2 * g8
                    if w >= 2:
                        t.wait_ge(ted, B["ted"] + w - 2)
                    t.transpose(pbt[g8 % 2][:, 0:128],
                                knf8[:, g8 * 128:(g8 + 1) * 128],
                                s.ident_f32).then_inc(trd, 1)
            t.wait_ge(wsp, B["wsp"] + 16)
            for tc in range(NTC):
                tg, tcq = tc // 4, tc % 4
                t.wait_ge(lnld, B["lnld"] + 16 * 2 * (tg + 1))
                if tc >= 2:
                    t.wait_ge(pfree, B["pfree"] + tc - 1)
                pa = s.pb[2 * (tc % 2)]
                pa1 = s.pb[2 * (tc % 2) + 1]
                lh = lnb_hi[tg % 2]
                ll = lnb_lo[tg % 2]
                for fc in range(16):
                    sl = slice(fc * 512 + tcq * 128, fc * 512 + (tcq + 1) * 128)
                    st_hi = lh[:, sl]
                    st_lo = ll[:, sl]
                    first, last = fc == 0, fc == 15
                    t.matmul(pa, st_hi, w_hi[:, fc * 1024:fc * 1024 + 512],
                             start=first, stop=False)
                    t.matmul(pa1, st_hi, w_hi[:, fc * 1024 + 512:(fc + 1) * 1024],
                             start=first, stop=False)
                    t.matmul(pa, st_hi, w_lo[:, fc * 1024:fc * 1024 + 512],
                             start=False, stop=False)
                    t.matmul(pa1, st_hi, w_lo[:, fc * 1024 + 512:(fc + 1) * 1024],
                             start=False, stop=False)
                    t.matmul(pa, st_lo, w_hi[:, fc * 1024:fc * 1024 + 512],
                             start=False, stop=last)
                    mm = t.matmul(pa1, st_lo, w_hi[:, fc * 1024 + 512:(fc + 1) * 1024],
                                  start=False, stop=last)
                    if last:
                        mm.then_inc(pdone, 1)
                if not is_v and tc >= 1:
                    emit_tr(tc - 1)
            if not is_v:
                emit_tr(NTC - 1)

        @block.gpsimd
        def _(g):
            if not PP_NO_PRE:
                feed_prepass(s, g, 20 if not is_q else 10)
    ctx.close()
    s.pp_base["wld"] += 16 * 16
    s.pp_base["wsp"] += 16
    s.pp_base["lnld"] += 16 * 2 * NTG
    s.pp_base["pdone"] += NTC
    s.pp_base["pfree"] += NTC
    if is_v:
        s.pp_base["ksr"] += NTC
        s.pp_base["spd"] += 16 * 2 * NTC
    else:
        s.pp_base["rs1"] += NTC
        s.pp_base["rs2"] += NTC
        s.pp_base["knr"] += NTC
        s.pp_base["trd"] += 8 * NTC
        s.pp_base["ted"] += 16 * NTC
        s.pp_base["spd"] += 16 * 16 * NTC


def emit_wo_split(s):
    nc = s.nc
    wld = s.sem("wo_wld")
    wsp = s.sem("wo_wsp")
    spd = s.sem("wo_spd")
    ctx = contextlib.ExitStack()
    wtmp = [ctx.enter_context(nc.sbuf_tensor(f"wo_t{i}", [128, D], F32)).ap() for i in range(2)]
    whi = [ctx.enter_context(nc.sbuf_tensor(f"wo_h{i}", [128, D], BF16)).ap() for i in range(2)]
    wlo = [ctx.enter_context(nc.sbuf_tensor(f"wo_l{i}", [128, D], BF16)).ap() for i in range(2)]
    with nc.Block() as block:
        @block.sync
        def _(sy):
            for i in range(16):
                if i >= 2:
                    sy.wait_ge(wsp, i - 1)
                sy.dma_start(out=wtmp[i % 2], in_=s.woT[i * 128:(i + 1) * 128, :]).then_inc(wld, 16)
            sy.wait_ge(spd, 16 * 32)
        @block.vector
        def _(v):
            for i in range(16):
                b = i % 2
                v.wait_ge(wld, 16 * (i + 1))
                if i >= 2:
                    v.wait_ge(spd, 16 * 2 * (i - 1))
                v.tensor_copy(whi[b], wtmp[b])
                v.tensor_tensor(wlo[b], wtmp[b], whi[b], ALU.subtract).then_inc(wsp, 1)
            feed_prepass_v(s, v, len(s.pre_plan) - s.pre_nv)
        @block.scalar
        def _(sc):
            for i in range(16):
                b = i % 2
                sc.wait_ge(wsp, i + 1)
                sc.dma_start(out=s.wo_hi[i * 128:(i + 1) * 128, :], in_=whi[b]).then_inc(spd, 16)
                sc.dma_start(out=s.wo_lo[i * 128:(i + 1) * 128, :], in_=wlo[b]).then_inc(spd, 16)
        @block.gpsimd
        def _(g):
            feed_prepass(s, g, len(s.pre_plan) - s.pre_n)
    ctx.close()


def emit_m_finalize(s):
    nc = s.nc
    fin = s.sem("mfin")
    with nc.Block() as block:
        @block.tensor
        def _(t):
            t.wait_ge(s.pre_done, 1)
            t.matmul(s.pb[0][0:1, 0:1], s.ones_col, s.m_acc1, start=True, stop=True)
            t.matmul(s.pb[1][0:1, 0:1], s.ones_col, s.m_acc2,
                     start=True, stop=True).then_inc(fin, 1)
        @block.vector
        def _(v):
            v.wait_ge(fin, 1)
            v.tensor_scalar(s.m1c, s.pb[0][0:1, 0:1], 1.0 / (D * DH), EPS_Q, ALU.mult, ALU.max)
            v.tensor_scalar(s.m2c, s.pb[1][0:1, 0:1], 1.0 / (D * DH), EPS_Q, ALU.mult, ALU.max)
            v.drain().then_inc(fin, 1)
    with nc.Block() as block:
        @block.tensor
        def _(t):
            t.wait_ge(fin, 2)
            t.matmul(s.pb[0][:, 0:1], s.ones_row, s.m2c, start=True, stop=True)
            t.matmul(s.pb[1][:, 0:1], s.ones_row, s.m1c, start=True, stop=True).then_inc(fin, 1)
        @block.vector
        def _(v):
            v.wait_ge(fin, 3)
            v.tensor_copy(s.m2col, s.pb[0][:, 0:1])
            v.reciprocal(s.s1col, s.pb[1][:, 0:1])
            v.reciprocal(s.s2col, s.pb[0][:, 0:1])
            if DEBUG:
                v.drain()
                v.tensor_copy(s.dbg_st[0:1, 0:1], s.m1c)
                v.tensor_copy(s.dbg_st[0:1, 1:2], s.m2c)
                v.tensor_copy(s.dbg_st[0:1, 2:3], s.s1col[0:1, 0:1])
                v.drain().then_inc(fin, 1)
        @block.scalar
        def _(sc):
            if DEBUG:
                sc.wait_ge(fin, 4)
                sc.dma_start(out=s.dbg_m, in_=s.dbg_st[0:1, 0:4]).then_inc(fin, 16)
                sc.wait_ge(fin, 20)


def emit_p2(s):
    nc = s.nc

    kvld = s.sem("a_kvld")
    qld = s.sem("a_qld")
    ssem = s.sem("a_ssem")
    esem = s.sem("a_esem")
    usem = s.sem("a_usem")
    csem = s.sem("a_csem")
    tosem = s.sem("a_tosem")
    recsem = s.sem("a_recsem")
    recbsem = s.sem("a_recbsem")
    recbfree = s.sem("a_recbfree")
    aosem = s.sem("a_aosem")
    wold = s.sem("a_wold")
    ydone = s.sem("a_ydone")
    yev = s.sem("a_yev")
    ytr = s.sem("a_ytr")
    yevp = s.sem("a_yevp")
    ysp = s.sem("a_ysp")

    ctx = contextlib.ExitStack()
    sbuf = nc.sbuf_tensor
    K_hi = [ctx.enter_context(sbuf(f"a_khi{i}", [128, L], BF16)).ap() for i in range(2)]
    K_lo = [ctx.enter_context(sbuf(f"a_klo{i}", [128, L], BF16)).ap() for i in range(2)]
    V_hi = [ctx.enter_context(sbuf(f"a_vhi{i}", [128, L], BF16)).ap() for i in range(2)]
    V_lo = [ctx.enter_context(sbuf(f"a_vlo{i}", [128, L], BF16)).ap() for i in range(2)]
    Q_hi = [ctx.enter_context(sbuf(f"a_qhi{i}", [128, 512], BF16)).ap() for i in range(2)]
    Q_lo = [ctx.enter_context(sbuf(f"a_qlo{i}", [128, 512], BF16)).ap() for i in range(2)]
    U_f = [ctx.enter_context(sbuf(f"a_uf{i}", [128, 512], F32)).ap() for i in range(2)]
    U_hi = [ctx.enter_context(sbuf(f"a_uhi{i}", [128, 512], BF16)).ap() for i in range(2)]
    U_lo = [ctx.enter_context(sbuf(f"a_ulo{i}", [128, 512], BF16)).ap() for i in range(2)]
    mask_sb = ctx.enter_context(sbuf("a_mask", [128, 8 * 1024], BF16)).ap()
    rec_row = [ctx.enter_context(sbuf(f"a_rr{i}", [1, 512], F32)).ap() for i in range(2)]
    rec_b = ctx.enter_context(sbuf("a_recb", [128, 512], F32)).ap()
    ao_f = ctx.enter_context(sbuf("a_aof", [128, 512], F32)).ap()
    ao_hi = ctx.enter_context(sbuf("a_aohi", [128, NH * 512], BF16)).ap()
    ao_lo = ctx.enter_context(sbuf("a_aolo", [128, NH * 512], BF16)).ap()
    woc_hi = [ctx.enter_context(sbuf(f"a_wochi{i}", [128, D], BF16)).ap() for i in range(2)]
    woc_lo = [ctx.enter_context(sbuf(f"a_woclo{i}", [128, D], BF16)).ap() for i in range(2)]
    y_oc = ctx.enter_context(sbuf("a_yoc", [128, 512], F32)).ap()
    piece = [ctx.enter_context(sbuf(f"a_piece{i}", [128, 128], F32)).ap() for i in range(4)]

    psS = [s.pb[0], s.pb[1]]
    psO = s.pb[2]
    psSum = s.pb[3][0:1, :]
    psRB = s.pb[4]
    psY = [s.pb[4], s.pb[5]]
    psTR = [s.pb[6], s.pb[7]]

    with nc.Block() as block:
        @block.sync
        def _(sy):
            sy.dma_start(out=mask_sb, in_=s.mask_d.rearrange("c p q -> p c q")).then_inc(qld, 16)
            for qg in range(4):
                for g in range(8):
                    gi = qg * 8 + g
                    if gi >= 2:
                        gp = gi - 2
                        sy.wait_ge(recbsem, (gp // 8) * 16 + (gp % 8) * 2 + 2)
                    sy.dma_start(out=K_hi[g % 2],
                                 in_=s.kn_hi[g].rearrange("kc hd t -> hd kc t")).then_inc(kvld, 16)
                    sy.dma_start(out=K_lo[g % 2],
                                 in_=s.kn_lo[g].rearrange("kc hd t -> hd kc t")).then_inc(kvld, 16)
                    sy.dma_start(out=V_hi[g % 2],
                                 in_=s.vt_hi[:, g * 128:(g + 1) * 128].rearrange(
                                     "(kc p) hd -> p kc hd", p=128)).then_inc(kvld, 16)
                    sy.dma_start(out=V_lo[g % 2],
                                 in_=s.vt_lo[:, g * 128:(g + 1) * 128].rearrange(
                                     "(kc p) hd -> p kc hd", p=128)).then_inc(kvld, 16)
                    for hh in range(2):
                        U = qg * 16 + g * 2 + hh
                        h = g * 2 + hh
                        if U >= 2:
                            sy.wait_ge(tosem, U - 1)
                        sy.dma_start(out=Q_hi[U % 2],
                                     in_=s.qn_hi[h, qg * 4:(qg + 1) * 4].rearrange(
                                         "tc hd t -> hd tc t")).then_inc(qld, 16)
                        sy.dma_start(out=Q_lo[U % 2],
                                     in_=s.qn_lo[h, qg * 4:(qg + 1) * 4].rearrange(
                                         "tc hd t -> hd tc t")).then_inc(qld, 16)
                for oc in range(16):
                    oci = qg * 16 + oc
                    if oci >= 2:
                        sy.wait_ge(ydone, oci - 1)
                    sy.dma_start(out=woc_hi[oc % 2],
                                 in_=s.wo_hi[:, oc * 128:(oc + 1) * 128].rearrange(
                                     "(hc f) o -> f hc o", f=128)).then_inc(wold, 16)
                    sy.dma_start(out=woc_lo[oc % 2],
                                 in_=s.wo_lo[:, oc * 128:(oc + 1) * 128].rearrange(
                                     "(hc f) o -> f hc o", f=128)).then_inc(wold, 16)
            sy.wait_ge(ysp, 16 * 64 * 4)

        @block.tensor
        def _(t):
            def consume(U, kc, vh, vl):
                t.wait_ge(usem, U * 32 + kc + 1)
                b = kc % 2
                ks = slice(kc * 128, (kc + 1) * 128)
                t.matmul(psSum, s.ones_bf, U_hi[b], start=(kc == 0), stop=False)
                mmsum = t.matmul(psSum, s.ones_bf, U_lo[b], start=False, stop=(kc == 31))
                if kc == 31:
                    mmsum.then_inc(tosem, 1)
                t.matmul(psO, vh[:, ks], U_hi[b], start=(kc == 0), stop=False)
                t.matmul(psO, vh[:, ks], U_lo[b], start=False, stop=False)
                t.matmul(psO, vl[:, ks], U_hi[b],
                         start=False, stop=(kc == 31)).then_inc(csem, 1)

            def ytrans(qg, oc):
                t.wait_ge(yev, qg * 16 + oc + 1)
                for pc in range(4):
                    n = (qg * 16 + oc) * 4 + pc
                    if n >= 2:
                        t.wait_ge(yevp, n - 1)
                    t.transpose(psTR[pc % 2][:, 0:128],
                                y_oc[:, pc * 128:(pc + 1) * 128], s.ident_f32).then_inc(ytr, 1)

            pending_rec = [None]
            def emit_rec():
                U, u = pending_rec[0]
                pending_rec[0] = None
                t.wait_ge(recsem, U + 1)
                if U >= 1:
                    t.wait_ge(recbfree, U)
                if u == 0:
                    t.wait_ge(yev, U)
                t.matmul(psRB, s.ones_row, rec_row[U % 2],
                         start=True, stop=True).then_inc(recbsem, 1)
            for qg in range(4):
                for g in range(8):
                    gi = qg * 8 + g
                    t.wait_ge(kvld, 16 * 4 * (gi + 1))
                    for hh in range(2):
                        u = g * 2 + hh
                        U = qg * 16 + u
                        t.wait_ge(qld, 16 * (2 * (U + 1) + 1))
                        t.wait_ge(recsem, U)
                        kh, kl = K_hi[g % 2], K_lo[g % 2]
                        vh, vl = V_hi[g % 2], V_lo[g % 2]
                        qh, ql = Q_hi[U % 2], Q_lo[U % 2]
                        for kc in range(32):
                            pS = psS[kc % 2]
                            t.wait_ge(esem, max(0, U * 32 + kc - 1))
                            ks = slice(kc * 128, (kc + 1) * 128)
                            masked = (8 <= kc < 16) and qg >= 2
                            t.matmul(pS, kh[:, ks], qh, start=True, stop=False)
                            t.matmul(pS, kh[:, ks], ql, start=False, stop=False)
                            mm3 = t.matmul(pS, kl[:, ks], qh, start=False, stop=not masked)
                            if masked:
                                t.matmul(pS, s.ident_bf,
                                         mask_sb[:, (kc - 8) * 1024 + (qg - 2) * 512:
                                                 (kc - 8) * 1024 + (qg - 2) * 512 + 512],
                                         start=False, stop=True).then_inc(ssem, 1)
                            else:
                                mm3.then_inc(ssem, 1)
                            if kc == 1:
                                if pending_rec[0] is not None:
                                    emit_rec()
                                t.wait_ge(aosem, U)
                            if kc >= 1:
                                consume(U, kc - 1, vh, vl)
                        consume(U, 31, vh, vl)
                        pending_rec[0] = (U, u)
                if pending_rec[0] is not None:
                    emit_rec()
                t.wait_ge(aosem, qg * 16 + 16)
                for oc in range(16):
                    oci = qg * 16 + oc
                    t.wait_ge(wold, 16 * 2 * (oci + 1))
                    if oc >= 2:
                        t.wait_ge(yev, qg * 16 + oc - 1)
                    pY = psY[oc % 2]
                    wh, wl = woc_hi[oc % 2], woc_lo[oc % 2]
                    for h in range(NH):
                        ah = ao_hi[:, h * 512:(h + 1) * 512]
                        al = ao_lo[:, h * 512:(h + 1) * 512]
                        wt_h = wh[:, h * 128:(h + 1) * 128]
                        wt_l = wl[:, h * 128:(h + 1) * 128]
                        t.matmul(pY, wt_h, ah, start=(h == 0), stop=False)
                        t.matmul(pY, wt_h, al, start=False, stop=False)
                        mmy = t.matmul(pY, wt_l, ah, start=False, stop=(h == NH - 1))
                    mmy.then_inc(ydone, 1)
                    if oc >= 1:
                        ytrans(qg, oc - 1)
                ytrans(qg, 15)

        @block.scalar
        def _(sc):
            for U in range(64):
                for kc in range(32):
                    sc.wait_ge(ssem, U * 32 + kc + 1)
                    sc.wait_ge(usem, max(0, U * 32 + kc - 1))
                    sc.activation(U_f[kc % 2], psS[kc % 2], AF.Exp, scale=SCALE).then_inc(esem, 1)

        @block.vector
        def _(v):
            for qg in range(4):
                for u in range(16):
                    U = qg * 16 + u
                    if u == 0 and qg >= 1:
                        v.wait_ge(ydone, qg * 16)
                    for kc in range(32):
                        b = kc % 2
                        v.wait_ge(esem, U * 32 + kc + 1)
                        v.wait_ge(csem, max(0, U * 32 + kc - 1))
                        v.tensor_copy(U_hi[b], U_f[b])
                        v.tensor_tensor(U_lo[b], U_f[b], U_hi[b], ALU.subtract).then_inc(usem, 1)
                    v.wait_ge(tosem, U + 1)
                    v.reciprocal(rec_row[U % 2], psSum).then_inc(recsem, 1)
                    v.wait_ge(recbsem, U + 1)
                    v.tensor_copy(rec_b, psRB).then_inc(recbfree, 1)
                    v.wait_ge(csem, U * 32 + 32)
                    v.tensor_tensor(ao_f, psO, rec_b, ALU.mult)
                    v.tensor_copy(ao_hi[:, u * 512:(u + 1) * 512], ao_f)
                    v.tensor_tensor(ao_lo[:, u * 512:(u + 1) * 512], ao_f,
                                    ao_hi[:, u * 512:(u + 1) * 512], ALU.subtract).then_inc(aosem, 1)
                for oc in range(16):
                    v.wait_ge(ydone, qg * 16 + oc + 1)
                    v.tensor_copy(y_oc, psY[oc % 2]).then_inc(yev, 1)
                    for pc in range(4):
                        n = (qg * 16 + oc) * 4 + pc
                        v.wait_ge(ytr, n + 1)
                        if n >= 4:
                            v.wait_ge(ysp, 16 * (n - 3))
                        v.tensor_copy(piece[pc],
                                      psTR[pc % 2][:, 0:128]).then_inc(yevp, 1)

        @block.gpsimd
        def _(g):
            for qg in range(4):
                for oc in range(16):
                    for pc in range(4):
                        n = (qg * 16 + oc) * 4 + pc
                        g.wait_ge(yevp, n + 1)
                        tc = qg * 4 + pc
                        g.dma_start(out=s.y_sc[tc * 128:(tc + 1) * 128, oc * 128:(oc + 1) * 128],
                                    in_=piece[pc]).then_inc(ysp, 16)
            g.wait_ge(ysp, 16 * 64 * 4)
    ctx.close()


def emit_p3(s):
    nc = s.nc
    yld = s.sem("p3_yld")
    x2r = s.sem("p3_x2r")
    rs1 = s.sem("p3_rs1")
    rs2 = s.sem("p3_rs2")
    er = s.sem("p3_er")
    sg = s.sem("p3_sg")
    xqr = s.sem("p3_xqr")
    trd = s.sem("p3_trd")
    ted = s.sem("p3_ted")
    spd = s.sem("p3_spd")

    ctx = contextlib.ExitStack()
    sbuf = nc.sbuf_tensor
    y_st = [ctx.enter_context(sbuf(f"p3_y{i}", [128, D], F32)).ap() for i in range(2)]
    x_st = [ctx.enter_context(sbuf(f"p3_x{i}", [128, D], F32)).ap() for i in range(2)]
    x2_st = [ctx.enter_context(sbuf(f"p3_x2{i}", [128, D], F32)).ap() for i in range(2)]
    ln2 = ctx.enter_context(sbuf("p3_ln2", [128, D], F32)).ap()
    z_st = ctx.enter_context(sbuf("p3_z", [128, D], F32)).ap()
    sti = ctx.enter_context(sbuf("p3_sti", [128, D], I32)).ap()
    e_st = ctx.enter_context(sbuf("p3_e", [128, D], F32)).ap()
    sg1 = ctx.enter_context(sbuf("p3_sg1", [128, D], BF16)).ap()
    sg2 = ctx.enter_context(sbuf("p3_sg2", [128, D], BF16)).ap()
    cs_st = ctx.enter_context(sbuf("p3_cs", [128, D], BF16)).ap()
    xq_st = [ctx.enter_context(sbuf(f"p3_xq{i}", [128, D], F32)).ap() for i in range(2)]
    xsp = [ctx.enter_context(sbuf(f"p3_xsp{i}", [128, D], BF16)).ap() for i in range(2)]
    xsq = ctx.enter_context(sbuf("p3_xsq", [128, D], F32)).ap()
    bn2 = ctx.enter_context(sbuf("p3_bn2", [128, 2], F32)).ap()
    q2c = ctx.enter_context(sbuf("p3_q2c", [128, 1], F32)).ap()
    col = ctx.enter_context(sbuf("p3_col", [128, 1], F32)).ap()
    rstd = ctx.enter_context(sbuf("p3_rstd", [128, 1], F32)).ap()
    amx = ctx.enter_context(sbuf("p3_amx", [128, 1], F32)).ap()
    scol = ctx.enter_context(sbuf("p3_scol", [128, 1], F32)).ap()
    pbt = [s.pb[2], s.pb[3]]

    with nc.Block() as block:
        @block.sync
        def _(sy):
            for tc in range(16):
                if tc >= 2:
                    sy.wait_ge(x2r, tc - 1)
                sy.dma_start(out=y_st[tc % 2],
                             in_=s.y_sc[tc * 128:(tc + 1) * 128, :]).then_inc(yld, 16)
                sy.dma_start(out=x_st[tc % 2],
                             in_=s.x_own[tc * 128:(tc + 1) * 128, :]).then_inc(yld, 16)
            sy.wait_ge(spd, 16 * 17 * 16)

        @block.vector
        def _(v):
            for tc in range(16):
                b = tc % 2
                v.wait_ge(yld, 16 * 2 * (tc + 1))
                if tc >= 2:
                    v.wait_ge(spd, 16 * 17 * (tc - 1))
                v.tensor_tensor(x2_st[b], y_st[b], x_st[b], ALU.add).then_inc(x2r, 1)
                v.tensor_reduce(bn2[:, 0:1], x2_st[b], AX.X, ALU.add)
                v.tensor_tensor(xsq, x2_st[b], x2_st[b], ALU.mult)
                v.tensor_reduce(q2c, xsq, AX.X, ALU.add)
                v.drain()
                v.tensor_scalar(bn2[:, 0:1], bn2[:, 0:1], 1.0 / D, None, ALU.mult)
                v.drain()
                v.tensor_tensor(col, bn2[:, 0:1], bn2[:, 0:1], ALU.mult)
                v.drain()
                v.scalar_tensor_tensor(col, q2c, 1.0 / D, col, ALU.mult, ALU.subtract)
                v.drain()
                v.tensor_scalar(col, col, EPS_LN, None, ALU.add)
                v.drain()
                v.reciprocal(col, col).then_inc(rs1, 1)
                v.wait_ge(rs2, tc + 1)
                v.tensor_scalar(ln2, x2_st[b], bn2[:, 0:1], None, ALU.subtract)
                v.tensor_scalar(ln2, ln2, rstd, None, ALU.mult)
                v.tensor_reduce(amx, ln2, AX.X, ALU.max, apply_absolute_value=True)
                v.drain()
                v.tensor_scalar(amx, amx, EPS_Q, None, ALU.max)
                v.drain()
                v.reciprocal(scol, amx)
                v.drain()
                v.tensor_scalar(scol, scol, 127.0, None, ALU.mult)
                v.drain()
                v.tensor_scalar(z_st, ln2, scol, None, ALU.mult)
                v.tensor_copy(sti, z_st)
                v.tensor_tensor(e_st, z_st, sti, ALU.subtract).then_inc(er, 1)
                v.wait_ge(sg, tc + 1)
                v.tensor_tensor(cs_st, sg1, sg2, ALU.add)
                v.scalar_tensor_tensor(xq_st[b], cs_st, 0.5, sti, ALU.mult, ALU.add).then_inc(xqr, 1)
                for fc in range(16):
                    n = tc * 17 + fc
                    v.wait_ge(trd, n + 1)
                    v.tensor_copy(xsp[b][:, fc * 128:(fc + 1) * 128],
                                  pbt[fc % 2][:, 0:128]).then_inc(ted, 1)
                v.wait_ge(trd, tc * 17 + 17)
                v.tensor_copy(s.amax1_row[0:1, tc * 128:(tc + 1) * 128],
                              s.pb[1][0:1, 0:128]).then_inc(ted, 1)

        @block.scalar
        def _(sc):
            for tc in range(16):
                sc.wait_ge(rs1, tc + 1)
                sc.activation(rstd, col, AF.Sqrt).then_inc(rs2, 1)
                sc.wait_ge(er, tc + 1)
                sc.sign(sg1, e_st, bias=s.nhalfc)
                sc.sign(sg2, e_st, bias=s.halfc).then_inc(sg, 1)

        @block.tensor
        def _(t):
            for tc in range(16):
                t.wait_ge(xqr, tc + 1)
                for fc in range(16):
                    n = tc * 17 + fc
                    if n >= 2:
                        t.wait_ge(ted, n - 1)
                    t.transpose(pbt[fc % 2][:, 0:128],
                                xq_st[tc % 2][:, fc * 128:(fc + 1) * 128],
                                s.ident_f32).then_inc(trd, 1)
                if tc >= 1:
                    t.wait_ge(ted, tc * 17)
                t.transpose(s.pb[1][0:1, 0:128], amx, s.ident_f32).then_inc(trd, 1)

        @block.gpsimd
        def _(g):
            for tc in range(16):
                b = tc % 2
                g.wait_ge(x2r, tc + 1)
                g.dma_start(out=s.x2_sc[tc * 128:(tc + 1) * 128, :],
                            in_=x2_st[b]).then_inc(spd, 16)
                g.wait_ge(ted, tc * 17 + 16)
                for fc in range(16):
                    g.dma_start(out=s.xq_sc[fc, :, tc * 128:(tc + 1) * 128],
                                in_=xsp[b][:, fc * 128:(fc + 1) * 128]).then_inc(spd, 16)
            g.wait_ge(spd, 16 * 17 * 16)
    ctx.close()


def emit_c_rows(s):
    nc = s.nc
    finv = s.sem("cr_finv")
    fint = s.sem("cr_fint")
    with nc.Block() as block:
        @block.vector
        def _(v):
            v.tensor_scalar(s.amax1_row, s.amax1_row, s.m1c, 1.0 / 254.0, ALU.mult, ALU.mult)
            v.drain()
            v.drain().then_inc(finv, 1)
        @block.tensor
        def _(t):
            t.wait_ge(finv, 1)
            for i in range(4):
                t.matmul(s.pb[i][:, 0:512], s.ones_row,
                         s.amax1_row[0:1, i * 512:(i + 1) * 512], start=True, stop=True)
            t.drain().then_inc(fint, 1)
    with nc.Block() as block:
        @block.vector
        def _(v):
            v.wait_ge(fint, 1)
            for i in range(4):
                v.tensor_copy(s.c1_b[:, i * 512:(i + 1) * 512], s.pb[i][:, 0:512])


def emit_ffn_a(s):
    nc = s.nc
    xqld = s.sem("fa_xqld")
    xqld2 = s.sem("fa_xqld2")
    w1ld = s.sem("fa_w1ld")
    w2ld = s.sem("fa_w2ld")
    sg1d = s.sem("fa_sg1d")
    sg2d = s.sem("fa_sg2d")
    t1r = s.sem("fa_t1r")
    t2r = s.sem("fa_t2r")
    mmd = s.sem("fa_mmd")
    hev = s.sem("fa_hev")
    gel = s.sem("fa_gel")
    mxd = s.sem("fa_mxd")
    gsp = s.sem("fa_gsp")
    t2sp = s.sem("fa_t2sp")
    amp = s.sem("fa_amp")
    amc = s.sem("fa_amc")

    ctx = contextlib.ExitStack()
    sbuf = nc.sbuf_tensor
    xqT = ctx.enter_context(sbuf("fa_xqT", [128, 16 * T], BF16)).ap()
    w1t = [ctx.enter_context(sbuf(f"fa_w1t{i}", [128, D], F32)).ap() for i in range(2)]
    w2t = ctx.enter_context(sbuf("fa_w2t", [128, D], F32)).ap()
    sa1 = ctx.enter_context(sbuf("fa_sa1", [128, D], BF16)).ap()
    sb1 = ctx.enter_context(sbuf("fa_sb1", [128, D], BF16)).ap()
    sa2 = ctx.enter_context(sbuf("fa_sa2", [128, D], BF16)).ap()
    sb2 = ctx.enter_context(sbuf("fa_sb2", [128, D], BF16)).ap()
    t1 = [ctx.enter_context(sbuf(f"fa_t1_{i}", [128, D], BF16)).ap() for i in range(2)]
    t2 = [ctx.enter_context(sbuf(f"fa_t2_{i}", [128, D], BF16)).ap() for i in range(2)]
    h_st = [ctx.enter_context(sbuf(f"fa_h{i}", [128, T], F32)).ap() for i in range(2)]
    g_st = [ctx.enter_context(sbuf(f"fa_g{i}", [128, T], F32)).ap() for i in range(2)]
    mx = ctx.enter_context(sbuf("fa_mx", [128, T], F32)).ap()
    mn = ctx.enter_context(sbuf("fa_mn", [128, T], F32)).ap()
    NDH = 64

    with nc.Block() as block:
        @block.sync
        def _(sy):
            for dh in range(2):
                sy.dma_start(out=w1t[dh],
                             in_=s.w1T[:, dh * 128:(dh + 1) * 128].rearrange(
                                 "(fc f) o -> f fc o", f=128)).then_inc(w1ld, 16)
            for fc in range(8, 16):
                sy.dma_start(out=xqT[:, fc * T:(fc + 1) * T],
                             in_=s.xq_sc[fc]).then_inc(xqld, 16)
            for dh in range(2, NDH):
                sy.wait_ge(sg1d, dh - 1)
                sy.dma_start(out=w1t[dh % 2],
                             in_=s.w1T[:, dh * 128:(dh + 1) * 128].rearrange(
                                 "(fc f) o -> f fc o", f=128)).then_inc(w1ld, 16)
            sy.wait_ge(gsp, 64 * NDH)
            sy.wait_ge(t2sp, 16 * NDH)

        @block.scalar
        def _(sc):
            def signs(dh):
                b = dh % 2
                sc.wait_ge(w1ld, 16 * (dh + 1))
                if dh >= 1:
                    sc.wait_ge(t1r, dh)
                sc.activation(sa1, w1t[b], AF.Sign, bias=s.nhalfc, scale=s.s1col)
                sc.activation(sb1, w1t[b], AF.Sign,
                              bias=s.halfc, scale=s.s1col).then_inc(sg1d, 1)
                sc.wait_ge(w2ld, 16 * (dh + 1))
                if dh >= 1:
                    sc.wait_ge(t2r, dh)
                sc.activation(sa2, w2t, AF.Sign, bias=s.nhalfc, scale=s.s2col)
                sc.activation(sb2, w2t, AF.Sign,
                              bias=s.halfc, scale=s.s2col).then_inc(sg2d, 1)
                if dh + 1 < NDH:
                    sc.wait_ge(sg2d, dh + 1)
                    sc.dma_start(out=w2t,
                                 in_=s.w2T[(dh + 1) * 128:(dh + 2) * 128, :]
                                 ).then_inc(w2ld, 16)
            sc.wait_ge(s.sems["mfin"], 3)
            sc.dma_start(out=w2t, in_=s.w2T[0:128, :]).then_inc(w2ld, 16)
            signs(0)
            for dh in range(NDH):
                if dh + 1 < NDH:
                    signs(dh + 1)
                sc.wait_ge(hev, dh + 1)
                if dh >= 2:
                    sc.wait_ge(gsp, 64 * (dh - 1))
                sc.activation(g_st[dh % 2], h_st[dh % 2], AF.Gelu).then_inc(gel, 1)

        @block.vector
        def _(v):
            for dh in range(NDH):
                b = dh % 2
                v.wait_ge(sg1d, dh + 1)
                v.tensor_tensor(t1[b], sa1, sb1, ALU.add).then_inc(t1r, 1)
                v.wait_ge(sg2d, dh + 1)
                if dh >= 2:
                    v.wait_ge(t2sp, 16 * (dh - 1))
                v.tensor_tensor(t2[b], sa2, sb2, ALU.add).then_inc(t2r, 1)
                v.wait_ge(mmd, dh + 1)
                pset = 4 * (dh % 2)
                for i in range(4):
                    tth = v.tensor_tensor(h_st[b][:, i * 512:(i + 1) * 512],
                                          s.pb[pset + i], s.c1_b[:, i * 512:(i + 1) * 512],
                                          ALU.mult)
                tth.then_inc(hev, 1)
                if dh >= 1:
                    v.wait_ge(gel, dh)
                    gp = g_st[(dh - 1) % 2]
                    if dh == 1:
                        v.tensor_copy(mx, gp)
                        v.tensor_copy(mn, gp).then_inc(mxd, 1)
                    else:
                        v.tensor_tensor(mx, mx, gp, ALU.max)
                        v.tensor_tensor(mn, mn, gp, ALU.min).then_inc(mxd, 1)
            v.wait_ge(gel, NDH)
            v.tensor_tensor(mx, mx, g_st[(NDH - 1) % 2], ALU.max)
            v.tensor_tensor(mn, mn, g_st[(NDH - 1) % 2], ALU.min).then_inc(mxd, 1)
            v.tensor_scalar(mn, mn, -1.0, None, ALU.mult)
            v.tensor_tensor(mx, mx, mn, ALU.max)
            v.drain().then_inc(amc, 1)
            v.wait_ge(amp, 1)
            for i in range(16):
                v.tensor_reduce(s.am2t[:, i:i + 1],
                                s.pb[i // 4][:, (i % 4) * 128:(i % 4 + 1) * 128],
                                AX.X, ALU.max)
            v.drain()
            v.tensor_scalar(s.am2t, s.am2t, EPS_Q, None, ALU.max)
            v.drain()
            v.tensor_tensor(s.c2_cols, s.am2t, s.m2col.broadcast_to([128, 16]), ALU.mult)
            v.drain()
            v.tensor_scalar(s.c2_cols, s.c2_cols, 1.0 / 254.0, None, ALU.mult)
            v.drain()

        @block.tensor
        def _(t):
            t.wait_ge(xqld, 16 * 8)
            t.wait_ge(xqld2, 16 * 8)
            for dh in range(NDH):
                b = dh % 2
                t.wait_ge(t1r, dh + 1)
                if dh >= 2:
                    t.wait_ge(hev, dh - 1)
                pset = 4 * (dh % 2)
                for fc in range(16):
                    for i in range(4):
                        mmf = t.matmul(s.pb[pset + i], t1[b][:, fc * 128:(fc + 1) * 128],
                                       xqT[:, fc * T + i * 512: fc * T + (i + 1) * 512],
                                       start=(fc == 0), stop=(fc == 15))
                mmf.then_inc(mmd, 1)
            t.wait_ge(amc, 1)
            for i in range(16):
                t.transpose(s.pb[i // 4][:, (i % 4) * 128:(i % 4 + 1) * 128],
                            mx[:, i * 128:(i + 1) * 128], s.ident_f32)
            t.drain().then_inc(amp, 1)

        @block.gpsimd
        def _(g):
            for fc in range(8):
                g.dma_start(out=xqT[:, fc * T:(fc + 1) * T],
                            in_=s.xq_sc[fc]).then_inc(xqld2, 16)
            for dh in range(NDH):
                b = dh % 2
                g.wait_ge(t2r, dh + 1)
                g.dma_start(out=s.t2_sc[dh * 128:(dh + 1) * 128, :], in_=t2[b]).then_inc(t2sp, 16)
                g.wait_ge(mxd, dh + 1)
                for tq in range(4):
                    g.dma_start(out=s.g_sc[tq, dh * 128:(dh + 1) * 128, :],
                                in_=g_st[b][:, tq * 512:(tq + 1) * 512]).then_inc(gsp, 16)
            g.wait_ge(gsp, 64 * NDH)
            g.wait_ge(t2sp, 16 * NDH)
    ctx.close()

    fpp = s.sem("fa_fpp")
    fpc = s.sem("fa_fpc")
    am16 = s.qnw_b[0:16, 0:128]
    with nc.Block() as block:
        @block.tensor
        def _(t):
            t.transpose(s.pb[2][0:16, 0:128], s.am2t, s.ident_f32).then_inc(fpp, 1)
        @block.vector
        def _(v):
            v.wait_ge(fpp, 1)
            v.tensor_copy(am16, s.pb[2][0:16, 0:128]).then_inc(fpc, 1)
        @block.scalar
        def _(sc):
            sc.wait_ge(fpc, 1)
            sc.dma_start(out=s.amax2_row,
                         in_=am16).then_inc(fpc, 16)
            sc.wait_ge(fpc, 17)
    f2v = s.sem("fa_f2v")
    f2t = s.sem("fa_f2t")
    with nc.Block() as block:
        @block.vector
        def _(v):
            v.reciprocal(s.amax2_row, s.amax2_row)
            v.drain()
            v.tensor_scalar(s.amax2_row, s.amax2_row, 127.0, None, ALU.mult)
            v.drain()
            v.drain().then_inc(f2v, 1)
        @block.tensor
        def _(t):
            t.wait_ge(f2v, 1)
            for i in range(4):
                t.matmul(s.pb[i][:, 0:512], s.ones_row,
                         s.amax2_row[0:1, i * 512:(i + 1) * 512], start=True, stop=True)
            t.drain().then_inc(f2t, 1)
    with nc.Block() as block:
        @block.vector
        def _(v):
            v.wait_ge(f2t, 1)
            for i in range(4):
                v.tensor_copy(s.s2_b[:, i * 512:(i + 1) * 512], s.pb[i][:, 0:512])


def emit_ffn_b(s):
    nc = s.nc
    gld = s.sem("fb_gld")
    zr = s.sem("fb_zr")
    sgd = s.sem("fb_sgd")
    hqr = s.sem("fb_hqr")
    hqsp = s.sem("fb_hqsp")
    t2ld = s.sem("fb_t2ld")
    hqld = s.sem("fb_hqld")
    x2ld = s.sem("fb_x2ld")
    mmh = s.sem("fb_mmh")
    yev = s.sem("fb_yev")
    yout = s.sem("fb_yout")
    NDH = 64
    NJ = 256

    ctx = contextlib.ExitStack()
    sbuf = nc.sbuf_tensor
    g_ld = [ctx.enter_context(sbuf(f"fb_g{i}", [128, 512], F32)).ap() for i in range(4)]
    z_st = ctx.enter_context(sbuf("fb_z", [128, 512], F32)).ap()
    sti = [ctx.enter_context(sbuf(f"fb_sti{i}", [128, 512], I32)).ap() for i in range(2)]
    e_st = [ctx.enter_context(sbuf(f"fb_e{i}", [128, 512], F32)).ap() for i in range(2)]
    sg1 = [ctx.enter_context(sbuf(f"fb_sg1{i}", [128, 512], BF16)).ap() for i in range(2)]
    sg2 = [ctx.enter_context(sbuf(f"fb_sg2{i}", [128, 512], BF16)).ap() for i in range(2)]
    cs_st = ctx.enter_context(sbuf("fb_cs", [128, 512], BF16)).ap()
    hq_st = [ctx.enter_context(sbuf(f"fb_hq{i}", [128, 512], BF16)).ap() for i in range(2)]
    T2 = ctx.enter_context(sbuf("fb_T2", [128, 64 * 512], BF16)).ap()
    hqb = [ctx.enter_context(sbuf(f"fb_hqb{i}", [128, 16 * 512], BF16)).ap() for i in range(4)]
    x2s = [ctx.enter_context(sbuf(f"fb_x2s{i}", [128, 4 * 512], F32)).ap() for i in range(2)]
    y_st = [ctx.enter_context(sbuf(f"fb_y{i}", [128, 4 * 512], F32)).ap() for i in range(2)]
    EV_AT = {100: 0, 155: 1, 220: 2}

    with nc.Block() as block:
        @block.sync
        def _(sy):
            for j in range(NJ):
                tq, dh = j // 64, j % 64
                if j >= 4:
                    sy.wait_ge(zr, j - 3)
                sy.dma_start(out=g_ld[j % 4],
                             in_=s.g_sc[tq, dh * 128:(dh + 1) * 128, :]).then_inc(gld, 16)
            sy.wait_ge(yout, 16 * 64)

        @block.scalar
        def _(sc):
            def x2_load(n):
                ocg, tcb = n // 4, n % 4
                if n >= 2:
                    sc.wait_ge(yev, 4 * (n - 1))
                for tci in range(4):
                    tc = tcb * 4 + tci
                    sc.dma_start(out=x2s[n % 2][:, tci * 512:(tci + 1) * 512],
                                 in_=s.x2_sc[tc * 128:(tc + 1) * 128,
                                             ocg * 512:(ocg + 1) * 512]).then_inc(x2ld, 16)
            def hq_halves(n):
                tcb = n % 4
                srcq = s.hq_sc[:, tcb * 512:(tcb + 1) * 512].rearrange(
                    "(dc p) o -> p dc o", p=128)
                for Hq in range(4):
                    Q = 4 * n + Hq
                    if Q >= 4:
                        sc.wait_ge(mmh, Q - 3)
                    for part in range(2):
                        dc0 = Hq * 16 + part * 8
                        sc.dma_start(out=hqb[Q % 4][:, part * 8 * 512:(part + 1) * 8 * 512],
                                     in_=srcq[:, dc0:dc0 + 8, :]).then_inc(hqld, 16)
            X2_AT = {130: 2, 200: 3}
            x2_load(0)
            x2_load(1)
            for j in range(NJ):
                sc.wait_ge(zr, j + 1)
                sc.sign(sg1[j % 2], e_st[j % 2], bias=s.nhalfc)
                sc.sign(sg2[j % 2], e_st[j % 2], bias=s.halfc).then_inc(sgd, 1)
                if j % 64 == 63:
                    tq = j // 64
                    sc.wait_ge(hqsp, 16 * 64 * (tq + 1))
                    hq_halves(tq)
                if j in X2_AT:
                    x2_load(X2_AT[j])
            for n in range(4, 16):
                hq_halves(n)
                x2_load(n)

        @block.gpsimd
        def _(g):
            def t2_parts(ocg):
                src = s.t2_sc[:, ocg * 512:(ocg + 1) * 512].rearrange(
                    "(dc p) o -> p dc o", p=128)
                for part in range(8):
                    g.dma_start(out=T2[:, part * 8 * 512:(part + 1) * 8 * 512],
                                in_=src[:, part * 8:(part + 1) * 8, :]).then_inc(t2ld, 16)
            def y_out(n):
                ocg, tcb = n // 4, n % 4
                g.wait_ge(yev, 4 * (n + 1))
                for tci in range(4):
                    tc = tcb * 4 + tci
                    g.dma_start(out=s.y[tc * 128:(tc + 1) * 128,
                                        ocg * 512:(ocg + 1) * 512],
                                in_=y_st[n % 2][:, tci * 512:(tci + 1) * 512]).then_inc(yout, 16)

            t2_parts(0)
            for tq in range(4):
                for dh in range(NDH):
                    j = tq * 64 + dh
                    g.wait_ge(hqr, j + 1)
                    g.dma_start(out=s.hq_sc[dh * 128:(dh + 1) * 128,
                                            tq * 512:(tq + 1) * 512],
                                in_=hq_st[j % 2]).then_inc(hqsp, 16)
                if tq >= 1:
                    y_out(tq - 1)
            for n in range(3, 16):
                ocg, tcb = n // 4, n % 4
                if tcb == 0:
                    g.wait_ge(mmh, 16 * ocg)
                    t2_parts(ocg)
                y_out(n - 1 if n > 3 else 3)
            y_out(15)
            g.wait_ge(yout, 16 * 64)

        @block.tensor
        def _(t):
            for n in range(16):
                ocg, tcb = n // 4, n % 4
                pset = 4 * (tcb % 2)
                t.wait_ge(t2ld, 16 * 8 * (ocg + 1))
                if n >= 2:
                    t.wait_ge(yev, 4 * (n - 1))
                for Hq in range(4):
                    Q = 4 * n + Hq
                    t.wait_ge(hqld, 32 * min(Q + 2, 4 * (n + 1)))
                    for dq in range(16):
                        dh = Hq * 16 + dq
                        for tci in range(4):
                            mmb = t.matmul(s.pb[pset + tci],
                                           hqb[Q % 4][:, dq * 512 + tci * 128:
                                                 dq * 512 + (tci + 1) * 128],
                                           T2[:, dh * 512:(dh + 1) * 512],
                                           start=(dh == 0), stop=(dh == NDH - 1))
                    mmb.then_inc(mmh, 1)

        @block.vector
        def _(v):
            def evac(n):
                tcb = n % 4
                pset = 4 * (tcb % 2)
                v.wait_ge(mmh, 4 * (n + 1))
                v.wait_ge(x2ld, 16 * 4 * (n + 1))
                if n >= 2:
                    v.wait_ge(yout, 16 * 4 * (n - 1))
                for tci in range(4):
                    tc = tcb * 4 + tci
                    v.scalar_tensor_tensor(y_st[n % 2][:, tci * 512:(tci + 1) * 512],
                                           s.pb[pset + tci], s.c2_cols[:, tc:tc + 1],
                                           x2s[n % 2][:, tci * 512:(tci + 1) * 512],
                                           ALU.mult, ALU.add).then_inc(yev, 1)

            def zchain(j):
                tq = j // 64
                v.wait_ge(gld, 16 * (j + 1))
                v.tensor_tensor(z_st, g_ld[j % 4],
                                s.s2_b[:, tq * 512:(tq + 1) * 512], ALU.mult)
                v.tensor_copy(sti[j % 2], z_st)
                v.tensor_tensor(e_st[j % 2], z_st, sti[j % 2], ALU.subtract).then_inc(zr, 1)

            nev = 0
            zchain(0)
            for j in range(NJ):
                if j + 1 < NJ:
                    zchain(j + 1)
                v.wait_ge(sgd, j + 1)
                v.tensor_tensor(cs_st, sg1[j % 2], sg2[j % 2], ALU.add)
                if j >= 2:
                    v.wait_ge(hqsp, 16 * (j - 1))
                v.scalar_tensor_tensor(hq_st[j % 2], cs_st, 0.5, sti[j % 2],
                                       ALU.mult, ALU.add).then_inc(hqr, 1)
                if j in EV_AT:
                    evac(EV_AT[j])
                    nev += 1
            for n in range(nev, 16):
                evac(n)
    ctx.close()


# version 06
# speedup vs baseline: 1.1171x; 1.0050x over previous
import contextlib
import numpy as np
import ml_dtypes
import concourse.bass as bass
import concourse.mybir as mybir
from concourse.bass_utils import run_bass_kernel_spmd

F32 = mybir.dt.float32
BF16 = mybir.dt.bfloat16
I32 = mybir.dt.int32
AF = mybir.ActivationFunctionType
ALU = mybir.AluOpType
AX = mybir.AxisListType

D = 2048
L = 4096
T = 2048
NH = 16
NKV = 8
HD = 128
DH = 8192
SCALE = float(HD) ** -0.5
EPS_LN = 1e-5
EPS_RMS = 1e-6
EPS_Q = 1e-5
NEG = -1.0e30

DEBUG = False
STOP_AFTER = None
RACE_DETECT = False
PP_LIMIT = None
PP_NO_PRE = False
PP_ORDER = ("k", "v", "qa", "qb")


def _bf(x):
    return np.asarray(x).astype(ml_dtypes.bfloat16)


def make_mask(h):
    m = np.zeros((8, 128, 1024), dtype=np.float32)
    if h == 1:
        for c in range(8):
            kk = c * 128 + np.arange(128)[:, None]
            qq = np.arange(1024)[None, :]
            m[c] = np.where(kk <= qq, NEG, 0.0)
    return _bf(m)


def kernel(L0_attn_hidden_states, L0_attn_Wq, L0_attn_Wk, L0_attn_Wv, L0_attn_Wo,
           L0_attn_q_norm_w, L0_attn_k_norm_w, L0_ffn_w1, L0_ffn_w2):
    x = np.ascontiguousarray(np.asarray(L0_attn_hidden_states, dtype=np.float32))
    wqT = np.ascontiguousarray(np.asarray(L0_attn_Wq, dtype=np.float32).T)
    wkT = np.ascontiguousarray(np.asarray(L0_attn_Wk, dtype=np.float32).T)
    wvT = np.ascontiguousarray(np.asarray(L0_attn_Wv, dtype=np.float32).T)
    woT = np.ascontiguousarray(np.asarray(L0_attn_Wo, dtype=np.float32).T)
    w1T = np.ascontiguousarray(np.asarray(L0_ffn_w1, dtype=np.float32).T)
    w2T = np.ascontiguousarray(np.asarray(L0_ffn_w2, dtype=np.float32).T)
    qnw = np.ascontiguousarray(np.asarray(L0_attn_q_norm_w, dtype=np.float32).reshape(1, HD))
    knw = np.ascontiguousarray(np.asarray(L0_attn_k_norm_w, dtype=np.float32).reshape(1, HD))
    ident_f32 = np.eye(128, dtype=np.float32)
    ident_bf = _bf(np.eye(128))
    masks = [make_mask(0), make_mask(1)]

    nc = build_nc()
    in_maps = []
    for c in range(8):
        b, h = c // 2, c % 2
        own = np.ascontiguousarray(x[b, h * T:(h + 1) * T, :])
        oth = np.ascontiguousarray(x[b, (1 - h) * T:(2 - h) * T, :])
        in_maps.append({
            "x_own": own, "x_oth": oth,
            "wqT": wqT, "wkT": wkT, "wvT": wvT, "woT": woT,
            "w1T": w1T, "w2T": w2T,
            "qnw_row": qnw, "knw_row": knw,
            "ident_f32": ident_f32, "ident_bf": ident_bf,
            "mask": masks[h],
        })
    res = run_bass_kernel_spmd(nc, in_maps, core_ids=list(range(8)))
    out = np.empty((4, L, D), dtype=np.float32)
    for c in range(8):
        b, h = c // 2, c % 2
        out[b, h * T:(h + 1) * T, :] = res.results[c]["y"]
    return out


class NS:
    pass


def build_nc():
    nc = bass.Bass(trn_type="TRN2", target_bir_lowering=False,
                   detect_race_conditions=RACE_DETECT)
    s = NS()
    s.nc = nc
    skind = "ExternalOutput" if DEBUG else "Internal"

    s.x_own = nc.dram_tensor("x_own", [T, D], F32, kind="ExternalInput").ap()
    s.x_oth = nc.dram_tensor("x_oth", [T, D], F32, kind="ExternalInput").ap()
    s.wqT = nc.dram_tensor("wqT", [D, D], F32, kind="ExternalInput").ap()
    s.wkT = nc.dram_tensor("wkT", [D, NKV * HD], F32, kind="ExternalInput").ap()
    s.wvT = nc.dram_tensor("wvT", [D, NKV * HD], F32, kind="ExternalInput").ap()
    s.woT = nc.dram_tensor("woT", [D, D], F32, kind="ExternalInput").ap()
    s.w1T = nc.dram_tensor("w1T", [D, DH], F32, kind="ExternalInput").ap()
    s.w2T = nc.dram_tensor("w2T", [DH, D], F32, kind="ExternalInput").ap()
    s.qnw_row_d = nc.dram_tensor("qnw_row", [1, HD], F32, kind="ExternalInput").ap()
    s.knw_row_d = nc.dram_tensor("knw_row", [1, HD], F32, kind="ExternalInput").ap()
    s.ident_f32_d = nc.dram_tensor("ident_f32", [128, 128], F32, kind="ExternalInput").ap()
    s.ident_bf_d = nc.dram_tensor("ident_bf", [128, 128], BF16, kind="ExternalInput").ap()
    s.mask_d = nc.dram_tensor("mask", [8, 128, 1024], BF16, kind="ExternalInput").ap()

    s.y = nc.dram_tensor("y", [T, D], F32, kind="ExternalOutput").ap()

    def scr(name, shape, dtype):
        return nc.dram_tensor(name, shape, dtype, kind=skind).ap()
    s.lnT_hi = scr("lnT_hi", [16, 128, L], BF16)
    s.lnT_lo = scr("lnT_lo", [16, 128, L], BF16)
    s.kn_hi = scr("kn_hi", [NKV, 32, 128, 128], BF16)
    s.kn_lo = scr("kn_lo", [NKV, 32, 128, 128], BF16)
    s.vt_hi = scr("vt_hi", [L, NKV * HD], BF16)
    s.vt_lo = scr("vt_lo", [L, NKV * HD], BF16)
    s.qn_hi = scr("qn_hi", [NH, 16, 128, 128], BF16)
    s.qn_lo = scr("qn_lo", [NH, 16, 128, 128], BF16)
    s.wo_hi = scr("wo_hi", [D, D], BF16)
    s.wo_lo = scr("wo_lo", [D, D], BF16)
    s.y_sc = scr("y_sc", [T, D], F32)
    s.x2_sc = scr("x2_sc", [T, D], F32)
    s.xq_sc = scr("xq_sc", [16, 128, T], BF16)
    s.g_sc = scr("g_sc", [4, DH, 512], F32)
    s.hq_sc = scr("hq_sc", [DH, T], BF16)
    s.t2_sc = scr("t2_sc", [DH, D], BF16)
    s.dbg_m = scr("dbg_m", [1, 4], F32)

    def sb(name, shape, dtype):
        return nc.alloc_sbuf_tensor(name, shape, dtype).ap()
    s.ident_f32 = sb("c_ident32", [128, 128], F32)
    s.ident_bf = sb("c_identbf", [128, 128], BF16)
    s.qnw_row = sb("c_qnw", [1, HD], F32)
    s.knw_row = sb("c_knw", [1, HD], F32)
    s.qnw_b = sb("c_qnwb", [128, HD], F32)
    s.knw_b = sb("c_knwb", [128, HD], F32)
    s.ones_bf = sb("c_onesbf", [128, 1], BF16)
    s.ones_col = sb("c_onescol", [128, 1], F32)
    s.ones_row = sb("c_onesrow", [1, 128], F32)
    s.halfc = sb("c_halfc", [128, 1], F32)
    s.nhalfc = sb("c_nhalfc", [128, 1], F32)
    s.m1c = sb("c_m1c", [1, 1], F32)
    s.m2c = sb("c_m2c", [1, 1], F32)
    s.s1col = sb("c_s1col", [128, 1], F32)
    s.s2col = sb("c_s2col", [128, 1], F32)
    s.m2col = sb("c_m2col", [128, 1], F32)
    s.mid_ctx = contextlib.ExitStack()
    s.m_acc1 = s.mid_ctx.enter_context(nc.sbuf_tensor("c_macc1", [128, 1], F32)).ap()
    s.m_acc2 = s.mid_ctx.enter_context(nc.sbuf_tensor("c_macc2", [128, 1], F32)).ap()
    s.m_tmpc = s.mid_ctx.enter_context(nc.sbuf_tensor("c_mtmpc", [128, 1], F32)).ap()
    s.pre_buf = s.mid_ctx.enter_context(nc.sbuf_tensor("c_prebuf", [128, 4096], F32)).ap()
    s.dbg_st = s.mid_ctx.enter_context(nc.sbuf_tensor("c_dbgst", [1, 4], F32)).ap()

    s.pb = [nc.alloc_psum_tensor(f"pb{i}", [128, 512], F32).ap() for i in range(8)]

    s.sems = {}
    def sem(name):
        h = nc.ctx.enter_context(nc.semaphore(name))
        s.sems[name] = h
        return h
    s.sem = sem

    s.pre_gld = sem("pre_gld")
    s.pre_red = sem("pre_red")
    s.pre_done = sem("pre_done")
    s.pre_plan = []
    for i in range(D // 128):
        for j in range(DH // 2048):
            s.pre_plan.append((s.w1T, i * 128, j * 2048, s.m_acc1))
    for i in range(DH // 128):
        s.pre_plan.append((s.w2T, i * 128, 0, s.m_acc2))
    s.pre_n = 0
    s.pre_nv = 0

    emit_init(s)
    if STOP_AFTER == "init":
        return nc
    emit_p1_ln(s)
    if STOP_AFTER == "ln":
        return nc
    for kind in PP_ORDER:
        emit_proj_pass(s, kind)
        if STOP_AFTER == kind:
            return nc
    emit_wo_split(s)
    emit_m_finalize(s)
    if STOP_AFTER == "p1":
        return nc
    s.mid_ctx.close()
    s.s2_b = sb("c_s2b", [128, T], F32)
    s.c2_cols = sb("c_c2cols", [128, 16], F32)
    s.am2t = sb("c_am2t", [128, 16], F32)
    late_ctx = contextlib.ExitStack()
    s.amax1_row = late_ctx.enter_context(nc.sbuf_tensor("c_am1row", [1, T], F32)).ap()
    s.amax2_row = late_ctx.enter_context(nc.sbuf_tensor("c_am2row", [1, T], F32)).ap()
    s.c1_b = late_ctx.enter_context(nc.sbuf_tensor("c_c1b", [128, T], F32)).ap()
    emit_p2(s)
    if STOP_AFTER == "p2":
        return nc
    emit_p3(s)
    emit_c_rows(s)
    if STOP_AFTER == "p3":
        return nc
    emit_ffn_a(s)
    if STOP_AFTER == "ffa":
        late_ctx.close()
        return nc
    late_ctx.close()
    emit_ffn_b(s)
    return nc


def feed_prepass(s, g, nchunks):
    for _ in range(nchunks):
        if s.pre_n >= len(s.pre_plan):
            return
        w, r0, c0, acc = s.pre_plan[s.pre_n]
        n = s.pre_n
        if n >= 2:
            g.wait_ge(s.pre_red, n - 1)
        buf = s.pre_buf[:, (n % 2) * 2048:(n % 2 + 1) * 2048]
        g.dma_start(out=buf, in_=w[r0:r0 + 128, c0:c0 + 2048]).then_inc(s.pre_gld, 16)
        s.pre_n += 1


def feed_prepass_v(s, v, nchunks):
    for _ in range(nchunks):
        if s.pre_nv >= len(s.pre_plan):
            return
        w, r0, c0, acc = s.pre_plan[s.pre_nv]
        n = s.pre_nv
        v.wait_ge(s.pre_gld, 16 * (n + 1))
        buf = s.pre_buf[:, (n % 2) * 2048:(n % 2 + 1) * 2048]
        v.tensor_reduce(s.m_tmpc, buf, AX.X, ALU.add, apply_absolute_value=True)
        v.drain()
        v.tensor_tensor(acc, acc, s.m_tmpc, ALU.add).then_inc(s.pre_red, 1)
        s.pre_nv += 1
        if s.pre_nv == len(s.pre_plan):
            v.drain().then_inc(s.pre_done, 1)


def emit_init(s):
    nc = s.nc
    ld = s.sem("init_ld")
    ldv = s.sem("init_ldv")
    ldt = s.sem("init_ldt")
    with nc.Block() as block:
        @block.sync
        def _(sy):
            sy.dma_start(out=s.ident_f32, in_=s.ident_f32_d).then_inc(ld, 16)
            sy.dma_start(out=s.ident_bf, in_=s.ident_bf_d).then_inc(ld, 16)
            sy.dma_start(out=s.qnw_row, in_=s.qnw_row_d).then_inc(ld, 16)
            sy.dma_start(out=s.knw_row, in_=s.knw_row_d).then_inc(ld, 16)
            sy.wait_ge(ld, 16 * 4)
        @block.vector
        def _(v):
            v.memset(s.ones_bf, 1.0)
            v.memset(s.ones_col, 1.0)
            v.memset(s.ones_row, 1.0)
            v.memset(s.halfc, 0.5)
            v.memset(s.nhalfc, -0.5)
            v.memset(s.m_acc1, 0.0)
            v.memset(s.m_acc2, 0.0)
            v.memset(s.m_tmpc, 0.0)
            v.drain().then_inc(ldv, 1)
    with nc.Block() as block:
        @block.tensor
        def _(t):
            t.wait_ge(ld, 16 * 4)
            t.wait_ge(ldv, 1)
            t.matmul(s.pb[0][:, 0:HD], s.ones_row, s.qnw_row, start=True, stop=True)
            t.matmul(s.pb[1][:, 0:HD], s.ones_row, s.knw_row, start=True, stop=True).then_inc(ldt, 1)
        @block.vector
        def _(v):
            v.wait_ge(ldt, 1)
            v.tensor_copy(s.qnw_b, s.pb[0][:, 0:HD])
            v.tensor_copy(s.knw_b, s.pb[1][:, 0:HD])


def emit_p1_ln(s):
    nc = s.nc
    xld = s.sem("l_xld")
    lnr = s.sem("l_lnr")
    rs1 = s.sem("l_rs1")
    rs2 = s.sem("l_rs2")
    trd = s.sem("l_trd")
    evd = s.sem("l_evd")
    spd = s.sem("l_spd")
    ctx = contextlib.ExitStack()
    sbuf = nc.sbuf_tensor
    x_st = [ctx.enter_context(sbuf(f"l_x{i}", [128, D], F32)).ap() for i in range(2)]
    ln_st = [ctx.enter_context(sbuf(f"l_ln{i}", [128, D], F32)).ap() for i in range(2)]
    hi_st = [ctx.enter_context(sbuf(f"l_hi{i}", [128, D], BF16)).ap() for i in range(2)]
    lo_st = [ctx.enter_context(sbuf(f"l_lo{i}", [128, D], BF16)).ap() for i in range(2)]
    xsq = ctx.enter_context(sbuf("l_xsq", [128, D], F32)).ap()
    bn2 = ctx.enter_context(sbuf("l_bn2", [128, 2], F32)).ap()
    q2c = ctx.enter_context(sbuf("l_q2c", [128, 1], F32)).ap()
    veps = ctx.enter_context(sbuf("l_veps", [128, 1], F32)).ap()
    rvar = ctx.enter_context(sbuf("l_rvar", [128, 1], F32)).ap()
    rstd = ctx.enter_context(sbuf("l_rstd", [128, 1], F32)).ap()
    NTC = 32

    with nc.Block() as block:
        @block.sync
        def _(sy):
            for tc in range(NTC):
                src = s.x_own if tc < 16 else s.x_oth
                r = (tc % 16) * 128
                if tc >= 2:
                    sy.wait_ge(lnr, tc - 1)
                sy.dma_start(out=x_st[tc % 2], in_=src[r:r + 128, :]).then_inc(xld, 16)
            sy.wait_ge(spd, 16 * 32 * NTC)

        @block.vector
        def _(v):
            for tc in range(NTC):
                b = tc % 2
                v.wait_ge(xld, 16 * (tc + 1))
                v.tensor_reduce(bn2[:, 0:1], x_st[b], AX.X, ALU.add)
                v.tensor_tensor(xsq, x_st[b], x_st[b], ALU.mult)
                v.tensor_reduce(q2c, xsq, AX.X, ALU.add)
                v.drain()
                v.tensor_scalar(bn2[:, 0:1], bn2[:, 0:1], 1.0 / D, None, ALU.mult)
                v.drain()
                v.tensor_tensor(veps, bn2[:, 0:1], bn2[:, 0:1], ALU.mult)
                v.drain()
                v.scalar_tensor_tensor(veps, q2c, 1.0 / D, veps, ALU.mult, ALU.subtract)
                v.drain()
                v.tensor_scalar(veps, veps, EPS_LN, None, ALU.add)
                v.drain()
                v.reciprocal(rvar, veps).then_inc(rs1, 1)
                if tc >= 2:
                    v.wait_ge(trd, 16 * (tc - 1))
                    v.wait_ge(spd, 16 * 32 * (tc - 1))
                v.wait_ge(rs2, tc + 1)
                v.tensor_scalar(ln_st[b], x_st[b], bn2[:, 0:1], None, ALU.subtract)
                v.tensor_scalar(ln_st[b], ln_st[b], rstd, None, ALU.mult).then_inc(lnr, 1)
                for fc in range(16):
                    pbt = s.pb[6 + (fc % 2)]
                    v.wait_ge(trd, tc * 16 + fc + 1)
                    v.tensor_copy(hi_st[b][:, fc * 128:(fc + 1) * 128], pbt[:, 0:128])
                    v.tensor_tensor(lo_st[b][:, fc * 128:(fc + 1) * 128], pbt[:, 0:128],
                                    hi_st[b][:, fc * 128:(fc + 1) * 128],
                                    ALU.subtract).then_inc(evd, 1)
                if tc % 8 == 7:
                    feed_prepass_v(s, v, 4)

        @block.scalar
        def _(sc):
            def spill(tc):
                bb = tc % 2
                sc.wait_ge(evd, 16 * (tc + 1))
                for fc in range(16):
                    sc.dma_start(out=s.lnT_hi[fc, :, tc * 128:(tc + 1) * 128],
                                 in_=hi_st[bb][:, fc * 128:(fc + 1) * 128]).then_inc(spd, 16)
                    sc.dma_start(out=s.lnT_lo[fc, :, tc * 128:(tc + 1) * 128],
                                 in_=lo_st[bb][:, fc * 128:(fc + 1) * 128]).then_inc(spd, 16)
            for tc in range(NTC):
                sc.wait_ge(rs1, tc + 1)
                sc.activation(rstd, rvar, AF.Sqrt).then_inc(rs2, 1)
                if tc >= 1:
                    spill(tc - 1)
            spill(NTC - 1)

        @block.tensor
        def _(t):
            for tc in range(NTC):
                t.wait_ge(lnr, tc + 1)
                for fc in range(16):
                    pbt = s.pb[6 + (fc % 2)]
                    if tc * 16 + fc >= 2:
                        t.wait_ge(evd, tc * 16 + fc - 1)
                    t.transpose(pbt[:, 0:128], ln_st[tc % 2][:, fc * 128:(fc + 1) * 128],
                                s.ident_f32).then_inc(trd, 1)

        @block.gpsimd
        def _(g):
            feed_prepass(s, g, 16)
    ctx.close()


def emit_proj_pass(s, kind):
    nc = s.nc
    is_q = kind in ("qa", "qb")
    is_v = kind == "v"
    NTC = 16 if is_q else 32
    if PP_LIMIT is not None:
        NTC = min(NTC, PP_LIMIT)
        NTC = max(4, NTC)
    w_dram = {"k": s.wkT, "v": s.wvT,
              "qa": s.wqT[:, 0:1024], "qb": s.wqT[:, 1024:2048]}[kind]
    head_off = 8 if kind == "qb" else 0
    nrm_b = s.qnw_b if is_q else s.knw_b
    dst_hi = s.qn_hi if is_q else s.kn_hi
    dst_lo = s.qn_lo if is_q else s.kn_lo

    names = ["wld", "wsp", "lnld", "pdone", "pfree", "rs1", "rs2",
             "knr", "trd", "ted", "ksr", "spd"]
    if not hasattr(s, "pp_sems"):
        s.pp_sems = {n: s.sem("pp_" + n) for n in names}
        s.pp_base = {n: 0 for n in names}
    B = dict(s.pp_base)
    SM = s.pp_sems
    wld, wsp, lnld, pdone, pfree, rs1, rs2, knr, trd, ted, ksr, spd = (
        SM[n] for n in names)

    p = kind + "_"
    ctx = contextlib.ExitStack()
    sbuf = nc.sbuf_tensor
    w_hi = ctx.enter_context(sbuf(p + "whi", [128, 16 * 1024], BF16)).ap()
    w_lo = ctx.enter_context(sbuf(p + "wlo", [128, 16 * 1024], BF16)).ap()
    wtmp = [ctx.enter_context(sbuf(p + f"wtmp{i}", [128, 1024], F32)).ap() for i in range(2)]
    lnb_hi = [ctx.enter_context(sbuf(p + f"lnhi{i}", [128, 16 * 512], BF16)).ap() for i in range(2)]
    lnb_lo = [ctx.enter_context(sbuf(p + f"lnlo{i}", [128, 16 * 512], BF16)).ap() for i in range(2)]
    kf = ctx.enter_context(sbuf(p + "kf", [128, 1024], F32)).ap()
    if is_v:
        vsp_hi = [ctx.enter_context(sbuf(p + f"vsphi{i}", [128, 1024], BF16)).ap() for i in range(2)]
        vsp_lo = [ctx.enter_context(sbuf(p + f"vsplo{i}", [128, 1024], BF16)).ap() for i in range(2)]
    else:
        ksq = ctx.enter_context(sbuf(p + "ksq", [128, 1024], F32)).ap()
        red8 = ctx.enter_context(sbuf(p + "red8", [128, 8], F32)).ap()
        rv8 = ctx.enter_context(sbuf(p + "rv8", [128, 8], F32)).ap()
        rstd8 = ctx.enter_context(sbuf(p + "rstd8", [128, 8], F32)).ap()
        knf8 = ctx.enter_context(sbuf(p + "knf8", [128, 1024], F32)).ap()
        sp_st = [ctx.enter_context(sbuf(p + f"spst{i}", [128, 16 * 128], BF16)).ap() for i in range(2)]

    NTG = NTC // 4
    pbt = [s.pb[4], s.pb[5]]

    with nc.Block() as block:
        @block.sync
        def _(sy):
            for fc in range(16):
                if fc >= 2:
                    sy.wait_ge(wsp, B["wsp"] + fc - 1)
                sy.dma_start(out=wtmp[fc % 2],
                             in_=w_dram[fc * 128:(fc + 1) * 128, :]).then_inc(wld, 16)
            for tg in range(NTG):
                if tg >= 2:
                    sy.wait_ge(pdone, B["pdone"] + 4 * (tg - 1))
                ts = tg * 512
                sy.dma_start(out=lnb_hi[tg % 2],
                             in_=s.lnT_hi[:, :, ts:ts + 512].rearrange("fc f t -> f fc t")
                             ).then_inc(lnld, 16)
                sy.dma_start(out=lnb_lo[tg % 2],
                             in_=s.lnT_lo[:, :, ts:ts + 512].rearrange("fc f t -> f fc t")
                             ).then_inc(lnld, 16)
            sy.wait_ge(spd, B["spd"] + 16 * (2 * NTC if is_v else 16 * NTC))

        @block.vector
        def _(v):
            for fc in range(16):
                v.wait_ge(wld, B["wld"] + 16 * (fc + 1))
                v.tensor_copy(w_hi[:, fc * 1024:(fc + 1) * 1024], wtmp[fc % 2])
                v.tensor_tensor(w_lo[:, fc * 1024:(fc + 1) * 1024], wtmp[fc % 2],
                                w_hi[:, fc * 1024:(fc + 1) * 1024], ALU.subtract).then_inc(wsp, 1)
            for tc in range(NTC):
                v.wait_ge(pdone, B["pdone"] + tc + 1)
                pa = s.pb[2 * (tc % 2)]
                pa1 = s.pb[2 * (tc % 2) + 1]
                v.tensor_copy(kf[:, 0:512], pa)
                v.tensor_copy(kf[:, 512:1024], pa1).then_inc(pfree, 1)
                bb = tc % 2
                if is_v:
                    if tc >= 2:
                        v.wait_ge(spd, B["spd"] + 16 * 2 * (tc - 1))
                    v.tensor_copy(vsp_hi[bb], kf)
                    v.tensor_tensor(vsp_lo[bb], kf, vsp_hi[bb], ALU.subtract).then_inc(ksr, 1)
                    if tc % 8 == 7:
                        feed_prepass_v(s, v, 7 if not is_q else 14)
                    continue
                v.tensor_tensor(ksq, kf, kf, ALU.mult)
                for g8 in range(8):
                    v.tensor_reduce(red8[:, g8:g8 + 1], ksq[:, g8 * 128:(g8 + 1) * 128],
                                    AX.X, ALU.add)
                v.drain()
                v.tensor_scalar(rv8, red8, 1.0 / HD, EPS_RMS, ALU.mult, ALU.add)
                v.drain()
                v.reciprocal(rv8, rv8).then_inc(rs1, 1)
                v.wait_ge(rs2, B["rs2"] + tc + 1)
                for g8 in range(8):
                    sl = slice(g8 * 128, (g8 + 1) * 128)
                    tt = v.scalar_tensor_tensor(knf8[:, sl], kf[:, sl], rstd8[:, g8:g8 + 1],
                                                nrm_b, ALU.mult, ALU.mult)
                tt.then_inc(knr, 1)
                if tc >= 2:
                    v.wait_ge(spd, B["spd"] + 16 * 16 * (tc - 1))
                for g8 in range(8):
                    v.wait_ge(trd, B["trd"] + tc * 8 + g8 + 1)
                    pbx = pbt[g8 % 2][:, 0:128]
                    hi_sl = sp_st[bb][:, (2 * g8) * 128:(2 * g8 + 1) * 128]
                    v.tensor_copy(hi_sl, pbx).then_inc(ted, 1)
                    v.tensor_tensor(sp_st[bb][:, (2 * g8 + 1) * 128:(2 * g8 + 2) * 128],
                                    pbx, hi_sl, ALU.subtract).then_inc(ted, 1)
                if tc % 8 == 7:
                    feed_prepass_v(s, v, 7 if not is_q else 14)

        @block.scalar
        def _(sc):
            def spill(tc):
                bb = tc % 2
                if is_v:
                    sc.wait_ge(ksr, B["ksr"] + tc + 1)
                    sc.dma_start(out=s.vt_hi[tc * 128:(tc + 1) * 128, :],
                                 in_=vsp_hi[bb]).then_inc(spd, 16)
                    sc.dma_start(out=s.vt_lo[tc * 128:(tc + 1) * 128, :],
                                 in_=vsp_lo[bb]).then_inc(spd, 16)
                else:
                    sc.wait_ge(ted, B["ted"] + (tc + 1) * 16)
                    for g8 in range(8):
                        h = head_off + g8
                        sc.dma_start(out=dst_hi[h, tc, :, :],
                                     in_=sp_st[bb][:, (2 * g8) * 128:(2 * g8 + 1) * 128]
                                     ).then_inc(spd, 16)
                        sc.dma_start(out=dst_lo[h, tc, :, :],
                                     in_=sp_st[bb][:, (2 * g8 + 1) * 128:(2 * g8 + 2) * 128]
                                     ).then_inc(spd, 16)
            for tc in range(NTC):
                if not is_v:
                    sc.wait_ge(rs1, B["rs1"] + tc + 1)
                    sc.activation(rstd8, rv8, AF.Sqrt).then_inc(rs2, 1)
                if tc >= 1:
                    spill(tc - 1)
            spill(NTC - 1)

        @block.tensor
        def _(t):
            def emit_tr(tc):
                t.wait_ge(knr, B["knr"] + tc + 1)
                for g8 in range(8):
                    w = tc * 16 + 2 * g8
                    if w >= 2:
                        t.wait_ge(ted, B["ted"] + w - 2)
                    t.transpose(pbt[g8 % 2][:, 0:128],
                                knf8[:, g8 * 128:(g8 + 1) * 128],
                                s.ident_f32).then_inc(trd, 1)
            t.wait_ge(wsp, B["wsp"] + 16)
            for tc in range(NTC):
                tg, tcq = tc // 4, tc % 4
                t.wait_ge(lnld, B["lnld"] + 16 * 2 * (tg + 1))
                if tc >= 2:
                    t.wait_ge(pfree, B["pfree"] + tc - 1)
                pa = s.pb[2 * (tc % 2)]
                pa1 = s.pb[2 * (tc % 2) + 1]
                lh = lnb_hi[tg % 2]
                ll = lnb_lo[tg % 2]
                for fc in range(16):
                    sl = slice(fc * 512 + tcq * 128, fc * 512 + (tcq + 1) * 128)
                    st_hi = lh[:, sl]
                    st_lo = ll[:, sl]
                    first, last = fc == 0, fc == 15
                    t.matmul(pa, st_hi, w_hi[:, fc * 1024:fc * 1024 + 512],
                             start=first, stop=False)
                    t.matmul(pa1, st_hi, w_hi[:, fc * 1024 + 512:(fc + 1) * 1024],
                             start=first, stop=False)
                    t.matmul(pa, st_hi, w_lo[:, fc * 1024:fc * 1024 + 512],
                             start=False, stop=False)
                    t.matmul(pa1, st_hi, w_lo[:, fc * 1024 + 512:(fc + 1) * 1024],
                             start=False, stop=False)
                    t.matmul(pa, st_lo, w_hi[:, fc * 1024:fc * 1024 + 512],
                             start=False, stop=last)
                    mm = t.matmul(pa1, st_lo, w_hi[:, fc * 1024 + 512:(fc + 1) * 1024],
                                  start=False, stop=last)
                    if last:
                        mm.then_inc(pdone, 1)
                if not is_v and tc >= 1:
                    emit_tr(tc - 1)
            if not is_v:
                emit_tr(NTC - 1)

        @block.gpsimd
        def _(g):
            if not PP_NO_PRE:
                feed_prepass(s, g, 28)
    ctx.close()
    s.pp_base["wld"] += 16 * 16
    s.pp_base["wsp"] += 16
    s.pp_base["lnld"] += 16 * 2 * NTG
    s.pp_base["pdone"] += NTC
    s.pp_base["pfree"] += NTC
    if is_v:
        s.pp_base["ksr"] += NTC
        s.pp_base["spd"] += 16 * 2 * NTC
    else:
        s.pp_base["rs1"] += NTC
        s.pp_base["rs2"] += NTC
        s.pp_base["knr"] += NTC
        s.pp_base["trd"] += 8 * NTC
        s.pp_base["ted"] += 16 * NTC
        s.pp_base["spd"] += 16 * 16 * NTC


def emit_wo_split(s):
    nc = s.nc
    wld = s.sem("wo_wld")
    wsp = s.sem("wo_wsp")
    spd = s.sem("wo_spd")
    ctx = contextlib.ExitStack()
    wtmp = [ctx.enter_context(nc.sbuf_tensor(f"wo_t{i}", [128, D], F32)).ap() for i in range(2)]
    whi = [ctx.enter_context(nc.sbuf_tensor(f"wo_h{i}", [128, D], BF16)).ap() for i in range(2)]
    wlo = [ctx.enter_context(nc.sbuf_tensor(f"wo_l{i}", [128, D], BF16)).ap() for i in range(2)]
    with nc.Block() as block:
        @block.sync
        def _(sy):
            for i in range(16):
                if i >= 2:
                    sy.wait_ge(wsp, i - 1)
                sy.dma_start(out=wtmp[i % 2], in_=s.woT[i * 128:(i + 1) * 128, :]).then_inc(wld, 16)
            sy.wait_ge(spd, 16 * 32)
        @block.vector
        def _(v):
            for i in range(16):
                b = i % 2
                v.wait_ge(wld, 16 * (i + 1))
                if i >= 2:
                    v.wait_ge(spd, 16 * 2 * (i - 1))
                v.tensor_copy(whi[b], wtmp[b])
                v.tensor_tensor(wlo[b], wtmp[b], whi[b], ALU.subtract).then_inc(wsp, 1)
            feed_prepass_v(s, v, len(s.pre_plan) - s.pre_nv)
        @block.scalar
        def _(sc):
            for i in range(16):
                b = i % 2
                sc.wait_ge(wsp, i + 1)
                sc.dma_start(out=s.wo_hi[i * 128:(i + 1) * 128, :], in_=whi[b]).then_inc(spd, 16)
                sc.dma_start(out=s.wo_lo[i * 128:(i + 1) * 128, :], in_=wlo[b]).then_inc(spd, 16)
        @block.gpsimd
        def _(g):
            feed_prepass(s, g, len(s.pre_plan) - s.pre_n)
    ctx.close()


def emit_m_finalize(s):
    nc = s.nc
    fin = s.sem("mfin")
    with nc.Block() as block:
        @block.tensor
        def _(t):
            t.wait_ge(s.pre_done, 1)
            t.matmul(s.pb[0][0:1, 0:1], s.ones_col, s.m_acc1, start=True, stop=True)
            t.matmul(s.pb[1][0:1, 0:1], s.ones_col, s.m_acc2,
                     start=True, stop=True).then_inc(fin, 1)
        @block.vector
        def _(v):
            v.wait_ge(fin, 1)
            v.tensor_scalar(s.m1c, s.pb[0][0:1, 0:1], 1.0 / (D * DH), EPS_Q, ALU.mult, ALU.max)
            v.tensor_scalar(s.m2c, s.pb[1][0:1, 0:1], 1.0 / (D * DH), EPS_Q, ALU.mult, ALU.max)
            v.drain().then_inc(fin, 1)
    with nc.Block() as block:
        @block.tensor
        def _(t):
            t.wait_ge(fin, 2)
            t.matmul(s.pb[0][:, 0:1], s.ones_row, s.m2c, start=True, stop=True)
            t.matmul(s.pb[1][:, 0:1], s.ones_row, s.m1c, start=True, stop=True).then_inc(fin, 1)
        @block.vector
        def _(v):
            v.wait_ge(fin, 3)
            v.tensor_copy(s.m2col, s.pb[0][:, 0:1])
            v.reciprocal(s.s1col, s.pb[1][:, 0:1])
            v.reciprocal(s.s2col, s.pb[0][:, 0:1])
            if DEBUG:
                v.drain()
                v.tensor_copy(s.dbg_st[0:1, 0:1], s.m1c)
                v.tensor_copy(s.dbg_st[0:1, 1:2], s.m2c)
                v.tensor_copy(s.dbg_st[0:1, 2:3], s.s1col[0:1, 0:1])
                v.drain().then_inc(fin, 1)
        @block.scalar
        def _(sc):
            if DEBUG:
                sc.wait_ge(fin, 4)
                sc.dma_start(out=s.dbg_m, in_=s.dbg_st[0:1, 0:4]).then_inc(fin, 16)
                sc.wait_ge(fin, 20)


def emit_p2(s):
    nc = s.nc

    kvld = s.sem("a_kvld")
    qld = s.sem("a_qld")
    ssem = s.sem("a_ssem")
    esem = s.sem("a_esem")
    usem = s.sem("a_usem")
    csem = s.sem("a_csem")
    tosem = s.sem("a_tosem")
    recsem = s.sem("a_recsem")
    recbsem = s.sem("a_recbsem")
    recbfree = s.sem("a_recbfree")
    aosem = s.sem("a_aosem")
    wold = s.sem("a_wold")
    ydone = s.sem("a_ydone")
    yev = s.sem("a_yev")
    ytr = s.sem("a_ytr")
    yevp = s.sem("a_yevp")
    ysp = s.sem("a_ysp")

    ctx = contextlib.ExitStack()
    sbuf = nc.sbuf_tensor
    K_hi = [ctx.enter_context(sbuf(f"a_khi{i}", [128, L], BF16)).ap() for i in range(2)]
    K_lo = [ctx.enter_context(sbuf(f"a_klo{i}", [128, L], BF16)).ap() for i in range(2)]
    V_hi = [ctx.enter_context(sbuf(f"a_vhi{i}", [128, L], BF16)).ap() for i in range(2)]
    V_lo = [ctx.enter_context(sbuf(f"a_vlo{i}", [128, L], BF16)).ap() for i in range(2)]
    Q_hi = [ctx.enter_context(sbuf(f"a_qhi{i}", [128, 512], BF16)).ap() for i in range(2)]
    Q_lo = [ctx.enter_context(sbuf(f"a_qlo{i}", [128, 512], BF16)).ap() for i in range(2)]
    U_f = [ctx.enter_context(sbuf(f"a_uf{i}", [128, 512], F32)).ap() for i in range(2)]
    U_hi = [ctx.enter_context(sbuf(f"a_uhi{i}", [128, 512], BF16)).ap() for i in range(2)]
    U_lo = [ctx.enter_context(sbuf(f"a_ulo{i}", [128, 512], BF16)).ap() for i in range(2)]
    mask_sb = ctx.enter_context(sbuf("a_mask", [128, 8 * 1024], BF16)).ap()
    rec_row = [ctx.enter_context(sbuf(f"a_rr{i}", [1, 512], F32)).ap() for i in range(2)]
    rec_b = ctx.enter_context(sbuf("a_recb", [128, 512], F32)).ap()
    ao_f = ctx.enter_context(sbuf("a_aof", [128, 512], F32)).ap()
    ao_hi = ctx.enter_context(sbuf("a_aohi", [128, NH * 512], BF16)).ap()
    ao_lo = ctx.enter_context(sbuf("a_aolo", [128, NH * 512], BF16)).ap()
    woc_hi = [ctx.enter_context(sbuf(f"a_wochi{i}", [128, D], BF16)).ap() for i in range(2)]
    woc_lo = [ctx.enter_context(sbuf(f"a_woclo{i}", [128, D], BF16)).ap() for i in range(2)]
    y_oc = ctx.enter_context(sbuf("a_yoc", [128, 512], F32)).ap()
    piece = [ctx.enter_context(sbuf(f"a_piece{i}", [128, 128], F32)).ap() for i in range(4)]

    psS = [s.pb[0], s.pb[1]]
    psO = s.pb[2]
    psSum = s.pb[3][0:1, :]
    psRB = s.pb[4]
    psY = [s.pb[4], s.pb[5]]
    psTR = [s.pb[6], s.pb[7]]

    with nc.Block() as block:
        @block.sync
        def _(sy):
            sy.dma_start(out=mask_sb, in_=s.mask_d.rearrange("c p q -> p c q")).then_inc(qld, 16)
            for qg in range(4):
                for g in range(8):
                    gi = qg * 8 + g
                    if gi >= 2:
                        gp = gi - 2
                        sy.wait_ge(recbsem, (gp // 8) * 16 + (gp % 8) * 2 + 2)
                    sy.dma_start(out=K_hi[g % 2],
                                 in_=s.kn_hi[g].rearrange("kc hd t -> hd kc t")).then_inc(kvld, 16)
                    sy.dma_start(out=K_lo[g % 2],
                                 in_=s.kn_lo[g].rearrange("kc hd t -> hd kc t")).then_inc(kvld, 16)
                    sy.dma_start(out=V_hi[g % 2],
                                 in_=s.vt_hi[:, g * 128:(g + 1) * 128].rearrange(
                                     "(kc p) hd -> p kc hd", p=128)).then_inc(kvld, 16)
                    sy.dma_start(out=V_lo[g % 2],
                                 in_=s.vt_lo[:, g * 128:(g + 1) * 128].rearrange(
                                     "(kc p) hd -> p kc hd", p=128)).then_inc(kvld, 16)
                    for hh in range(2):
                        U = qg * 16 + g * 2 + hh
                        h = g * 2 + hh
                        if U >= 2:
                            sy.wait_ge(tosem, U - 1)
                        sy.dma_start(out=Q_hi[U % 2],
                                     in_=s.qn_hi[h, qg * 4:(qg + 1) * 4].rearrange(
                                         "tc hd t -> hd tc t")).then_inc(qld, 16)
                        sy.dma_start(out=Q_lo[U % 2],
                                     in_=s.qn_lo[h, qg * 4:(qg + 1) * 4].rearrange(
                                         "tc hd t -> hd tc t")).then_inc(qld, 16)
                for oc in range(16):
                    oci = qg * 16 + oc
                    if oci >= 2:
                        sy.wait_ge(ydone, oci - 1)
                    sy.dma_start(out=woc_hi[oc % 2],
                                 in_=s.wo_hi[:, oc * 128:(oc + 1) * 128].rearrange(
                                     "(hc f) o -> f hc o", f=128)).then_inc(wold, 16)
                    sy.dma_start(out=woc_lo[oc % 2],
                                 in_=s.wo_lo[:, oc * 128:(oc + 1) * 128].rearrange(
                                     "(hc f) o -> f hc o", f=128)).then_inc(wold, 16)
            sy.wait_ge(ysp, 16 * 64 * 4)

        @block.tensor
        def _(t):
            def consume(U, kc, vh, vl):
                t.wait_ge(usem, U * 32 + kc + 1)
                b = kc % 2
                ks = slice(kc * 128, (kc + 1) * 128)
                t.matmul(psSum, s.ones_bf, U_hi[b], start=(kc == 0), stop=False)
                mmsum = t.matmul(psSum, s.ones_bf, U_lo[b], start=False, stop=(kc == 31))
                if kc == 31:
                    mmsum.then_inc(tosem, 1)
                t.matmul(psO, vh[:, ks], U_hi[b], start=(kc == 0), stop=False)
                t.matmul(psO, vh[:, ks], U_lo[b], start=False, stop=False)
                t.matmul(psO, vl[:, ks], U_hi[b],
                         start=False, stop=(kc == 31)).then_inc(csem, 1)

            def ytrans(qg, oc):
                t.wait_ge(yev, qg * 16 + oc + 1)
                for pc in range(4):
                    n = (qg * 16 + oc) * 4 + pc
                    if n >= 2:
                        t.wait_ge(yevp, n - 1)
                    t.transpose(psTR[pc % 2][:, 0:128],
                                y_oc[:, pc * 128:(pc + 1) * 128], s.ident_f32).then_inc(ytr, 1)

            pending_rec = [None]
            def emit_rec():
                U, u = pending_rec[0]
                pending_rec[0] = None
                t.wait_ge(recsem, U + 1)
                if U >= 1:
                    t.wait_ge(recbfree, U)
                if u == 0:
                    t.wait_ge(yev, U)
                t.matmul(psRB, s.ones_row, rec_row[U % 2],
                         start=True, stop=True).then_inc(recbsem, 1)
            for qg in range(4):
                for g in range(8):
                    gi = qg * 8 + g
                    t.wait_ge(kvld, 16 * 4 * (gi + 1))
                    for hh in range(2):
                        u = g * 2 + hh
                        U = qg * 16 + u
                        t.wait_ge(qld, 16 * (2 * (U + 1) + 1))
                        t.wait_ge(recsem, U)
                        kh, kl = K_hi[g % 2], K_lo[g % 2]
                        vh, vl = V_hi[g % 2], V_lo[g % 2]
                        qh, ql = Q_hi[U % 2], Q_lo[U % 2]
                        for kc in range(32):
                            pS = psS[kc % 2]
                            t.wait_ge(esem, max(0, U * 32 + kc - 1))
                            ks = slice(kc * 128, (kc + 1) * 128)
                            masked = (8 <= kc < 16) and qg >= 2
                            t.matmul(pS, kh[:, ks], qh, start=True, stop=False)
                            t.matmul(pS, kh[:, ks], ql, start=False, stop=False)
                            mm3 = t.matmul(pS, kl[:, ks], qh, start=False, stop=not masked)
                            if masked:
                                t.matmul(pS, s.ident_bf,
                                         mask_sb[:, (kc - 8) * 1024 + (qg - 2) * 512:
                                                 (kc - 8) * 1024 + (qg - 2) * 512 + 512],
                                         start=False, stop=True).then_inc(ssem, 1)
                            else:
                                mm3.then_inc(ssem, 1)
                            if kc == 1:
                                if pending_rec[0] is not None:
                                    emit_rec()
                                t.wait_ge(aosem, U)
                            if kc >= 1:
                                consume(U, kc - 1, vh, vl)
                        consume(U, 31, vh, vl)
                        pending_rec[0] = (U, u)
                if pending_rec[0] is not None:
                    emit_rec()
                t.wait_ge(aosem, qg * 16 + 16)
                for oc in range(16):
                    oci = qg * 16 + oc
                    t.wait_ge(wold, 16 * 2 * (oci + 1))
                    if oc >= 2:
                        t.wait_ge(yev, qg * 16 + oc - 1)
                    pY = psY[oc % 2]
                    wh, wl = woc_hi[oc % 2], woc_lo[oc % 2]
                    for h in range(NH):
                        ah = ao_hi[:, h * 512:(h + 1) * 512]
                        al = ao_lo[:, h * 512:(h + 1) * 512]
                        wt_h = wh[:, h * 128:(h + 1) * 128]
                        wt_l = wl[:, h * 128:(h + 1) * 128]
                        t.matmul(pY, wt_h, ah, start=(h == 0), stop=False)
                        t.matmul(pY, wt_h, al, start=False, stop=False)
                        mmy = t.matmul(pY, wt_l, ah, start=False, stop=(h == NH - 1))
                    mmy.then_inc(ydone, 1)
                    if oc >= 1:
                        ytrans(qg, oc - 1)
                ytrans(qg, 15)

        @block.scalar
        def _(sc):
            for U in range(64):
                for kc in range(32):
                    sc.wait_ge(ssem, U * 32 + kc + 1)
                    sc.wait_ge(usem, max(0, U * 32 + kc - 1))
                    sc.activation(U_f[kc % 2], psS[kc % 2], AF.Exp, scale=SCALE).then_inc(esem, 1)

        @block.vector
        def _(v):
            for qg in range(4):
                for u in range(16):
                    U = qg * 16 + u
                    if u == 0 and qg >= 1:
                        v.wait_ge(ydone, qg * 16)
                    for kc in range(32):
                        b = kc % 2
                        v.wait_ge(esem, U * 32 + kc + 1)
                        v.wait_ge(csem, max(0, U * 32 + kc - 1))
                        v.tensor_copy(U_hi[b], U_f[b])
                        v.tensor_tensor(U_lo[b], U_f[b], U_hi[b], ALU.subtract).then_inc(usem, 1)
                    v.wait_ge(tosem, U + 1)
                    v.reciprocal(rec_row[U % 2], psSum).then_inc(recsem, 1)
                    v.wait_ge(recbsem, U + 1)
                    v.tensor_copy(rec_b, psRB).then_inc(recbfree, 1)
                    v.wait_ge(csem, U * 32 + 32)
                    v.tensor_tensor(ao_f, psO, rec_b, ALU.mult)
                    v.tensor_copy(ao_hi[:, u * 512:(u + 1) * 512], ao_f)
                    v.tensor_tensor(ao_lo[:, u * 512:(u + 1) * 512], ao_f,
                                    ao_hi[:, u * 512:(u + 1) * 512], ALU.subtract).then_inc(aosem, 1)
                for oc in range(16):
                    v.wait_ge(ydone, qg * 16 + oc + 1)
                    v.tensor_copy(y_oc, psY[oc % 2]).then_inc(yev, 1)
                    for pc in range(4):
                        n = (qg * 16 + oc) * 4 + pc
                        v.wait_ge(ytr, n + 1)
                        if n >= 4:
                            v.wait_ge(ysp, 16 * (n - 3))
                        v.tensor_copy(piece[pc],
                                      psTR[pc % 2][:, 0:128]).then_inc(yevp, 1)

        @block.gpsimd
        def _(g):
            for qg in range(4):
                for oc in range(16):
                    for pc in range(4):
                        n = (qg * 16 + oc) * 4 + pc
                        g.wait_ge(yevp, n + 1)
                        tc = qg * 4 + pc
                        g.dma_start(out=s.y_sc[tc * 128:(tc + 1) * 128, oc * 128:(oc + 1) * 128],
                                    in_=piece[pc]).then_inc(ysp, 16)
            g.wait_ge(ysp, 16 * 64 * 4)
    ctx.close()


def emit_p3(s):
    nc = s.nc
    yld = s.sem("p3_yld")
    x2r = s.sem("p3_x2r")
    rs1 = s.sem("p3_rs1")
    rs2 = s.sem("p3_rs2")
    er = s.sem("p3_er")
    sg = s.sem("p3_sg")
    xqr = s.sem("p3_xqr")
    trd = s.sem("p3_trd")
    ted = s.sem("p3_ted")
    spd = s.sem("p3_spd")

    ctx = contextlib.ExitStack()
    sbuf = nc.sbuf_tensor
    y_st = [ctx.enter_context(sbuf(f"p3_y{i}", [128, D], F32)).ap() for i in range(2)]
    x_st = [ctx.enter_context(sbuf(f"p3_x{i}", [128, D], F32)).ap() for i in range(2)]
    x2_st = [ctx.enter_context(sbuf(f"p3_x2{i}", [128, D], F32)).ap() for i in range(2)]
    ln2 = ctx.enter_context(sbuf("p3_ln2", [128, D], F32)).ap()
    z_st = ctx.enter_context(sbuf("p3_z", [128, D], F32)).ap()
    sti = ctx.enter_context(sbuf("p3_sti", [128, D], I32)).ap()
    e_st = ctx.enter_context(sbuf("p3_e", [128, D], F32)).ap()
    sg1 = ctx.enter_context(sbuf("p3_sg1", [128, D], BF16)).ap()
    sg2 = ctx.enter_context(sbuf("p3_sg2", [128, D], BF16)).ap()
    cs_st = ctx.enter_context(sbuf("p3_cs", [128, D], BF16)).ap()
    xq_st = [ctx.enter_context(sbuf(f"p3_xq{i}", [128, D], F32)).ap() for i in range(2)]
    xsp = [ctx.enter_context(sbuf(f"p3_xsp{i}", [128, D], BF16)).ap() for i in range(2)]
    xsq = ctx.enter_context(sbuf("p3_xsq", [128, D], F32)).ap()
    bn2 = ctx.enter_context(sbuf("p3_bn2", [128, 2], F32)).ap()
    q2c = ctx.enter_context(sbuf("p3_q2c", [128, 1], F32)).ap()
    col = ctx.enter_context(sbuf("p3_col", [128, 1], F32)).ap()
    rstd = ctx.enter_context(sbuf("p3_rstd", [128, 1], F32)).ap()
    amx = ctx.enter_context(sbuf("p3_amx", [128, 1], F32)).ap()
    scol = ctx.enter_context(sbuf("p3_scol", [128, 1], F32)).ap()
    pbt = [s.pb[2], s.pb[3]]

    with nc.Block() as block:
        @block.sync
        def _(sy):
            for tc in range(16):
                if tc >= 2:
                    sy.wait_ge(x2r, tc - 1)
                sy.dma_start(out=y_st[tc % 2],
                             in_=s.y_sc[tc * 128:(tc + 1) * 128, :]).then_inc(yld, 16)
                sy.dma_start(out=x_st[tc % 2],
                             in_=s.x_own[tc * 128:(tc + 1) * 128, :]).then_inc(yld, 16)
            sy.wait_ge(spd, 16 * 17 * 16)

        @block.vector
        def _(v):
            for tc in range(16):
                b = tc % 2
                v.wait_ge(yld, 16 * 2 * (tc + 1))
                if tc >= 2:
                    v.wait_ge(spd, 16 * 17 * (tc - 1))
                v.tensor_tensor(x2_st[b], y_st[b], x_st[b], ALU.add).then_inc(x2r, 1)
                v.tensor_reduce(bn2[:, 0:1], x2_st[b], AX.X, ALU.add)
                v.tensor_tensor(xsq, x2_st[b], x2_st[b], ALU.mult)
                v.tensor_reduce(q2c, xsq, AX.X, ALU.add)
                v.drain()
                v.tensor_scalar(bn2[:, 0:1], bn2[:, 0:1], 1.0 / D, None, ALU.mult)
                v.drain()
                v.tensor_tensor(col, bn2[:, 0:1], bn2[:, 0:1], ALU.mult)
                v.drain()
                v.scalar_tensor_tensor(col, q2c, 1.0 / D, col, ALU.mult, ALU.subtract)
                v.drain()
                v.tensor_scalar(col, col, EPS_LN, None, ALU.add)
                v.drain()
                v.reciprocal(col, col).then_inc(rs1, 1)
                v.wait_ge(rs2, tc + 1)
                v.tensor_scalar(ln2, x2_st[b], bn2[:, 0:1], None, ALU.subtract)
                v.tensor_scalar(ln2, ln2, rstd, None, ALU.mult)
                v.tensor_reduce(amx, ln2, AX.X, ALU.max, apply_absolute_value=True)
                v.drain()
                v.tensor_scalar(amx, amx, EPS_Q, None, ALU.max)
                v.drain()
                v.reciprocal(scol, amx)
                v.drain()
                v.tensor_scalar(scol, scol, 127.0, None, ALU.mult)
                v.drain()
                v.tensor_scalar(z_st, ln2, scol, None, ALU.mult)
                v.tensor_copy(sti, z_st)
                v.tensor_tensor(e_st, z_st, sti, ALU.subtract).then_inc(er, 1)
                v.wait_ge(sg, tc + 1)
                v.tensor_tensor(cs_st, sg1, sg2, ALU.add)
                v.scalar_tensor_tensor(xq_st[b], cs_st, 0.5, sti, ALU.mult, ALU.add).then_inc(xqr, 1)
                for fc in range(16):
                    n = tc * 17 + fc
                    v.wait_ge(trd, n + 1)
                    v.tensor_copy(xsp[b][:, fc * 128:(fc + 1) * 128],
                                  pbt[fc % 2][:, 0:128]).then_inc(ted, 1)
                v.wait_ge(trd, tc * 17 + 17)
                v.tensor_copy(s.amax1_row[0:1, tc * 128:(tc + 1) * 128],
                              s.pb[1][0:1, 0:128]).then_inc(ted, 1)

        @block.scalar
        def _(sc):
            for tc in range(16):
                sc.wait_ge(rs1, tc + 1)
                sc.activation(rstd, col, AF.Sqrt).then_inc(rs2, 1)
                sc.wait_ge(er, tc + 1)
                sc.sign(sg1, e_st, bias=s.nhalfc)
                sc.sign(sg2, e_st, bias=s.halfc).then_inc(sg, 1)

        @block.tensor
        def _(t):
            for tc in range(16):
                t.wait_ge(xqr, tc + 1)
                for fc in range(16):
                    n = tc * 17 + fc
                    if n >= 2:
                        t.wait_ge(ted, n - 1)
                    t.transpose(pbt[fc % 2][:, 0:128],
                                xq_st[tc % 2][:, fc * 128:(fc + 1) * 128],
                                s.ident_f32).then_inc(trd, 1)
                if tc >= 1:
                    t.wait_ge(ted, tc * 17)
                t.transpose(s.pb[1][0:1, 0:128], amx, s.ident_f32).then_inc(trd, 1)

        @block.gpsimd
        def _(g):
            for tc in range(16):
                b = tc % 2
                g.wait_ge(x2r, tc + 1)
                g.dma_start(out=s.x2_sc[tc * 128:(tc + 1) * 128, :],
                            in_=x2_st[b]).then_inc(spd, 16)
                g.wait_ge(ted, tc * 17 + 16)
                for fc in range(16):
                    g.dma_start(out=s.xq_sc[fc, :, tc * 128:(tc + 1) * 128],
                                in_=xsp[b][:, fc * 128:(fc + 1) * 128]).then_inc(spd, 16)
            g.wait_ge(spd, 16 * 17 * 16)
    ctx.close()


def emit_c_rows(s):
    nc = s.nc
    finv = s.sem("cr_finv")
    fint = s.sem("cr_fint")
    with nc.Block() as block:
        @block.vector
        def _(v):
            v.tensor_scalar(s.amax1_row, s.amax1_row, s.m1c, 1.0 / 254.0, ALU.mult, ALU.mult)
            v.drain()
            v.drain().then_inc(finv, 1)
        @block.tensor
        def _(t):
            t.wait_ge(finv, 1)
            for i in range(4):
                t.matmul(s.pb[i][:, 0:512], s.ones_row,
                         s.amax1_row[0:1, i * 512:(i + 1) * 512], start=True, stop=True)
            t.drain().then_inc(fint, 1)
    with nc.Block() as block:
        @block.vector
        def _(v):
            v.wait_ge(fint, 1)
            for i in range(4):
                v.tensor_copy(s.c1_b[:, i * 512:(i + 1) * 512], s.pb[i][:, 0:512])


def emit_ffn_a(s):
    nc = s.nc
    xqld = s.sem("fa_xqld")
    xqld2 = s.sem("fa_xqld2")
    w1ld = s.sem("fa_w1ld")
    w2ld = s.sem("fa_w2ld")
    sg1d = s.sem("fa_sg1d")
    sg2d = s.sem("fa_sg2d")
    t1r = s.sem("fa_t1r")
    t2r = s.sem("fa_t2r")
    mmd = s.sem("fa_mmd")
    hev = s.sem("fa_hev")
    gel = s.sem("fa_gel")
    mxd = s.sem("fa_mxd")
    gsp = s.sem("fa_gsp")
    t2sp = s.sem("fa_t2sp")
    amp = s.sem("fa_amp")
    amc = s.sem("fa_amc")

    ctx = contextlib.ExitStack()
    sbuf = nc.sbuf_tensor
    xqT = ctx.enter_context(sbuf("fa_xqT", [128, 16 * T], BF16)).ap()
    w1t = [ctx.enter_context(sbuf(f"fa_w1t{i}", [128, D], F32)).ap() for i in range(2)]
    w2t = ctx.enter_context(sbuf("fa_w2t", [128, D], F32)).ap()
    sa1 = ctx.enter_context(sbuf("fa_sa1", [128, D], BF16)).ap()
    sb1 = ctx.enter_context(sbuf("fa_sb1", [128, D], BF16)).ap()
    sa2 = ctx.enter_context(sbuf("fa_sa2", [128, D], BF16)).ap()
    sb2 = ctx.enter_context(sbuf("fa_sb2", [128, D], BF16)).ap()
    t1 = [ctx.enter_context(sbuf(f"fa_t1_{i}", [128, D], BF16)).ap() for i in range(2)]
    t2 = [ctx.enter_context(sbuf(f"fa_t2_{i}", [128, D], BF16)).ap() for i in range(2)]
    h_st = [ctx.enter_context(sbuf(f"fa_h{i}", [128, T], F32)).ap() for i in range(2)]
    g_st = [ctx.enter_context(sbuf(f"fa_g{i}", [128, T], F32)).ap() for i in range(2)]
    mx = ctx.enter_context(sbuf("fa_mx", [128, T], F32)).ap()
    mn = ctx.enter_context(sbuf("fa_mn", [128, T], F32)).ap()
    NDH = 64

    with nc.Block() as block:
        @block.sync
        def _(sy):
            for dh in range(2):
                sy.dma_start(out=w1t[dh],
                             in_=s.w1T[:, dh * 128:(dh + 1) * 128].rearrange(
                                 "(fc f) o -> f fc o", f=128)).then_inc(w1ld, 16)
            for fc in range(8, 16):
                sy.dma_start(out=xqT[:, fc * T:(fc + 1) * T],
                             in_=s.xq_sc[fc]).then_inc(xqld, 16)
            for dh in range(2, NDH):
                sy.wait_ge(sg1d, dh - 1)
                sy.dma_start(out=w1t[dh % 2],
                             in_=s.w1T[:, dh * 128:(dh + 1) * 128].rearrange(
                                 "(fc f) o -> f fc o", f=128)).then_inc(w1ld, 16)
            sy.wait_ge(gsp, 64 * NDH)
            sy.wait_ge(t2sp, 16 * NDH)

        @block.scalar
        def _(sc):
            def signs(dh):
                b = dh % 2
                sc.wait_ge(w1ld, 16 * (dh + 1))
                if dh >= 1:
                    sc.wait_ge(t1r, dh)
                sc.activation(sa1, w1t[b], AF.Sign, bias=s.nhalfc, scale=s.s1col)
                sc.activation(sb1, w1t[b], AF.Sign,
                              bias=s.halfc, scale=s.s1col).then_inc(sg1d, 1)
                sc.wait_ge(w2ld, 16 * (dh + 1))
                if dh >= 1:
                    sc.wait_ge(t2r, dh)
                sc.activation(sa2, w2t, AF.Sign, bias=s.nhalfc, scale=s.s2col)
                sc.activation(sb2, w2t, AF.Sign,
                              bias=s.halfc, scale=s.s2col).then_inc(sg2d, 1)
                if dh + 1 < NDH:
                    sc.wait_ge(sg2d, dh + 1)
                    sc.dma_start(out=w2t,
                                 in_=s.w2T[(dh + 1) * 128:(dh + 2) * 128, :]
                                 ).then_inc(w2ld, 16)
            sc.wait_ge(s.sems["mfin"], 3)
            sc.dma_start(out=w2t, in_=s.w2T[0:128, :]).then_inc(w2ld, 16)
            signs(0)
            for dh in range(NDH):
                if dh + 1 < NDH:
                    signs(dh + 1)
                sc.wait_ge(hev, dh + 1)
                if dh >= 2:
                    sc.wait_ge(gsp, 64 * (dh - 1))
                sc.activation(g_st[dh % 2], h_st[dh % 2], AF.Gelu).then_inc(gel, 1)

        @block.vector
        def _(v):
            for dh in range(NDH):
                b = dh % 2
                v.wait_ge(sg1d, dh + 1)
                v.tensor_tensor(t1[b], sa1, sb1, ALU.add).then_inc(t1r, 1)
                v.wait_ge(sg2d, dh + 1)
                if dh >= 2:
                    v.wait_ge(t2sp, 16 * (dh - 1))
                v.tensor_tensor(t2[b], sa2, sb2, ALU.add).then_inc(t2r, 1)
                v.wait_ge(mmd, dh + 1)
                pset = 4 * (dh % 2)
                for i in range(4):
                    tth = v.tensor_tensor(h_st[b][:, i * 512:(i + 1) * 512],
                                          s.pb[pset + i], s.c1_b[:, i * 512:(i + 1) * 512],
                                          ALU.mult)
                tth.then_inc(hev, 1)
                if dh >= 1:
                    v.wait_ge(gel, dh)
                    gp = g_st[(dh - 1) % 2]
                    if dh == 1:
                        v.tensor_copy(mx, gp)
                        v.tensor_copy(mn, gp).then_inc(mxd, 1)
                    else:
                        v.tensor_tensor(mx, mx, gp, ALU.max)
                        v.tensor_tensor(mn, mn, gp, ALU.min).then_inc(mxd, 1)
            v.wait_ge(gel, NDH)
            v.tensor_tensor(mx, mx, g_st[(NDH - 1) % 2], ALU.max)
            v.tensor_tensor(mn, mn, g_st[(NDH - 1) % 2], ALU.min).then_inc(mxd, 1)
            v.tensor_scalar(mn, mn, -1.0, None, ALU.mult)
            v.tensor_tensor(mx, mx, mn, ALU.max)
            v.drain().then_inc(amc, 1)
            v.wait_ge(amp, 1)
            for i in range(16):
                v.tensor_reduce(s.am2t[:, i:i + 1],
                                s.pb[i // 4][:, (i % 4) * 128:(i % 4 + 1) * 128],
                                AX.X, ALU.max)
            v.drain()
            v.tensor_scalar(s.am2t, s.am2t, EPS_Q, None, ALU.max)
            v.drain()
            v.tensor_tensor(s.c2_cols, s.am2t, s.m2col.broadcast_to([128, 16]), ALU.mult)
            v.drain()
            v.tensor_scalar(s.c2_cols, s.c2_cols, 1.0 / 254.0, None, ALU.mult)
            v.drain()

        @block.tensor
        def _(t):
            t.wait_ge(xqld, 16 * 8)
            t.wait_ge(xqld2, 16 * 8)
            for dh in range(NDH):
                b = dh % 2
                t.wait_ge(t1r, dh + 1)
                if dh >= 2:
                    t.wait_ge(hev, dh - 1)
                pset = 4 * (dh % 2)
                for fc in range(16):
                    for i in range(4):
                        mmf = t.matmul(s.pb[pset + i], t1[b][:, fc * 128:(fc + 1) * 128],
                                       xqT[:, fc * T + i * 512: fc * T + (i + 1) * 512],
                                       start=(fc == 0), stop=(fc == 15))
                mmf.then_inc(mmd, 1)
            t.wait_ge(amc, 1)
            for i in range(16):
                t.transpose(s.pb[i // 4][:, (i % 4) * 128:(i % 4 + 1) * 128],
                            mx[:, i * 128:(i + 1) * 128], s.ident_f32)
            t.drain().then_inc(amp, 1)

        @block.gpsimd
        def _(g):
            for fc in range(8):
                g.dma_start(out=xqT[:, fc * T:(fc + 1) * T],
                            in_=s.xq_sc[fc]).then_inc(xqld2, 16)
            for dh in range(NDH):
                b = dh % 2
                g.wait_ge(t2r, dh + 1)
                g.dma_start(out=s.t2_sc[dh * 128:(dh + 1) * 128, :], in_=t2[b]).then_inc(t2sp, 16)
                g.wait_ge(mxd, dh + 1)
                for tq in range(4):
                    g.dma_start(out=s.g_sc[tq, dh * 128:(dh + 1) * 128, :],
                                in_=g_st[b][:, tq * 512:(tq + 1) * 512]).then_inc(gsp, 16)
            g.wait_ge(gsp, 64 * NDH)
            g.wait_ge(t2sp, 16 * NDH)
    ctx.close()

    fpp = s.sem("fa_fpp")
    fpc = s.sem("fa_fpc")
    am16 = s.qnw_b[0:16, 0:128]
    with nc.Block() as block:
        @block.tensor
        def _(t):
            t.transpose(s.pb[2][0:16, 0:128], s.am2t, s.ident_f32).then_inc(fpp, 1)
        @block.vector
        def _(v):
            v.wait_ge(fpp, 1)
            v.tensor_copy(am16, s.pb[2][0:16, 0:128]).then_inc(fpc, 1)
        @block.scalar
        def _(sc):
            sc.wait_ge(fpc, 1)
            sc.dma_start(out=s.amax2_row,
                         in_=am16).then_inc(fpc, 16)
            sc.wait_ge(fpc, 17)
    f2v = s.sem("fa_f2v")
    f2t = s.sem("fa_f2t")
    with nc.Block() as block:
        @block.vector
        def _(v):
            v.reciprocal(s.amax2_row, s.amax2_row)
            v.drain()
            v.tensor_scalar(s.amax2_row, s.amax2_row, 127.0, None, ALU.mult)
            v.drain()
            v.drain().then_inc(f2v, 1)
        @block.tensor
        def _(t):
            t.wait_ge(f2v, 1)
            for i in range(4):
                t.matmul(s.pb[i][:, 0:512], s.ones_row,
                         s.amax2_row[0:1, i * 512:(i + 1) * 512], start=True, stop=True)
            t.drain().then_inc(f2t, 1)
    with nc.Block() as block:
        @block.vector
        def _(v):
            v.wait_ge(f2t, 1)
            for i in range(4):
                v.tensor_copy(s.s2_b[:, i * 512:(i + 1) * 512], s.pb[i][:, 0:512])


def emit_ffn_b(s):
    nc = s.nc
    gld = s.sem("fb_gld")
    gld2 = s.sem("fb_gld2")
    zr = s.sem("fb_zr")
    sgd = s.sem("fb_sgd")
    hqr = s.sem("fb_hqr")
    hqsp = s.sem("fb_hqsp")
    t2ld = s.sem("fb_t2ld")
    hqld = s.sem("fb_hqld")
    x2ld = s.sem("fb_x2ld")
    mmh = s.sem("fb_mmh")
    yev = s.sem("fb_yev")
    yout = s.sem("fb_yout")
    NDH = 64
    NJ = 256

    ctx = contextlib.ExitStack()
    sbuf = nc.sbuf_tensor
    g_ld = [ctx.enter_context(sbuf(f"fb_g{i}", [128, 512], F32)).ap() for i in range(4)]
    z_st = ctx.enter_context(sbuf("fb_z", [128, 512], F32)).ap()
    sti = [ctx.enter_context(sbuf(f"fb_sti{i}", [128, 512], I32)).ap() for i in range(2)]
    e_st = [ctx.enter_context(sbuf(f"fb_e{i}", [128, 512], F32)).ap() for i in range(2)]
    sg1 = [ctx.enter_context(sbuf(f"fb_sg1{i}", [128, 512], BF16)).ap() for i in range(2)]
    sg2 = [ctx.enter_context(sbuf(f"fb_sg2{i}", [128, 512], BF16)).ap() for i in range(2)]
    cs_st = ctx.enter_context(sbuf("fb_cs", [128, 512], BF16)).ap()
    hq_st = [ctx.enter_context(sbuf(f"fb_hq{i}", [128, 512], BF16)).ap() for i in range(2)]
    T2 = ctx.enter_context(sbuf("fb_T2", [128, 64 * 512], BF16)).ap()
    hqb = [ctx.enter_context(sbuf(f"fb_hqb{i}", [128, 16 * 512], BF16)).ap() for i in range(4)]
    x2s = [ctx.enter_context(sbuf(f"fb_x2s{i}", [128, 4 * 512], F32)).ap() for i in range(2)]
    y_st = [ctx.enter_context(sbuf(f"fb_y{i}", [128, 4 * 512], F32)).ap() for i in range(2)]
    EV_AT = {100: 0, 155: 1, 220: 2}

    with nc.Block() as block:
        @block.sync
        def _(sy):
            for j in range(0, NJ, 2):
                tq, dh = j // 64, j % 64
                if j >= 4:
                    sy.wait_ge(zr, j - 3)
                sy.dma_start(out=g_ld[j % 4],
                             in_=s.g_sc[tq, dh * 128:(dh + 1) * 128, :]).then_inc(gld, 16)
            sy.wait_ge(yout, 16 * 64)

        @block.scalar
        def _(sc):
            def x2_load(n):
                ocg, tcb = n // 4, n % 4
                if n >= 2:
                    sc.wait_ge(yev, 4 * (n - 1))
                for tci in range(4):
                    tc = tcb * 4 + tci
                    sc.dma_start(out=x2s[n % 2][:, tci * 512:(tci + 1) * 512],
                                 in_=s.x2_sc[tc * 128:(tc + 1) * 128,
                                             ocg * 512:(ocg + 1) * 512]).then_inc(x2ld, 16)
            def hq_halves(n):
                tcb = n % 4
                srcq = s.hq_sc[:, tcb * 512:(tcb + 1) * 512].rearrange(
                    "(dc p) o -> p dc o", p=128)
                for Hq in range(4):
                    Q = 4 * n + Hq
                    if Q >= 4:
                        sc.wait_ge(mmh, Q - 3)
                    for part in range(2):
                        dc0 = Hq * 16 + part * 8
                        sc.dma_start(out=hqb[Q % 4][:, part * 8 * 512:(part + 1) * 8 * 512],
                                     in_=srcq[:, dc0:dc0 + 8, :]).then_inc(hqld, 16)
            X2_AT = {130: 2, 200: 3}
            x2_load(0)
            x2_load(1)
            for j in range(NJ):
                sc.wait_ge(zr, j + 1)
                sc.sign(sg1[j % 2], e_st[j % 2], bias=s.nhalfc)
                sc.sign(sg2[j % 2], e_st[j % 2], bias=s.halfc).then_inc(sgd, 1)
                if j % 64 == 63:
                    tq = j // 64
                    sc.wait_ge(hqsp, 16 * 64 * (tq + 1))
                    hq_halves(tq)
                if j in X2_AT:
                    x2_load(X2_AT[j])
            for n in range(4, 16):
                hq_halves(n)
                x2_load(n)

        @block.gpsimd
        def _(g):
            def t2_parts(ocg):
                src = s.t2_sc[:, ocg * 512:(ocg + 1) * 512].rearrange(
                    "(dc p) o -> p dc o", p=128)
                for part in range(8):
                    g.dma_start(out=T2[:, part * 8 * 512:(part + 1) * 8 * 512],
                                in_=src[:, part * 8:(part + 1) * 8, :]).then_inc(t2ld, 16)
            def y_out(n):
                ocg, tcb = n // 4, n % 4
                g.wait_ge(yev, 4 * (n + 1))
                for tci in range(4):
                    tc = tcb * 4 + tci
                    g.dma_start(out=s.y[tc * 128:(tc + 1) * 128,
                                        ocg * 512:(ocg + 1) * 512],
                                in_=y_st[n % 2][:, tci * 512:(tci + 1) * 512]).then_inc(yout, 16)

            def g_load(j):
                tq, dh = j // 64, j % 64
                if j >= 4:
                    g.wait_ge(zr, j - 3)
                g.dma_start(out=g_ld[j % 4],
                            in_=s.g_sc[tq, dh * 128:(dh + 1) * 128, :]).then_inc(gld2, 16)
            g_load(1)
            g_load(3)
            t2_parts(0)
            for tq in range(4):
                for dh in range(NDH):
                    j = tq * 64 + dh
                    if j + 5 < NJ and (j + 5) % 2 == 1:
                        g_load(j + 5)
                    g.wait_ge(hqr, j + 1)
                    g.dma_start(out=s.hq_sc[dh * 128:(dh + 1) * 128,
                                            tq * 512:(tq + 1) * 512],
                                in_=hq_st[j % 2]).then_inc(hqsp, 16)
                if tq >= 1:
                    y_out(tq - 1)
            for n in range(3, 16):
                ocg, tcb = n // 4, n % 4
                if tcb == 0:
                    g.wait_ge(mmh, 16 * ocg)
                    t2_parts(ocg)
                y_out(n - 1 if n > 3 else 3)
            y_out(15)
            g.wait_ge(yout, 16 * 64)

        @block.tensor
        def _(t):
            for n in range(16):
                ocg, tcb = n // 4, n % 4
                pset = 4 * (tcb % 2)
                t.wait_ge(t2ld, 16 * 8 * (ocg + 1))
                if n >= 2:
                    t.wait_ge(yev, 4 * (n - 1))
                for Hq in range(4):
                    Q = 4 * n + Hq
                    t.wait_ge(hqld, 32 * min(Q + 2, 4 * (n + 1)))
                    for dq in range(16):
                        dh = Hq * 16 + dq
                        for tci in range(4):
                            mmb = t.matmul(s.pb[pset + tci],
                                           hqb[Q % 4][:, dq * 512 + tci * 128:
                                                 dq * 512 + (tci + 1) * 128],
                                           T2[:, dh * 512:(dh + 1) * 512],
                                           start=(dh == 0), stop=(dh == NDH - 1))
                    mmb.then_inc(mmh, 1)

        @block.vector
        def _(v):
            def evac(n):
                tcb = n % 4
                pset = 4 * (tcb % 2)
                v.wait_ge(mmh, 4 * (n + 1))
                v.wait_ge(x2ld, 16 * 4 * (n + 1))
                if n >= 2:
                    v.wait_ge(yout, 16 * 4 * (n - 1))
                for tci in range(4):
                    tc = tcb * 4 + tci
                    v.scalar_tensor_tensor(y_st[n % 2][:, tci * 512:(tci + 1) * 512],
                                           s.pb[pset + tci], s.c2_cols[:, tc:tc + 1],
                                           x2s[n % 2][:, tci * 512:(tci + 1) * 512],
                                           ALU.mult, ALU.add).then_inc(yev, 1)

            def zchain(j):
                tq = j // 64
                if j % 2 == 0:
                    v.wait_ge(gld, 16 * (j // 2 + 1))
                else:
                    v.wait_ge(gld2, 16 * ((j + 1) // 2))
                v.tensor_tensor(z_st, g_ld[j % 4],
                                s.s2_b[:, tq * 512:(tq + 1) * 512], ALU.mult)
                v.tensor_copy(sti[j % 2], z_st)
                v.tensor_tensor(e_st[j % 2], z_st, sti[j % 2], ALU.subtract).then_inc(zr, 1)

            nev = 0
            zchain(0)
            for j in range(NJ):
                if j + 1 < NJ:
                    zchain(j + 1)
                v.wait_ge(sgd, j + 1)
                v.tensor_tensor(cs_st, sg1[j % 2], sg2[j % 2], ALU.add)
                if j >= 2:
                    v.wait_ge(hqsp, 16 * (j - 1))
                v.scalar_tensor_tensor(hq_st[j % 2], cs_st, 0.5, sti[j % 2],
                                       ALU.mult, ALU.add).then_inc(hqr, 1)
                if j in EV_AT:
                    evac(EV_AT[j])
                    nev += 1
            for n in range(nev, 16):
                evac(n)
    ctx.close()
